# version 06
# baseline (speedup 1.0000x reference)
import numpy as np
import concourse.bass as bass
import concourse.mybir as mybir
from concourse.bass_utils import run_bass_kernel_spmd
from contextlib import ExitStack

F32 = mybir.dt.float32
F32R = mybir.dt.float32r
BF16 = mybir.dt.bfloat16
I32 = mybir.dt.int32
AFT = mybir.ActivationFunctionType
ALU = mybir.AluOpType
AXX = mybir.AxisListType.X

P = 128
S = 8192
D = 2048
NDC = 16
KVH = 4
HG = 4
H = 16
DH = 128
NTG = 16
SCALE = float(DH) ** -0.5
SQ128 = float(DH) ** 0.5
SHIFT = 40.0
L_PAIR = [19, 51, 83, 115]
BLK = 4
NCH = 72
P5_TRUNC = 0
P5_NFC = 64

ENGS = ("sync", "gpsimd", "tensor", "vector", "scalar")


def snake_tiles(c):
    return [8 * j + (c if j % 2 == 0 else 7 - c) for j in range(8)]


class Sched:
    def __init__(self):
        self.ops = {e: [] for e in ENGS}
        self.cnt = {}

    def n(self, sem):
        return self.cnt.get(sem, 0)

    def emit(self, eng, fn, waits=(), inc=None, amt=1):
        w = [(s, v) for (s, v) in waits if v > 0]
        self.ops[eng].append((fn, w, inc, amt))
        if inc is not None:
            self.cnt[inc] = self.cnt.get(inc, 0) + amt


def build_nc(phases=(1, 2, 3, 4, 5, 6), debug=False):
    nc = bass.Bass("TRN2", target_bir_lowering=False, detect_race_conditions=False)
    dt_scr = "ExternalOutput" if debug else "Internal"

    xT = nc.dram_tensor("xT", [D, S], F32, kind="ExternalInput")
    xTo = nc.dram_tensor("xTo", [D, 1024], F32, kind="ExternalInput")
    wkvT = nc.dram_tensor("wkvT", [D, 1024], F32, kind="ExternalInput")
    wqT = nc.dram_tensor("wqT", [D, D], F32, kind="ExternalInput")
    woT = nc.dram_tensor("woT", [D, D], F32, kind="ExternalInput")
    wgT = nc.dram_tensor("wgT", [D, S], F32, kind="ExternalInput")
    wuT = nc.dram_tensor("wuT", [D, S], F32, kind="ExternalInput")
    wdT = nc.dram_tensor("wdT", [S, D], F32, kind="ExternalInput")
    cosF = nc.dram_tensor("cosF", [P, S], F32, kind="ExternalInput")
    sinF = nc.dram_tensor("sinF", [P, S], F32, kind="ExternalInput")
    cosO = nc.dram_tensor("cosO", [P, 1024], F32, kind="ExternalInput")
    sinO = nc.dram_tensor("sinO", [P, 1024], F32, kind="ExternalInput")
    eyeD = nc.dram_tensor("eyeD", [P, P], F32, kind="ExternalInput")
    masksD = nc.dram_tensor("masksD", [P, 1024], F32, kind="ExternalInput")
    mtabD = nc.dram_tensor("mtabD", [72, P, 512], F32, kind="ExternalInput")
    cpD = nc.dram_tensor("cpD", [1, 8], I32, kind="ExternalInput")

    out_tok = nc.dram_tensor("out_tok", [1024, D], F32, kind="ExternalOutput")
    kscr = nc.dram_tensor("kscr", [NCH, KVH, P, P], F32, kind=dt_scr)
    vscr = nc.dram_tensor("vscr", [NCH, P, 512], F32, kind=dt_scr)
    q2scr = nc.dram_tensor("q2scr", [H, 1024], F32, kind=dt_scr)
    stscr = nc.dram_tensor("stscr", [D, 1024], F32, kind=dt_scr)
    dbgT = {}
    if debug:
        for nm, shp in (("qsD", [P, 16384]), ("oTD", [P, 16384]),
                        ("cvD", [P, 4096]), ("dnD", [P, 16384]),
                        ("xqD", [P, 16384]), ("a1D", [P, 16384])):
            dbgT[nm] = nc.dram_tensor(nm, shp, F32, kind="ExternalOutput")
        dbgT["bfaD"] = nc.dram_tensor("bfaD", [P, 16384], BF16,
                                      kind="ExternalOutput")

    with ExitStack() as ctx:
        e = ctx.enter_context
        a1 = e(nc.sbuf_tensor("a1", [P, 16384], F32))
        a2 = e(nc.sbuf_tensor("a2", [P, 16384], F32))
        bfa = e(nc.sbuf_tensor("bfa", [P, 16384], BF16))
        sgnb = e(nc.sbuf_tensor("sgnb", [P, 4096], BF16))
        pool = e(nc.sbuf_tensor("pool", [P, 5632], F32))
        masks = e(nc.sbuf_tensor("masks", [P, 1024], F32))
        eye = e(nc.sbuf_tensor("eye", [P, P], F32))
        ones128 = e(nc.sbuf_tensor("ones128", [P, 1], F32))
        onescol = e(nc.sbuf_tensor("onescol", [1, P], F32))
        onescbf = e(nc.sbuf_tensor("onescbf", [1, P], BF16))
        k2max = e(nc.sbuf_tensor("k2max", [1, 8], F32))
        s2row = e(nc.sbuf_tensor("s2row", [1, 8], F32))
        cpS = e(nc.sbuf_tensor("cpS", [1, 8], I32))
        epsA = e(nc.sbuf_tensor("epsA", [P, 4], F32))
        zcol2 = e(nc.sbuf_tensor("zcol2", [1, P], F32))
        zrow2 = e(nc.sbuf_tensor("zrow2", [1, 512], F32))
        ctmp = e(nc.sbuf_tensor("ctmp", [1, 1024], F32))
        rowb = e(nc.sbuf_tensor("rowb", [1, 1024], F32))
        maxab = e(nc.sbuf_tensor("maxab", [P, 1], F32))
        stok = e(nc.sbuf_tensor("stok", [P, 1], F32))
        ivtok = e(nc.sbuf_tensor("ivtok", [P, 8], F32))
        ps2a = e(nc.psum_tensor("ps2a", [P, 2, 512], F32))
        ps2b = e(nc.psum_tensor("ps2b", [P, 2, 512], F32))
        ps2c = e(nc.psum_tensor("ps2c", [P, 2, 512], F32))
        ps1a = e(nc.psum_tensor("ps1a", [P, 512], F32))
        ps1b = e(nc.psum_tensor("ps1b", [P, 512], F32))

        sch = Sched()
        E = sch.emit
        N = sch.n
        REGS = {}

        PCOS, PSIN, PKRAW, PKSH, PKOUT, PSQ, PVE, PK2 = (
            0, 512, 1024, 1536, 2048, 2560, 3584, 4096)
        PW2, PCO, PSO, PQRAW, PQSH, PSQ2 = 0, 2048, 3072, 4096, 4608, 5120
        PK, PV, PEXP, PZB, PMSK = 0, 1024, 4096, 5120, 2048
        PWG, PWU, PQR, PQM, PWD5 = 0, 2048, 0, 2048, 4096
        PSTX, POTL = 0, 2048
        A1PH, A1PL, A1T1, A1T3, A1PP, A1WD, A1IV = (
            0, 4096, 8192, 9216, 10240, 11264, 13312)

        def dma(eng, out_ap, in_ap, waits=(), inc=None):
            def fn(ee, o=out_ap, i=in_ap):
                return ee.dma_start(out=o(ee) if callable(o) else o,
                                    in_=i(ee) if callable(i) else i)
            E(eng, fn, waits, inc, 16)

        dma("sync", masks[:], masksD.ap(), inc="ldc")
        dma("sync", eye[:], eyeD.ap(), inc="ldc")
        dma("sync", cpS[:], cpD.ap(), inc="ldc")
        def fmemset(v):
            v.memset(ones128[:], 1.0)
            v.memset(onescol[:], 1.0)
            v.memset(onescbf[:], 1.0)
            v.memset(k2max[:], 0.0)
            v.memset(zcol2[:], 0.0)
            v.memset(zrow2[:], 0.0)
            v.memset(epsA[:, 0:1], 1e-5)
            v.memset(epsA[:, 1:2], 1e-8)
            v.memset(epsA[:, 2:3], 128.5)
            return v.memset(a2[:, 0:512], 0.0)
        E("vector", fmemset, inc="vz")
        for z in range(64, NCH):
            dma("gpsimd", bass.AP(kscr, z * 65536, [[512, P], [1, 512]]),
                a2[:, 0:512], waits=[("vz", 1)], inc="zw")
            dma("gpsimd", bass.AP(vscr, z * 65536, [[512, P], [1, 512]]),
                a2[:, 0:512], waits=[("vz", 1)], inc="zw")
        ZW = N("zw")
        dma("sync", bass.AP(a1, 0, [[16384, P], [1024, NDC], [1, 1024]]),
            bass.AP(wkvT, 0, [[1024, P], [P * 1024, NDC], [1, 1024]]), inc="ldw")

        def ln_group(xbuf, x_off, nb_xn, xn_off, ntk, eps, gate, first_waits,
                     sq0, sq1, cstride=None):
            eps_ap = epsA[0:1, 0:1] if eps == 1e-5 else epsA[0:1, 1:2]
            """LN of [D, ntk] feature-major: xbuf@x_off (chunk i at
            x_off+i*cstride) -> nb_xn@xn_off (chunk i at xn_off+i*cstride)."""
            cs = cstride if cstride is not None else ntk
            base_ts = N("t_sum")
            for i in range(NDC):
                sl = sq0 if i % 2 == 0 else sq1
                wsq = [gate] + first_waits + ([("vz", 1)] if i == 0 else [])
                wsq.append(("t_sum", base_ts if i < 2 else N("t_sum") - 1))
                E("scalar", (lambda sc, i=i, sl=sl: sc.activation(
                    out=pool[:, sl:sl + ntk],
                    in_=xbuf[:, x_off + i * cs:x_off + i * cs + ntk],
                    func=AFT.Square)), waits=wsq, inc="a_sq")
                wmm = [gate] + (first_waits + [("v_st", N("v_st")),
                                               ("vz", 1), ("t_bc", N("t_bc"))]
                                if i == 0 else [])
                E("tensor", (lambda t, i=i: t.matmul(
                    bass.AP(ps2c, 0, [[1024, 1], [1, ntk]]),
                    ones128[:], xbuf[:, x_off + i * cs:x_off + i * cs + ntk],
                    start=(i == 0), stop=(i == NDC - 1))), waits=wmm)
                E("tensor", (lambda t, i=i, sl=sl: t.matmul(
                    bass.AP(ps2c, 32 * 1024, [[1024, 1], [1, ntk]]),
                    ones128[:], pool[:, sl:sl + ntk],
                    start=(i == 0), stop=(i == NDC - 1))),
                    waits=[("a_sq", N("a_sq"))], inc="t_sum")
            ts = N("t_sum")
            m_ap = pool[0:1, sq0:sq0 + ntk]
            v_ap = pool[0:1, sq1:sq1 + ntk]
            r_ap = ctmp[0:1, 0:ntk]
            E("vector", (lambda v: v.tensor_scalar_mul(
                out=m_ap,
                in0=bass.AP(ps2c, 0, [[1024, 1], [1, ntk]]), scalar1=1.0 / D)),
                waits=[("t_sum", ts)])
            E("vector", (lambda v: v.tensor_mul(v_ap, m_ap, m_ap)))
            E("vector", (lambda v: v.scalar_tensor_tensor(
                out=v_ap,
                in0=bass.AP(ps2c, 32 * 1024, [[1024, 1], [1, ntk]]),
                scalar=1.0 / D, in1=v_ap,
                op0=ALU.mult, op1=ALU.subtract)), inc="v_st")
            E("scalar", (lambda sc: sc.activation(
                out=r_ap, in_=v_ap, func=AFT.Sqrt,
                bias=eps_ap)), waits=[("v_st", N("v_st")), ("a_cv", N("a_cv"))],
                inc="a_rs")
            E("vector", (lambda v: v.reciprocal(out=r_ap, in_=r_ap)),
                waits=[("a_rs", N("a_rs"))], inc="v_rr")
            E("tensor", (lambda t: t.matmul(
                ps1a[:, 0:ntk], onescol[:], m_ap,
                start=True, stop=True)),
                waits=[("v_rr", N("v_rr")), ("v_xn", N("v_xn"))])
            E("tensor", (lambda t: t.matmul(
                ps1b[:, 0:ntk], onescol[:], r_ap,
                start=True, stop=True)), inc="t_bc")
            tbc = N("t_bc")
            for i in range(NDC):
                E("vector", (lambda v, i=i: v.tensor_sub(
                    nb_xn[:, xn_off + i * cs:xn_off + i * cs + ntk],
                    xbuf[:, x_off + i * cs:x_off + i * cs + ntk],
                    ps1a[:, 0:ntk])), waits=[("t_bc", tbc)] if i == 0 else [])
                E("vector", (lambda v, i=i: v.tensor_mul(
                    nb_xn[:, xn_off + i * cs:xn_off + i * cs + ntk],
                    nb_xn[:, xn_off + i * cs:xn_off + i * cs + ntk],
                    ps1b[:, 0:ntk])), inc="v_xn")

        if 1 in phases:
            ldw = N("ldw")
            for tg in range(NTG):
                dma("sync",
                    bass.AP(a2, 8192, [[16384, P], [512, NDC], [1, 512]]),
                    bass.AP(xT, tg * 512, [[S, P], [P * S, NDC], [1, 512]]),
                    waits=[("v_xn", N("v_xn"))], inc="xs")
                dma("sync", pool[:, PCOS:PCOS + 512],
                    bass.AP(cosF, tg * 512, [[S, P], [1, 512]]),
                    waits=[("v_ko", N("v_ko"))], inc="cs")
                dma("sync", pool[:, PSIN:PSIN + 512],
                    bass.AP(sinF, tg * 512, [[S, P], [1, 512]]), inc="cs")
                ln_group(a2, 8192, a2, 0, 512, 1e-5, ("xs", N("xs")),
                         [("t_k", N("t_k")), ("t_v", N("t_v"))], PSQ, PSQ + 512)
                vxn = N("v_xn")
                for g in range(KVH):
                    bank = g % 2
                    E("tensor", (lambda t, g=g, bank=bank: [
                        t.matmul(ps2a[:, bank, :],
                                 a1[:, i * 1024 + g * P:i * 1024 + g * P + P],
                                 a2[:, i * 512:(i + 1) * 512],
                                 start=(i == 0), stop=(i == NDC - 1))
                        for i in range(NDC)][-1]),
                        waits=[("v_xn", vxn), ("ldw", ldw),
                               ("v_kc", max(0, N("v_kc") - 1))], inc="t_k")
                    E("vector", (lambda v, bank=bank: v.tensor_copy(
                        out=pool[:, PKRAW:PKRAW + 512], in_=ps2a[:, bank, :])),
                        waits=[("t_k", N("t_k")), ("g_sh", N("g_sh"))],
                        inc="v_kc")
                    dma("gpsimd", pool[0:64, PKSH:PKSH + 512],
                        pool[64:128, PKRAW:PKRAW + 512],
                        waits=[("v_kc", N("v_kc"))], inc="g_sh")
                    dma("gpsimd", pool[64:128, PKSH:PKSH + 512],
                        pool[0:64, PKRAW:PKRAW + 512], inc="g_sh")
                    E("vector", (lambda v: v.tensor_mul(
                        pool[:, PKRAW:PKRAW + 512], pool[:, PKRAW:PKRAW + 512],
                        pool[:, PCOS:PCOS + 512])),
                        waits=[("g_sh", N("g_sh")), ("cs", N("cs")),
                               ("kw", N("kw")), ("a_k2", N("a_k2"))])
                    E("vector", (lambda v: v.tensor_mul(
                        pool[:, PKSH:PKSH + 512], pool[:, PKSH:PKSH + 512],
                        pool[:, PSIN:PSIN + 512])))
                    E("vector", (lambda v: v.tensor_add(
                        pool[:, PKOUT:PKOUT + 512], pool[:, PKRAW:PKRAW + 512],
                        pool[:, PKSH:PKSH + 512])), inc="v_ko")
                    vko = N("v_ko")
                    E("scalar", (lambda sc: sc.activation(
                        out=pool[:, PK2:PK2 + 512],
                        in_=pool[:, PKOUT:PKOUT + 512], func=AFT.Square)),
                        waits=[("v_ko", vko), ("t_k2", N("t_k2"))], inc="a_k2")
                    E("tensor", (lambda t: t.matmul(
                        bass.AP(ps2c, 512, [[1024, 1], [1, 512]]),
                        ones128[:], pool[:, PK2:PK2 + 512],
                        start=True, stop=True)),
                        waits=[("a_k2", N("a_k2")), ("v_km", N("v_km"))],
                        inc="t_k2")
                    E("vector", (lambda v: v.reduce_max(
                        ctmp[0:1, 512:513],
                        bass.AP(ps2c, 512, [[1024, 1], [1, 512]]), axis=AXX)),
                        waits=[("t_k2", N("t_k2"))], inc="v_kr")
                    E("vector", (lambda v, g=g: v.tensor_max(
                        k2max[0:1, g:g + 1], k2max[0:1, g:g + 1],
                        ctmp[0:1, 512:513])),
                        waits=[("v_kr", N("v_kr"))], inc="v_km")
                    dma("gpsimd",
                        bass.AP(kscr, (4 * tg) * 65536 + g * 16384,
                                [[P, P], [65536, 4], [1, P]]),
                        pool[:, PKOUT:PKOUT + 512],
                        waits=[("v_ko", vko), ("zw", ZW)], inc="kw")
                for tt in range(4):
                    bank = tt % 2
                    E("tensor", (lambda t, tt=tt, bank=bank: [
                        t.matmul(ps2b[:, bank, :],
                                 a2[:, i * 512 + tt * P:i * 512 + tt * P + P],
                                 a1[:, i * 1024 + 512:(i + 1) * 1024],
                                 start=(i == 0), stop=(i == NDC - 1))
                        for i in range(NDC)][-1]),
                        waits=[("v_xn", vxn), ("ldw", ldw),
                               ("v_ve", max(0, N("v_ve") - 1))], inc="t_v")
                    E("vector", (lambda v, bank=bank: v.tensor_copy(
                        out=pool[:, PVE:PVE + 512], in_=ps2b[:, bank, :])),
                        waits=[("t_v", N("t_v")), ("vw", N("vw"))],
                        inc="v_ve")
                    dma("gpsimd",
                        bass.AP(vscr, (4 * tg + tt) * 65536, [[512, P], [1, 512]]),
                        pool[:, PVE:PVE + 512],
                        waits=[("v_ve", N("v_ve")), ("zw", ZW)], inc="vw")
            E("vector", (lambda v: v.tensor_scalar_mul(
                out=s2row[:], in0=k2max[:], scalar1=1.0 / 128.0)),
                waits=[("v_km", N("v_km"))], inc="v_s2")

        P1D = dict(kw=N("kw"), vw=N("vw"), v_s2=N("v_s2"),
                   v_ko=N("v_ko"), a_k2=N("a_k2"), cs=N("cs"),
                   g_sh=N("g_sh"), v_ve=N("v_ve"), t_k2=N("t_k2"))

        if 2 in phases:
            dma("sync", pool[:, PCO:PCO + 1024], cosO.ap(),
                waits=[("kw", P1D["kw"]), ("a_k2", P1D["a_k2"]),
                       ("v_ko", P1D["v_ko"]), ("v_ve", P1D["v_ve"]),
                       ("vw", P1D["vw"])], inc="cso")
            dma("sync", pool[:, PSO:PSO + 1024], sinO.ap(), inc="cso")
            for grp in range(2):
                dma("sync",
                    bass.AP(a2, 8192, [[16384, P], [512, NDC], [1, 512]]),
                    bass.AP(xTo, grp * 512, [[1024, P], [P * 1024, NDC], [1, 512]]),
                    waits=[("v_xn", N("v_xn"))], inc="xs")
                ln_group(a2, 8192, a2, 0, 512, 1e-5, ("xs", N("xs")),
                         [("t_q", N("t_q")), ("v_qo", N("v_qo")),
                          ("v_ko", P1D["v_ko"]), ("t_k2", P1D["t_k2"])],
                         PQRAW, PQSH)
                vxn = N("v_xn")
                for h in range(H):
                    g, hg = h // HG, h % HG
                    bank = h % 2
                    dma("sync",
                        bass.AP(pool, PW2, [[5632, P], [P, NDC], [1, P]]),
                        bass.AP(wqT, h * P, [[D, P], [P * D, NDC], [1, P]]),
                        waits=[("t_q", N("t_q")), ("v_ko", P1D["v_ko"]),
                               ("g_sh", P1D["g_sh"])], inc="wqs")
                    E("tensor", (lambda t, bank=bank: [
                        t.matmul(ps2a[:, bank, :],
                                 pool[:, PW2 + i * P:PW2 + (i + 1) * P],
                                 a2[:, i * 512:(i + 1) * 512],
                                 start=(i == 0), stop=(i == NDC - 1))
                        for i in range(NDC)][-1]),
                        waits=[("v_xn", vxn), ("wqs", N("wqs")),
                               ("v_qc", max(0, N("v_qc") - 1))], inc="t_q")
                    E("vector", (lambda v, bank=bank: v.tensor_copy(
                        out=pool[:, PQRAW:PQRAW + 512], in_=ps2a[:, bank, :])),
                        waits=[("t_q", N("t_q")), ("g_qs", N("g_qs")),
                               ("a_sq", N("a_sq"))], inc="v_qc")
                    dma("gpsimd", pool[0:64, PQSH:PQSH + 512],
                        pool[64:128, PQRAW:PQRAW + 512],
                        waits=[("v_qc", N("v_qc"))], inc="g_qs")
                    dma("gpsimd", pool[64:128, PQSH:PQSH + 512],
                        pool[0:64, PQRAW:PQRAW + 512], inc="g_qs")
                    qcol = g * 4096 + hg * 1024 + grp * 512
                    E("vector", (lambda v, grp=grp: v.tensor_mul(
                        pool[:, PQRAW:PQRAW + 512], pool[:, PQRAW:PQRAW + 512],
                        pool[:, PCO + grp * 512:PCO + grp * 512 + 512])),
                        waits=[("g_qs", N("g_qs")), ("cso", N("cso"))])
                    E("vector", (lambda v, grp=grp: v.tensor_mul(
                        pool[:, PQSH:PQSH + 512], pool[:, PQSH:PQSH + 512],
                        pool[:, PSO + grp * 512:PSO + grp * 512 + 512])))
                    E("vector", (lambda v, qcol=qcol: v.tensor_add(
                        a1[:, qcol:qcol + 512], pool[:, PQRAW:PQRAW + 512],
                        pool[:, PQSH:PQSH + 512])), inc="v_qo")
                    E("scalar", (lambda sc, qcol=qcol: sc.activation(
                        out=pool[:, PSQ2:PSQ2 + 512], in_=a1[:, qcol:qcol + 512],
                        func=AFT.Square)),
                        waits=[("v_qo", N("v_qo")), ("t_q2", N("t_q2"))],
                        inc="a_q2")
                    E("tensor", (lambda t: t.matmul(
                        bass.AP(ps2c, 512, [[1024, 1], [1, 512]]),
                        ones128[:], pool[:, PSQ2:PSQ2 + 512],
                        start=True, stop=True)),
                        waits=[("a_q2", N("a_q2")), ("v_q2", N("v_q2")),
                               ("v_km", P1D["v_s2"] and N("v_km"))], inc="t_q2")
                    E("vector", (lambda v: v.tensor_copy(
                        out=rowb[0:1, 0:512],
                        in_=bass.AP(ps2c, 512, [[1024, 1], [1, 512]]))),
                        waits=[("t_q2", N("t_q2")), ("g_q2", N("g_q2"))],
                        inc="v_q2")
                    dma("gpsimd",
                        bass.AP(q2scr, h * 1024 + grp * 512, [[512, 1], [1, 512]]),
                        rowb[0:1, 0:512],
                        waits=[("v_q2", N("v_q2"))], inc="g_q2")

        P2D = dict(v_qo=N("v_qo"), g_q2=N("g_q2"), t_q=N("t_q"),
                   v_qc=N("v_qc"))

        NIT = 0
        if 3 in phases:
            NJ = [8 * j + 9 for j in range(8)]
            sections = []
            iters = []
            it0 = 0
            for g in range(KVH):
                for j in range(8):
                    for i in range(NJ[j]):
                        iters.append((g, j, i))
                    sections.append((g, j, it0, it0 + NJ[j] - 1))
                    it0 += NJ[j]
            NIT = len(iters)
            blocks = []
            bidx = 0
            it0 = 0
            for g in range(KVH):
                for j in range(8):
                    nblk = (NJ[j] + BLK - 1) // BLK
                    for n in range(nblk):
                        f = it0 + n * BLK
                        l = it0 + min((n + 1) * BLK, NJ[j]) - 1
                        blocks.append((g, j, n, bidx, f, l))
                        bidx += 1
                    it0 += NJ[j]
            it2blk = {}
            for b in blocks:
                for it in range(b[4], b[5] + 1):
                    it2blk[it] = b
            masked = {}
            for it, (g, j, i) in enumerate(iters):
                if i >= 8 * j:
                    masked[it] = (j, i - 8 * j)
            masked_list = sorted(masked)
            mask_pos = {it: k + 1 for k, it in enumerate(masked_list)}
            mask_idx = {it: k for k, it in enumerate(masked_list)}

            ldc = N("ldc")
            BAR3 = [("kw", P1D["kw"]), ("vw", P1D["vw"]), ("ldc", ldc),
                    ("v_qo", P2D["v_qo"]), ("g_q2", P2D["g_q2"]),
                    ("zw", ZW), ("v_s2", P1D["v_s2"])]
            TSC0 = N("t_sc")
            TZA0 = N("t_za")
            AE0 = N("a_e")

            def sc_cnt(it):
                return TSC0 + it + 1

            def za_cnt(it):
                return TZA0 + it + 1

            def e_cnt(it):
                return AE0 + it + 1

            SE0 = N("v_se")

            def se_cnt(it):
                return SE0 + it + 1

            AC0 = N("a_c1")

            def c1_cnt(it):
                return AC0 + it + 1

            def emit_cvec(g):
                for hg in range(HG):
                    h = 4 * g + hg
                    dma("sync",
                        ctmp[0:1, :],
                        bass.AP(q2scr, h * 1024, [[1024, 1], [1, 1024]]),
                        waits=BAR3 + [("a_cv", N("a_cv")), ("v_cv", N("v_cv"))],
                        inc="cq")
                    E("scalar", (lambda sc, g=g: sc.activation(
                        out=ctmp[0:1, :], in_=ctmp[0:1, :], func=AFT.Sqrt,
                        scale=s2row[0:1, g:g + 1])),
                        waits=[("cq", N("cq")), ("v_cv", N("v_cv"))],
                        inc="a_cv")
                    E("vector", (lambda v, hg=hg: v.tensor_scalar(
                        out=bfa[0:1, hg * 1024:hg * 1024 + 1024],
                        in0=ctmp[0:1, :],
                        scalar1=-1.0, scalar2=SHIFT,
                        op0=ALU.mult, op1=ALU.add)),
                        waits=[("a_cv", N("a_cv")), ("t_cb", 8 * g)],
                        inc="v_cv")

            def emit_block_loads(b):
                g, j, n, bi, f, l = b
                dbl = bi % 2
                kwv = ("t_sc", sc_cnt(blocks[bi - 2][5])) if bi >= 2                     else ("t_sc", 0)
                vwv = ("t_za", za_cnt(blocks[bi - 2][5])) if bi >= 2                     else ("t_za", 0)
                dma("sync",
                    bass.AP(pool, PK + dbl * 512, [[5632, P], [P, 4], [1, P]]),
                    bass.AP(kscr, 4 * n * 65536 + g * 16384,
                            [[P, P], [65536, 4], [1, P]]),
                    waits=BAR3 + [kwv], inc="kl")
                dma("gpsimd",
                    bass.AP(pool, PV + dbl * 512, [[5632, P], [P, 4], [1, P]]),
                    bass.AP(vscr, 4 * n * 65536 + g * P,
                            [[512, P], [65536, 4], [1, P]]),
                    waits=BAR3 + [vwv], inc="vl")

            mask_loaded = [0]

            def emit_mask_load_mi(mi):
                it = masked_list[mi]
                jj, b = masked[it]
                wv = ("v_m", mi - 3) if mi >= 4 else ("v_m", 0)
                dma("gpsimd",
                    pool[:, PMSK + (mi % 4) * 512:PMSK + (mi % 4) * 512 + 512],
                    bass.AP(mtabD, (jj * 9 + b) * 65536, [[512, P], [1, 512]]),
                    waits=BAR3 + [wv], inc="ml")
                mask_loaded[0] = mi + 1

            def emit_iter(it, si, s_f):
                g, j, i = iters[it]
                b = it2blk[it]
                bi, n = b[3], b[2]
                dbl = bi % 2
                sbank = it % 2
                sec = g * 8 + j
                kslot = PK + dbl * 512 + (i - 4 * n) * P
                vslot = PV + dbl * 512 + (i - 4 * n) * P
                qcol = g * 4096 + j * P

                def fsc(t, kslot=kslot, qcol=qcol, sbank=sbank):
                    return t.matmul(
                        ps2a[:, sbank, :],
                        pool[:, kslot:kslot + P],
                        bass.AP(a1, qcol, [[16384, P], [1024, 4], [1, P]]),
                        start=True, stop=True, skip_group_check=True)
                w = [("kl", kdma_after[bi] * 16)]
                if it >= 2:
                    w.append(("a_c1", c1_cnt(it - 2)))
                E("tensor", fsc, waits=w, inc="t_sc")

                we = [("t_sc", sc_cnt(it))]
                if it >= 2:
                    we.append(("t_za", za_cnt(it - 2)))
                E("scalar", (lambda sc, sbank=sbank: sc.activation(
                    out=pool[:, PEXP + sbank * 512:PEXP + sbank * 512 + 512],
                    in_=ps2a[:, sbank, :], func=AFT.Copy, scale=SCALE)),
                    waits=we, inc="a_c1")
                E("vector", (lambda v, sbank=sbank: v.tensor_add(
                    pool[:, PEXP + sbank * 512:PEXP + sbank * 512 + 512],
                    pool[:, PEXP + sbank * 512:PEXP + sbank * 512 + 512],
                    masks[:, 0:512])),
                    waits=[("a_c1", c1_cnt(it)), ("v_cb", N("v_cb"))],
                    inc="v_se")
                E("scalar", (lambda sc, sbank=sbank: sc.activation(
                    out=pool[:, PEXP + sbank * 512:PEXP + sbank * 512 + 512],
                    in_=pool[:, PEXP + sbank * 512:PEXP + sbank * 512 + 512],
                    func=AFT.Exp)),
                    waits=[("v_se", se_cnt(it))], inc="a_e")

                if it in masked:
                    mi = mask_idx[it]
                    E("vector", (lambda v, sbank=sbank, mi=mi: v.tensor_tensor(
                        out=pool[:, PEXP + sbank * 512:PEXP + sbank * 512 + 512],
                        in0=pool[:, PEXP + sbank * 512:PEXP + sbank * 512 + 512],
                        in1=pool[:, PMSK + (mi % 4) * 512:
                                 PMSK + (mi % 4) * 512 + 512],
                        op=ALU.mult)),
                        waits=[("a_e", e_cnt(it)), ("ml", mdma_after[it] * 16)],
                        inc="v_m")

                zps = ps1a if sec % 2 == 0 else ps1b
                obank = sec % 2
                first = (it == s_f)

                def fza(t, vslot=vslot, sbank=sbank, zps=zps, obank=obank,
                        first=first):
                    pexp_ap = pool[:, PEXP + sbank * 512:PEXP + sbank * 512 + 512]
                    t.matmul(bass.AP(zps, 0, [[512, 1], [1, 512]]),
                             ones128[:], pexp_ap, start=first, stop=False,
                             skip_group_check=True)
                    return t.matmul(ps2b[:, obank, :],
                                    pool[:, vslot:vslot + P], pexp_ap,
                                    start=first, stop=False,
                                    skip_group_check=True)
                wz = [("vl", vdma_after[bi] * 16)]
                if first:
                    wz += [("a_rz", max(0, si - 1)),
                           ("v_e3", max(0, 2 * si - 2))]
                if it in masked:
                    wz.append(("v_m", mask_pos[it]))
                else:
                    wz.append(("a_e", e_cnt(it)))
                E("tensor", fza, waits=wz, inc="t_za")

            kdma_after = {}
            vdma_after = {}
            kc = 0
            for b in blocks:
                kc += 1
                kdma_after[b[3]] = kc
                vdma_after[b[3]] = kc
            mdma_after = {it: mask_idx[it] + 1 for it in masked}

            for si, (g, j, s_f, s_l) in enumerate(sections):
                if j == 0:
                    emit_cvec(g)
                obank = si % 2
                zps = ps1a if si % 2 == 0 else ps1b
                E("tensor", (lambda t, j=j: t.matmul(
                    ps2c[:, 0, :], onescbf[:],
                    bass.AP(bfa, j * P, [[16384, 1], [1024, 4], [1, P]]),
                    start=True, stop=True, skip_group_check=True)),
                    waits=BAR3 + [("v_cv", N("v_cv")),
                                  ("v_cb", N("v_cb"))], inc="t_cb")
                E("vector", (lambda v: v.tensor_copy(
                    out=masks[:, 0:512], in_=ps2c[:, 0, :])),
                    waits=[("t_cb", N("t_cb"))], inc="v_cb")
                sec_blocks = [b for b in blocks if b[0] == g and b[1] == j]
                for k2 in range(min(2, len(sec_blocks))):
                    emit_block_loads(sec_blocks[k2])
                if si == 0:
                    for _mi in range(4):
                        emit_mask_load_mi(_mi)
                for it in range(s_f, s_l + 1):
                    i = it - s_f
                    if it in masked:
                        mi = mask_idx[it]
                        while mask_loaded[0] <= mi + 4 and \
                                mask_loaded[0] < len(masked_list):
                            emit_mask_load_mi(mask_loaded[0])
                    if i % BLK == 0:
                        nb = i // BLK + 2
                        if nb < len(sec_blocks):
                            emit_block_loads(sec_blocks[nb])
                    emit_iter(it, si, s_f)
                za_end = za_cnt(s_l)
                E("vector", (lambda v, zps=zps: v.reciprocal(
                    out=rowb[0:1, 0:512],
                    in_=bass.AP(zps, 0, [[512, 1], [1, 512]]))),
                    waits=[("t_za", za_end), ("t_ob", N("t_ob"))], inc="a_rz")
                E("tensor", (lambda t: t.matmul(
                    ps2c[:, 1, :], onescol[:], rowb[0:1, 0:512],
                    start=True, stop=True, skip_group_check=True)),
                    waits=[("a_rz", N("a_rz")), ("v_e3", max(0, 2 * si - 1))],
                    inc="t_ob")
                E("vector", (lambda v: v.tensor_copy(
                    out=pool[:, PZB:PZB + 512], in_=ps2c[:, 1, :])),
                    waits=[("t_ob", N("t_ob"))], inc="v_e3")
                ocol = 4 * g * 1024 + j * P
                E("vector", (lambda v, obank=obank, ocol=ocol: v.tensor_tensor(
                    out=bass.AP(a2, ocol, [[16384, P], [1024, 4], [1, P]]),
                    in0=bass.AP(ps2b, obank * 512, [[1024, P], [P, 4], [1, P]]),
                    in1=bass.AP(pool, PZB, [[5632, P], [P, 4], [1, P]]),
                    op=ALU.mult)), inc="v_e3")

        P3D = dict(v_e3=N("v_e3"), t_za=N("t_za"), t_sc=N("t_sc"),
                   a_e=N("a_e"), kl=N("kl"), vl=N("vl"), v_se=N("v_se"))

        if debug and 2 in phases:
            dma("sync", dbgT["qsD"].ap(), a1[:],
                waits=[("v_qo", P2D["v_qo"])] +
                      ([("t_sc", P3D["t_sc"])] if 3 in phases else []),
                inc="dbg")
        if debug and 3 in phases:
            dma("sync", dbgT["oTD"].ap(), a2[:],
                waits=[("v_e3", P3D["v_e3"])], inc="dbg")
            dma("gpsimd", bass.AP(dbgT["cvD"], 0, [[4096, 1], [1, 4096]]),
                bfa[0:1, 0:4096], waits=[("v_cv", N("v_cv"))], inc="dbg2")

        if 4 in phases:
            for grp in range(2):
                for j in range(NDC):
                    bank = (grp * NDC + j) % 2
                    dma("sync",
                        bass.AP(pool, PW2, [[5632, P], [P, NDC], [1, P]]),
                        bass.AP(woT, j * P, [[D, P], [P * D, NDC], [1, P]]),
                        waits=[("t_s4", N("t_s4")), ("v_e3", P3D["v_e3"]),
                               ("kl", P3D["kl"])], inc="wos")
                    E("tensor", (lambda t, bank=bank, grp=grp: [
                        t.matmul(ps2a[:, bank, :],
                                 pool[:, PW2 + i * P:PW2 + (i + 1) * P],
                                 a2[:, i * 1024 + grp * 512:
                                    i * 1024 + grp * 512 + 512],
                                 start=(i == 0), stop=(i == NDC - 1))
                        for i in range(NDC)][-1]),
                        waits=[("wos", N("wos")), ("v_e3", P3D["v_e3"]),
                               ("a_e", P3D["a_e"]),
                               ("v_s4", max(0, N("v_s4") - 1))], inc="t_s4")
                    E("vector", (lambda v, bank=bank: v.tensor_copy(
                        out=pool[:, PVE:PVE + 512], in_=ps2a[:, bank, :])),
                        waits=[("t_s4", N("t_s4")), ("g_s4", N("g_s4")),
                               ("vl", P3D["vl"])], inc="v_s4")
                    dma("gpsimd",
                        bass.AP(stscr, j * P * 1024 + grp * 512,
                                [[1024, P], [1, 512]]),
                        pool[:, PVE:PVE + 512],
                        waits=[("v_s4", N("v_s4"))], inc="g_s4")

        P4D = dict(g_s4=N("g_s4"), t_s4=N("t_s4"), v_s4=N("v_s4"))

        P5D = {}
        if 5 in phases:
            B5 = [("g_s4", P4D["g_s4"]), ("t_s4", P4D["t_s4"]),
                  ("v_s4", P4D["v_s4"])]
            dma("sync",
                bass.AP(a1, 0, [[16384, P], [1024, NDC], [1, 1024]]),
                bass.AP(xTo, 0, [[1024, P], [P * 1024, NDC], [1, 1024]]),
                waits=B5 + [("t_sc", P3D["t_sc"]), ("v_xn", N("v_xn"))],
                inc="x5")
            for rnd, eps in ((0, 1e-5), (1, 1e-8)):
                for cg in range(2):
                    ln_group(a1, cg * 512, a1, cg * 512, 512, eps,
                             ("x5", N("x5")),
                             [("v_s4", P4D["v_s4"]), ("t_q2", N("t_q2")),
                              ("v_xn", N("v_xn")), ("v_se", P3D["v_se"])],
                             PQR, PQM, cstride=1024)
            VXN5 = N("v_xn")
            for j in range(8):
                for i in range(NDC):
                    idx = j * NDC + i
                    psu = ps1a if idx % 2 == 0 else ps1b
                    E("tensor", (lambda t, j=j, i=i, psu=psu: t.transpose(
                        psu[:, 0:P], a1[:, i * 1024 + j * P:i * 1024 + j * P + P],
                        eye[:])),
                        waits=([("v_xn", VXN5), ("v_e3", P3D["v_e3"])]
                               if idx == 0 else []) +
                              [("v_tp", max(0, N("v_tp") - 1))], inc="t_tp")
                    E("vector", (lambda v, j=j, i=i, psu=psu: v.tensor_copy(
                        out=a2[:, j * D + i * P:j * D + (i + 1) * P],
                        in_=psu[:, 0:P])),
                        waits=[("t_tp", N("t_tp"))], inc="v_tp")
            for j in range(8):
                E("vector", (lambda v, j=j: v.reduce_max(
                    maxab[:], a2[:, j * D:(j + 1) * D], axis=AXX,
                    apply_absolute_value=True)),
                    waits=[("v_tp", 16 * (j + 1)), ("a_u", N("a_u"))],
                    inc="v_qm")
                E("vector", (lambda v, j=j: v.tensor_scalar_mul(
                    out=ivtok[:, j:j + 1], in0=maxab[:], scalar1=1.0 / 127.0)),
                    waits=[("v_qm", N("v_qm"))], inc="v_iv0")
                E("vector", (lambda v: v.reciprocal(
                    out=stok[:], in_=maxab[:])),
                    waits=[("v_qm", N("v_qm")), ("a_u", N("a_u"))], inc="v_rc")
                E("vector", (lambda v: v.tensor_scalar_mul(
                    out=stok[:], in0=stok[:], scalar1=127.0)),
                    waits=[("v_rc", N("v_rc"))], inc="a_st")
                E("vector", (lambda v, j=j: v.tensor_scalar(
                    out=a2[:, j * D:(j + 1) * D], in0=a2[:, j * D:(j + 1) * D],
                    scalar1=stok[:], scalar2=128.5,
                    op0=ALU.mult, op1=ALU.add)),
                    waits=[("a_st", N("a_st"))], inc="a_u")
                E("vector", (lambda v, j=j: v.tensor_copy(
                    out=pool[:, PQR:PQR + D], in_=a2[:, j * D:(j + 1) * D])),
                    waits=[("a_u", N("a_u"))])
                for k in range(7, -1, -1):
                    cst = float(1 << k)
                    E("vector", (lambda v, cst=cst: v.tensor_single_scalar(
                        out=pool[:, PQM:PQM + D], in_=pool[:, PQR:PQR + D],
                        scalar=cst, op=ALU.is_ge)))
                    E("vector", (lambda v, cst=cst: v.tensor_scalar_mul(
                        out=pool[:, PQM:PQM + D], in0=pool[:, PQM:PQM + D],
                        scalar1=cst)))
                    E("vector", (lambda v: v.tensor_sub(
                        pool[:, PQR:PQR + D], pool[:, PQR:PQR + D],
                        pool[:, PQM:PQM + D])))
                E("vector", (lambda v, j=j: v.scalar_tensor_tensor(
                    out=a2[:, j * D:(j + 1) * D], in0=a2[:, j * D:(j + 1) * D],
                    scalar=-128.0, in1=pool[:, PQR:PQR + D],
                    op0=ALU.add, op1=ALU.subtract)), inc="v_n")
            for j in range(8):
                for i in range(NDC):
                    idx = j * NDC + i
                    psu = ps1a if idx % 2 == 0 else ps1b
                    E("tensor", (lambda t, j=j, i=i, psu=psu: t.transpose(
                        psu[:, 0:P], a2[:, j * D + i * P:j * D + (i + 1) * P],
                        eye[:])),
                        waits=[("v_n", N("v_n"))] +
                              [("v_tp2", max(0, N("v_tp2") - 1))], inc="t_tp2")
                    E("vector", (lambda v, j=j, i=i, psu=psu: v.tensor_copy(
                        out=bfa[:, i * 1024 + j * P:i * 1024 + j * P + P],
                        in_=psu[:, 0:P])),
                        waits=[("t_tp2", N("t_tp2"))], inc="v_tp2")
            XQDONE = N("v_tp2")
            if P5_TRUNC:
                sch.ops
            for j in range(8):
                psu = ps1a if j % 2 == 0 else ps1b
                E("tensor", (lambda t, psu=psu, j=j: t.transpose(
                    psu[0:1, 0:P], ivtok[:, j:j + 1], eye[:])),
                    waits=[("v_iv0", 8), ("v_iv1", max(0, N("v_iv1") - 1)),
                           ("v_tp2", XQDONE)], inc="t_iv")
                E("vector", (lambda v, psu=psu, j=j: v.tensor_copy(
                    out=rowb[0:1, j * P:(j + 1) * P], in_=psu[0:1, 0:P])),
                    waits=[("t_iv", N("t_iv"))], inc="v_iv1")
            for hf in range(2):
                E("tensor", (lambda t, hf=hf: t.matmul(
                    ps1a[:, 0:512], onescol[:],
                    rowb[0:1, hf * 512:hf * 512 + 512],
                    start=True, stop=True, skip_group_check=True)),
                    waits=[("v_iv1", 8), ("v_iv2", N("v_iv2"))], inc="t_iv2")
                E("vector", (lambda v, hf=hf: v.tensor_copy(
                    out=a1[:, A1IV + hf * 512:A1IV + hf * 512 + 512],
                    in_=ps1a[:, 0:512])),
                    waits=[("t_iv2", N("t_iv2"))], inc="v_iv2")
            IVDONE = N("v_iv2")
            for fc in range(P5_NFC):
                grpH, posH = fc // 4, fc % 4
                dma("sync",
                    bass.AP(pool, PWG, [[5632, P], [P, NDC], [1, P]]),
                    bass.AP(wgT, fc * P, [[S, P], [P * S, NDC], [1, P]]),
                    waits=[("a_sg", N("a_sg")), ("x5", N("x5")),
                           ("v_n", N("v_n")), ("a_u", N("a_u"))],
                    inc="wgl")
                dma("sync",
                    bass.AP(pool, PWU, [[5632, P], [P, NDC], [1, P]]),
                    bass.AP(wuT, fc * P, [[S, P], [P * S, NDC], [1, P]]),
                    waits=[("a_su", N("a_su")), ("v_n", N("v_n"))], inc="wul")
                E("scalar", (lambda sc: sc.activation(
                    out=sgnb[:, 0:2048], in_=pool[:, PWG:PWG + 2048],
                    func=AFT.Sign)),
                    waits=[("wgl", N("wgl")), ("t_g", N("t_g"))],
                    inc="a_sg")
                E("scalar", (lambda sc: sc.activation(
                    out=sgnb[:, 2048:4096], in_=pool[:, PWU:PWU + 2048],
                    func=AFT.Sign)),
                    waits=[("wul", N("wul")), ("t_u", N("t_u"))],
                    inc="a_su")
                E("tensor", (lambda t: [
                    t.matmul(ps2a[:, hf, :], sgnb[:, i * P:(i + 1) * P],
                             bfa[:, i * 1024 + hf * 512:i * 1024 + hf * 512 + 512],
                             start=(i == 0), stop=(i == NDC - 1))
                    for hf in range(2) for i in range(NDC)][-1]),
                    waits=[("a_sg", N("a_sg")), ("v_tp2", XQDONE),
                           ("v_t1", N("v_t1"))], inc="t_g")
                E("tensor", (lambda t: [
                    t.matmul(ps2c[:, hf, :],
                             sgnb[:, 2048 + i * P:2048 + (i + 1) * P],
                             bfa[:, i * 1024 + hf * 512:i * 1024 + hf * 512 + 512],
                             start=(i == 0), stop=(i == NDC - 1))
                    for hf in range(2) for i in range(NDC)][-1]),
                    waits=[("a_su", N("a_su")), ("v_t3", N("v_t3")),
                           ("t_q2", N("t_q2"))], inc="t_u")
                E("vector", (lambda v: v.tensor_tensor(
                    out=a1[:, A1T1:A1T1 + 1024],
                    in0=bass.AP(ps2a, 0, [[1024, P], [1, 1024]]),
                    in1=a1[:, A1IV:A1IV + 1024], op=ALU.mult)),
                    waits=[("t_g", N("t_g")), ("v_iv2", IVDONE),
                           ("a_si", N("a_si"))], inc="v_t1")
                E("scalar", (lambda sc: sc.activation(
                    out=a1[:, A1T1:A1T1 + 1024], in_=a1[:, A1T1:A1T1 + 1024],
                    func=AFT.Silu)),
                    waits=[("v_t1", N("v_t1"))], inc="a_si")
                E("vector", (lambda v: v.tensor_tensor(
                    out=a1[:, A1T3:A1T3 + 1024], in0=a1[:, A1T1:A1T1 + 1024],
                    in1=bass.AP(ps2c, 0, [[1024, P], [1, 1024]]),
                    op=ALU.mult)),
                    waits=[("a_si", N("a_si")), ("t_u", N("t_u"))], inc="v_t3")
                po = posH * 1024
                E("vector", (lambda v, po=po: v.tensor_mul(
                    a1[:, po:po + 1024], a1[:, A1T3:A1T3 + 1024],
                    a1[:, A1IV:A1IV + 1024])),
                    waits=[("t_d", N("t_d"))], inc="v_pp")
                if posH == 3:
                    vpp = N("v_pp")
                    for j in range(NDC):
                        dbl = j % 2
                        dma("sync",
                            bass.AP(pool, PWD5 + dbl * 512,
                                    [[5632, P], [P, 4], [1, P]]),
                            bass.AP(wdT, (grpH * 512) * D + j * P,
                                    [[D, P], [P * D, 4], [1, P]]),
                            waits=[("a_sd", N("a_sd") - (1 if j >= 2 else 0)),
                                   ("v_qm", 8)], inc="wdl")
                        E("scalar", (lambda sc, dbl=dbl: sc.activation(
                            out=a1[:, A1WD + dbl * 512:A1WD + dbl * 512 + 512],
                            in_=pool[:, PWD5 + dbl * 512:PWD5 + dbl * 512 + 512],
                            func=AFT.Sign)),
                            waits=[("wdl", N("wdl")),
                                   ("t_d", max(0, N("t_d") - 1))], inc="a_sd")
                        E("tensor", (lambda t, dbl=dbl, vpp=vpp: [
                            t.matmul(
                                ps2b[:, hf, :],
                                a1[:, A1WD + dbl * 512 + c4 * P:
                                   A1WD + dbl * 512 + (c4 + 1) * P],
                                a1[:, c4 * 1024 + hf * 512:
                                   c4 * 1024 + hf * 512 + 512],
                                start=(c4 == 0), stop=(c4 == 3),
                                skip_group_check=True)
                            for hf in range(2)
                            for c4 in range(4)][-1]),
                            waits=[("a_sd", N("a_sd")), ("v_pp", vpp),
                                   ("v_dn", N("v_dn"))], inc="t_d")
                        if grpH == 0:
                            E("vector", (lambda v, j=j: v.tensor_copy(
                                out=a2[:, j * 1024:(j + 1) * 1024],
                                in_=bass.AP(ps2b, 0, [[1024, P], [1, 1024]]))),
                                waits=[("t_d", N("t_d")), ("v_n", N("v_n"))],
                                inc="v_dn")
                        else:
                            E("vector", (lambda v, j=j: v.tensor_tensor(
                                out=a2[:, j * 1024:(j + 1) * 1024],
                                in0=bass.AP(ps2b, 0, [[1024, P], [1, 1024]]),
                                in1=a2[:, j * 1024:(j + 1) * 1024],
                                op=ALU.add)),
                                waits=[("t_d", N("t_d"))], inc="v_dn")
                        if debug and j == 0:
                            dma("gpsimd",
                                bass.AP(dbgT["xqD"], grpH * 1024,
                                        [[16384, P], [1, 1024]]),
                                a2[:, 0:1024],
                                waits=[("v_dn", N("v_dn"))], inc="dbg2")
            P5D = dict(v_dn=N("v_dn"), t_d=N("t_d"))

        if debug and 5 in phases:
            dma("sync", dbgT["dnD"].ap(), a2[:],
                waits=[("v_dn", P5D["v_dn"])], inc="dbg")
            dma("sync", dbgT["bfaD"].ap(), bfa[:],
                waits=[("v_tp2", N("v_tp2"))], inc="dbg")

        if 6 in phases:
            for i in range(NDC):
                E("scalar", (lambda sc, i=i: sc.activation(
                    out=a2[:, i * 1024:(i + 1) * 1024],
                    in_=a2[:, i * 1024:(i + 1) * 1024], func=AFT.Silu)),
                    waits=[("v_dn", P5D["v_dn"])] if i == 0 else [],
                    inc="a_gf")
            AGF = N("a_gf")
            for i in range(NDC):
                dma("sync",
                    bass.AP(pool, PSTX, [[5632, P], [1, 1024]]),
                    bass.AP(stscr, i * P * 1024, [[1024, P], [1, 1024]]),
                    waits=[("v_o6", N("v_o6")), ("t_d", P5D["t_d"]),
                           ("g_s4", P4D["g_s4"])], inc="sx6")
                dma("sync",
                    bass.AP(pool, PSTX + 1024, [[5632, P], [1, 1024]]),
                    bass.AP(xTo, i * P * 1024, [[1024, P], [1, 1024]]),
                    inc="sx6")
                E("vector", (lambda v, i=i: v.tensor_tensor(
                    out=a1[:, i * 1024:(i + 1) * 1024],
                    in0=pool[:, PSTX:PSTX + 1024],
                    in1=a2[:, i * 1024:(i + 1) * 1024], op=ALU.mult)),
                    waits=[("sx6", N("sx6")), ("a_gf", AGF)] +
                          ([("t_d", P5D["t_d"])] if i == 0 else []))
                E("vector", (lambda v, i=i: v.tensor_add(
                    a1[:, i * 1024:(i + 1) * 1024],
                    a1[:, i * 1024:(i + 1) * 1024],
                    pool[:, PSTX + 1024:PSTX + 2048])), inc="v_o6")
            VO6 = N("v_o6")
            for j in range(8):
                for i in range(NDC):
                    idx = j * NDC + i
                    psu = ps1a if idx % 2 == 0 else ps1b
                    E("tensor", (lambda t, j=j, i=i, psu=psu: t.transpose(
                        psu[:, 0:P], a1[:, i * 1024 + j * P:i * 1024 + j * P + P],
                        eye[:])),
                        waits=([("v_o6", VO6)] if idx == 0 else []) +
                              [("v_t6", max(0, N("v_t6") - 1))], inc="t_t6")
                    E("vector", (lambda v, i=i, psu=psu: v.tensor_copy(
                        out=pool[:, POTL + i * P:POTL + (i + 1) * P],
                        in_=psu[:, 0:P])),
                        waits=[("t_t6", N("t_t6")),
                               ("st", N("st"))],
                        inc="v_t6")
                dma("sync",
                    bass.AP(out_tok, j * P * D, [[D, P], [1, D]]),
                    bass.AP(pool, POTL, [[5632, P], [1, D]]),
                    waits=[("v_t6", N("v_t6"))], inc="st")
            E("vector", (lambda v: v.engine_nop()),
              waits=[("st", N("st"))])

        semset = {}
        for eng in ENGS:
            for (fn, w, inc, amt) in sch.ops[eng]:
                for (sm, v) in w:
                    semset.setdefault(sm, 0)
                if inc:
                    semset.setdefault(inc, 0)
        sems = {sm: e(nc.semaphore("sem_" + sm)) for sm in semset}
        block = e(nc.Block())

        def mk(en):
            def run(ee):
                for oi, (fn, w, inc, amt) in enumerate(sch.ops[en]):
                    for (sm, v) in w:
                        ee.wait_ge(sems[sm], v)
                    try:
                        inst = fn(ee)
                    except Exception as ex:
                        raise RuntimeError(
                            f"emit failed: engine={en} op_index={oi} "
                            f"inc={inc}: {type(ex).__name__}: {ex}") from ex
                    if inc is not None:
                        assert inst is not None, (en, inc)
                        inst.then_inc(sems[inc], amt)
            return run
        block.sync(mk("sync"))
        block.gpsimd(mk("gpsimd"))
        block.tensor(mk("tensor"))
        block.vector(mk("vector"))
        block.scalar(mk("scalar"))
    return nc


_TBL = {}


def _tables():
    if _TBL:
        return _TBL
    import jax
    jax.config.update("jax_enable_x64", True)
    import jax.numpy as jnp
    with jax.default_device(jax.devices("cpu")[0]):
        inv_freq = 1.0 / (10000.0 ** (jnp.arange(0, DH, 2, dtype=jnp.float32) / DH))
        freqs = jnp.arange(S, dtype=jnp.float32)[:, None] * inv_freq[None, :]
        cos = np.asarray(jnp.repeat(jnp.cos(freqs), 2, axis=-1))
        sin = np.asarray(jnp.repeat(jnp.sin(freqs), 2, axis=-1))
    sinN = sin.copy()
    sinN[:, :64] = -sinN[:, :64]
    _TBL["cos"] = np.ascontiguousarray(cos.T)
    _TBL["sinN"] = np.ascontiguousarray(sinN.T)
    tri = (np.arange(P)[:, None] <= np.arange(P)[None, :]).astype(np.float32)
    _TBL["tri4"] = np.tile(tri, (1, 4))
    m = np.zeros((P, 1024), dtype=np.float32)
    m[:, :512] = _TBL["tri4"]
    _TBL["masks"] = m
    _TBL["eye"] = np.eye(P, dtype=np.float32)
    return _TBL


def stage_inputs(L0_attn_x, L0_attn_Wq, L0_attn_Wk, L0_attn_Wv, L0_attn_Wo,
                 L0_ffn_W_g, L0_ffn_W_u, L0_ffn_W_d):
    t = _tables()
    xT = np.ascontiguousarray(np.asarray(L0_attn_x)[0].T)
    wkvT = np.ascontiguousarray(
        np.concatenate([np.asarray(L0_attn_Wk).T, np.asarray(L0_attn_Wv).T],
                       axis=1))
    wqT = np.ascontiguousarray(np.asarray(L0_attn_Wq).T)
    woT = np.ascontiguousarray(np.asarray(L0_attn_Wo).T)
    wgT = np.ascontiguousarray(np.asarray(L0_ffn_W_g).T)
    wuT = np.ascontiguousarray(np.asarray(L0_ffn_W_u).T)
    wdT = np.ascontiguousarray(np.asarray(L0_ffn_W_d).T)
    maps = []
    for c in range(8):
        tiles = snake_tiles(c)
        cols = np.concatenate([np.arange(tt * P, tt * P + P) for tt in tiles])
        cp = np.zeros((1, 8), dtype=np.int32)
        cp[0, 0] = c
        mtab = np.zeros((72, P, 512), dtype=np.float32)
        for j in range(8):
            tt = tiles[j]
            for b in range(9):
                i = 8 * j + b
                if i <= tt:
                    mtab[j * 9 + b] = 1.0
                elif i == tt + 1 and i <= 63:
                    mtab[j * 9 + b] = t["tri4"]
        maps.append(dict(
            xT=xT, xTo=np.ascontiguousarray(xT[:, cols]),
            wkvT=wkvT, wqT=wqT, woT=woT, wgT=wgT, wuT=wuT, wdT=wdT,
            cosF=t["cos"], sinF=t["sinN"],
            cosO=np.ascontiguousarray(t["cos"][:, cols]),
            sinO=np.ascontiguousarray(t["sinN"][:, cols]),
            eyeD=t["eye"], masksD=t["masks"], cpD=cp,
            mtabD=mtab))
    return maps


def kernel(**inputs):
    maps = stage_inputs(**inputs)
    nc = build_nc()
    res = run_bass_kernel_spmd(nc, maps, core_ids=list(range(8)))
    out = np.zeros((S, D), dtype=np.float32)
    for c in range(8):
        tiles = snake_tiles(c)
        oc = res.results[c]["out_tok"]
        for j, tt in enumerate(tiles):
            out[tt * P:(tt + 1) * P, :] = oc[j * P:(j + 1) * P, :]
    return out.reshape(1, S, D)


# version 07
# speedup vs baseline: 1.2333x; 1.2333x over previous
import numpy as np
import concourse.bass as bass
import concourse.mybir as mybir
from concourse.bass_utils import run_bass_kernel_spmd
from contextlib import ExitStack

F32 = mybir.dt.float32
F32R = mybir.dt.float32r
BF16 = mybir.dt.bfloat16
I32 = mybir.dt.int32
AFT = mybir.ActivationFunctionType
ALU = mybir.AluOpType
AXX = mybir.AxisListType.X

P = 128
S = 8192
D = 2048
NDC = 16
KVH = 4
HG = 4
H = 16
DH = 128
NTG = 16
SCALE = float(DH) ** -0.5
SQ128 = float(DH) ** 0.5
SHIFT = 40.0
L_PAIR = [19, 51, 83, 115]
BLK = 4
NCH = 72
P5_TRUNC = 0
P5_NFC = 64

ENGS = ("sync", "gpsimd", "tensor", "vector", "scalar")


def snake_tiles(c):
    return [8 * j + (c if j % 2 == 0 else 7 - c) for j in range(8)]


class Sched:
    def __init__(self):
        self.ops = {e: [] for e in ENGS}
        self.cnt = {}

    def n(self, sem):
        return self.cnt.get(sem, 0)

    def emit(self, eng, fn, waits=(), inc=None, amt=1):
        w = [(s, v) for (s, v) in waits if v > 0]
        self.ops[eng].append((fn, w, inc, amt))
        if inc is not None:
            self.cnt[inc] = self.cnt.get(inc, 0) + amt


def build_nc(phases=(1, 2, 3, 4, 5, 6), debug=False):
    nc = bass.Bass("TRN2", target_bir_lowering=False, detect_race_conditions=False)
    dt_scr = "ExternalOutput" if debug else "Internal"

    xT = nc.dram_tensor("xT", [D, S], F32, kind="ExternalInput")
    xTo = nc.dram_tensor("xTo", [D, 1024], F32, kind="ExternalInput")
    wkvT = nc.dram_tensor("wkvT", [D, 1024], F32, kind="ExternalInput")
    wqT = nc.dram_tensor("wqT", [D, D], F32, kind="ExternalInput")
    woT = nc.dram_tensor("woT", [D, D], F32, kind="ExternalInput")
    wgT = nc.dram_tensor("wgT", [D, S], F32, kind="ExternalInput")
    wuT = nc.dram_tensor("wuT", [D, S], F32, kind="ExternalInput")
    wdT = nc.dram_tensor("wdT", [S, D], F32, kind="ExternalInput")
    cosF = nc.dram_tensor("cosF", [P, S], F32, kind="ExternalInput")
    sinF = nc.dram_tensor("sinF", [P, S], F32, kind="ExternalInput")
    cosO = nc.dram_tensor("cosO", [P, 1024], F32, kind="ExternalInput")
    sinO = nc.dram_tensor("sinO", [P, 1024], F32, kind="ExternalInput")
    eyeD = nc.dram_tensor("eyeD", [P, P], F32, kind="ExternalInput")
    masksD = nc.dram_tensor("masksD", [P, 1024], F32, kind="ExternalInput")
    mtabD = nc.dram_tensor("mtabD", [72, P, 512], F32, kind="ExternalInput")
    cpD = nc.dram_tensor("cpD", [1, 8], I32, kind="ExternalInput")

    out_tok = nc.dram_tensor("out_tok", [1024, D], F32, kind="ExternalOutput")
    kscr = nc.dram_tensor("kscr", [NCH, KVH, P, P], F32, kind=dt_scr)
    vscr = nc.dram_tensor("vscr", [NCH, P, 512], F32, kind=dt_scr)
    q2scr = nc.dram_tensor("q2scr", [H, 1024], F32, kind=dt_scr)
    stscr = nc.dram_tensor("stscr", [D, 1024], F32, kind=dt_scr)
    dbgT = {}
    if debug:
        for nm, shp in (("qsD", [P, 16384]), ("oTD", [P, 16384]),
                        ("cvD", [P, 4096]), ("dnD", [P, 16384]),
                        ("xqD", [P, 16384]), ("a1D", [P, 16384])):
            dbgT[nm] = nc.dram_tensor(nm, shp, F32, kind="ExternalOutput")
        dbgT["bfaD"] = nc.dram_tensor("bfaD", [P, 16384], BF16,
                                      kind="ExternalOutput")

    with ExitStack() as ctx:
        e = ctx.enter_context
        a1 = e(nc.sbuf_tensor("a1", [P, 16384], F32))
        a2 = e(nc.sbuf_tensor("a2", [P, 16384], F32))
        bfa = e(nc.sbuf_tensor("bfa", [P, 16384], BF16))
        sgnb = e(nc.sbuf_tensor("sgnb", [P, 4096], BF16))
        pool = e(nc.sbuf_tensor("pool", [P, 5632], F32))
        masks = e(nc.sbuf_tensor("masks", [P, 1024], F32))
        eye = e(nc.sbuf_tensor("eye", [P, P], F32))
        ones128 = e(nc.sbuf_tensor("ones128", [P, 1], F32))
        onescol = e(nc.sbuf_tensor("onescol", [1, P], F32))
        onescbf = e(nc.sbuf_tensor("onescbf", [1, P], BF16))
        k2max = e(nc.sbuf_tensor("k2max", [1, 8], F32))
        s2row = e(nc.sbuf_tensor("s2row", [1, 8], F32))
        cpS = e(nc.sbuf_tensor("cpS", [1, 8], I32))
        epsA = e(nc.sbuf_tensor("epsA", [P, 4], F32))
        zcol2 = e(nc.sbuf_tensor("zcol2", [1, P], F32))
        zrow2 = e(nc.sbuf_tensor("zrow2", [1, 512], F32))
        ctmp = e(nc.sbuf_tensor("ctmp", [1, 1024], F32))
        rowb = e(nc.sbuf_tensor("rowb", [1, 1024], F32))
        maxab = e(nc.sbuf_tensor("maxab", [P, 1], F32))
        stok = e(nc.sbuf_tensor("stok", [P, 1], F32))
        ivtok = e(nc.sbuf_tensor("ivtok", [P, 8], F32))
        ps2a = e(nc.psum_tensor("ps2a", [P, 2, 512], F32))
        ps2b = e(nc.psum_tensor("ps2b", [P, 2, 512], F32))
        ps2c = e(nc.psum_tensor("ps2c", [P, 2, 512], F32))
        ps1a = e(nc.psum_tensor("ps1a", [P, 512], F32))
        ps1b = e(nc.psum_tensor("ps1b", [P, 512], F32))

        sch = Sched()
        E = sch.emit
        N = sch.n
        REGS = {}

        PCOS, PSIN, PKRAW, PKSH, PKOUT, PSQ, PVE, PK2 = (
            0, 512, 1024, 1536, 2048, 2560, 3584, 4096)
        PW2, PCO, PSO, PQRAW, PQSH, PSQ2 = 0, 2048, 3072, 4096, 4608, 5120
        PK, PV, PEXP, PZB, PMSK = 0, 1024, 4096, 5120, 2048
        PWG, PWU, PQR, PQM, PWD5 = 0, 2048, 0, 2048, 4096
        PSTX, POTL = 0, 2048
        A1PH, A1PL, A1T1, A1T3, A1PP, A1WD, A1IV = (
            0, 4096, 8192, 9216, 10240, 11264, 13312)

        def dma(eng, out_ap, in_ap, waits=(), inc=None):
            def fn(ee, o=out_ap, i=in_ap):
                return ee.dma_start(out=o(ee) if callable(o) else o,
                                    in_=i(ee) if callable(i) else i)
            E(eng, fn, waits, inc, 16)

        dma("sync", masks[:], masksD.ap(), inc="ldc")
        dma("sync", eye[:], eyeD.ap(), inc="ldc")
        dma("sync", cpS[:], cpD.ap(), inc="ldc")
        def fmemset(v):
            v.memset(ones128[:], 1.0)
            v.memset(onescol[:], 1.0)
            v.memset(onescbf[:], 1.0)
            v.memset(k2max[:], 0.0)
            v.memset(zcol2[:], 0.0)
            v.memset(zrow2[:], 0.0)
            v.memset(epsA[:, 0:1], 1e-5)
            v.memset(epsA[:, 1:2], 1e-8)
            v.memset(epsA[:, 2:3], 128.5)
            return v.memset(a2[:, 0:512], 0.0)
        E("vector", fmemset, inc="vz")
        for z in range(64, NCH):
            dma("gpsimd", bass.AP(kscr, z * 65536, [[512, P], [1, 512]]),
                a2[:, 0:512], waits=[("vz", 1)], inc="zw")
            dma("gpsimd", bass.AP(vscr, z * 65536, [[512, P], [1, 512]]),
                a2[:, 0:512], waits=[("vz", 1)], inc="zw")
        ZW = N("zw")
        dma("sync", bass.AP(a1, 0, [[16384, P], [1024, NDC], [1, 1024]]),
            bass.AP(wkvT, 0, [[1024, P], [P * 1024, NDC], [1, 1024]]), inc="ldw")

        def ln_group(xbuf, x_off, nb_xn, xn_off, ntk, eps, gate, first_waits,
                     sq0, sq1, cstride=None):
            eps_ap = epsA[0:1, 0:1] if eps == 1e-5 else epsA[0:1, 1:2]
            """LN of [D, ntk] feature-major: xbuf@x_off (chunk i at
            x_off+i*cstride) -> nb_xn@xn_off (chunk i at xn_off+i*cstride)."""
            cs = cstride if cstride is not None else ntk
            base_ts = N("t_sum")
            for i in range(NDC):
                sl = sq0 if i % 2 == 0 else sq1
                wsq = [gate] + first_waits + ([("vz", 1)] if i == 0 else [])
                wsq.append(("t_sum", base_ts if i < 2 else N("t_sum") - 1))
                E("scalar", (lambda sc, i=i, sl=sl: sc.activation(
                    out=pool[:, sl:sl + ntk],
                    in_=xbuf[:, x_off + i * cs:x_off + i * cs + ntk],
                    func=AFT.Square)), waits=wsq, inc="a_sq")
                wmm = [gate] + (first_waits + [("v_st", N("v_st")),
                                               ("vz", 1), ("t_bc", N("t_bc"))]
                                if i == 0 else [])
                E("tensor", (lambda t, i=i: t.matmul(
                    bass.AP(ps2c, 0, [[1024, 1], [1, ntk]]),
                    ones128[:], xbuf[:, x_off + i * cs:x_off + i * cs + ntk],
                    start=(i == 0), stop=(i == NDC - 1))), waits=wmm)
                E("tensor", (lambda t, i=i, sl=sl: t.matmul(
                    bass.AP(ps2c, 32 * 1024, [[1024, 1], [1, ntk]]),
                    ones128[:], pool[:, sl:sl + ntk],
                    start=(i == 0), stop=(i == NDC - 1))),
                    waits=[("a_sq", N("a_sq"))], inc="t_sum")
            ts = N("t_sum")
            m_ap = pool[0:1, sq0:sq0 + ntk]
            v_ap = pool[0:1, sq1:sq1 + ntk]
            r_ap = ctmp[0:1, 0:ntk]
            E("vector", (lambda v: v.tensor_scalar_mul(
                out=m_ap,
                in0=bass.AP(ps2c, 0, [[1024, 1], [1, ntk]]), scalar1=1.0 / D)),
                waits=[("t_sum", ts)])
            E("vector", (lambda v: v.tensor_mul(v_ap, m_ap, m_ap)))
            E("vector", (lambda v: v.scalar_tensor_tensor(
                out=v_ap,
                in0=bass.AP(ps2c, 32 * 1024, [[1024, 1], [1, ntk]]),
                scalar=1.0 / D, in1=v_ap,
                op0=ALU.mult, op1=ALU.subtract)), inc="v_st")
            E("scalar", (lambda sc: sc.activation(
                out=r_ap, in_=v_ap, func=AFT.Sqrt,
                bias=eps_ap)), waits=[("v_st", N("v_st")), ("a_cv", N("a_cv"))],
                inc="a_rs")
            E("vector", (lambda v: v.reciprocal(out=r_ap, in_=r_ap)),
                waits=[("a_rs", N("a_rs"))], inc="v_rr")
            E("tensor", (lambda t: t.matmul(
                ps1a[:, 0:ntk], onescol[:], m_ap,
                start=True, stop=True)),
                waits=[("v_rr", N("v_rr")), ("v_xn", N("v_xn"))])
            E("tensor", (lambda t: t.matmul(
                ps1b[:, 0:ntk], onescol[:], r_ap,
                start=True, stop=True)), inc="t_bc")
            tbc = N("t_bc")
            for i in range(NDC):
                E("vector", (lambda v, i=i: v.tensor_sub(
                    nb_xn[:, xn_off + i * cs:xn_off + i * cs + ntk],
                    xbuf[:, x_off + i * cs:x_off + i * cs + ntk],
                    ps1a[:, 0:ntk])), waits=[("t_bc", tbc)] if i == 0 else [])
                E("vector", (lambda v, i=i: v.tensor_mul(
                    nb_xn[:, xn_off + i * cs:xn_off + i * cs + ntk],
                    nb_xn[:, xn_off + i * cs:xn_off + i * cs + ntk],
                    ps1b[:, 0:ntk])), inc="v_xn")

        if 1 in phases:
            ldw = N("ldw")
            for tg in range(NTG):
                dma("sync",
                    bass.AP(a2, 8192, [[16384, P], [512, NDC], [1, 512]]),
                    bass.AP(xT, tg * 512, [[S, P], [P * S, NDC], [1, 512]]),
                    waits=[("v_xn", N("v_xn"))], inc="xs")
                dma("sync", pool[:, PCOS:PCOS + 512],
                    bass.AP(cosF, tg * 512, [[S, P], [1, 512]]),
                    waits=[("v_ko", N("v_ko"))], inc="cs")
                dma("sync", pool[:, PSIN:PSIN + 512],
                    bass.AP(sinF, tg * 512, [[S, P], [1, 512]]), inc="cs")
                ln_group(a2, 8192, a2, 0, 512, 1e-5, ("xs", N("xs")),
                         [("t_k", N("t_k")), ("t_v", N("t_v"))], PSQ, PSQ + 512)
                vxn = N("v_xn")
                for g in range(KVH):
                    bank = g % 2
                    E("tensor", (lambda t, g=g, bank=bank: [
                        t.matmul(ps2a[:, bank, :],
                                 a1[:, i * 1024 + g * P:i * 1024 + g * P + P],
                                 a2[:, i * 512:(i + 1) * 512],
                                 start=(i == 0), stop=(i == NDC - 1))
                        for i in range(NDC)][-1]),
                        waits=[("v_xn", vxn), ("ldw", ldw),
                               ("v_kc", max(0, N("v_kc") - 1))], inc="t_k")
                    E("vector", (lambda v, bank=bank: v.tensor_copy(
                        out=pool[:, PKRAW:PKRAW + 512], in_=ps2a[:, bank, :])),
                        waits=[("t_k", N("t_k")), ("g_sh", N("g_sh"))],
                        inc="v_kc")
                    dma("gpsimd", pool[0:64, PKSH:PKSH + 512],
                        pool[64:128, PKRAW:PKRAW + 512],
                        waits=[("v_kc", N("v_kc"))], inc="g_sh")
                    dma("gpsimd", pool[64:128, PKSH:PKSH + 512],
                        pool[0:64, PKRAW:PKRAW + 512], inc="g_sh")
                    E("vector", (lambda v: v.tensor_mul(
                        pool[:, PKRAW:PKRAW + 512], pool[:, PKRAW:PKRAW + 512],
                        pool[:, PCOS:PCOS + 512])),
                        waits=[("g_sh", N("g_sh")), ("cs", N("cs")),
                               ("kw", N("kw")), ("a_k2", N("a_k2"))])
                    E("vector", (lambda v: v.tensor_mul(
                        pool[:, PKSH:PKSH + 512], pool[:, PKSH:PKSH + 512],
                        pool[:, PSIN:PSIN + 512])))
                    E("vector", (lambda v: v.tensor_add(
                        pool[:, PKOUT:PKOUT + 512], pool[:, PKRAW:PKRAW + 512],
                        pool[:, PKSH:PKSH + 512])), inc="v_ko")
                    vko = N("v_ko")
                    E("scalar", (lambda sc: sc.activation(
                        out=pool[:, PK2:PK2 + 512],
                        in_=pool[:, PKOUT:PKOUT + 512], func=AFT.Square)),
                        waits=[("v_ko", vko), ("t_k2", N("t_k2"))], inc="a_k2")
                    E("tensor", (lambda t: t.matmul(
                        bass.AP(ps2c, 512, [[1024, 1], [1, 512]]),
                        ones128[:], pool[:, PK2:PK2 + 512],
                        start=True, stop=True)),
                        waits=[("a_k2", N("a_k2")), ("v_km", N("v_km"))],
                        inc="t_k2")
                    E("vector", (lambda v: v.reduce_max(
                        ctmp[0:1, 512:513],
                        bass.AP(ps2c, 512, [[1024, 1], [1, 512]]), axis=AXX)),
                        waits=[("t_k2", N("t_k2"))], inc="v_kr")
                    E("vector", (lambda v, g=g: v.tensor_max(
                        k2max[0:1, g:g + 1], k2max[0:1, g:g + 1],
                        ctmp[0:1, 512:513])),
                        waits=[("v_kr", N("v_kr"))], inc="v_km")
                    dma("gpsimd",
                        bass.AP(kscr, (4 * tg) * 65536 + g * 16384,
                                [[P, P], [65536, 4], [1, P]]),
                        pool[:, PKOUT:PKOUT + 512],
                        waits=[("v_ko", vko), ("zw", ZW)], inc="kw")
                for tt in range(4):
                    bank = tt % 2
                    E("tensor", (lambda t, tt=tt, bank=bank: [
                        t.matmul(ps2b[:, bank, :],
                                 a2[:, i * 512 + tt * P:i * 512 + tt * P + P],
                                 a1[:, i * 1024 + 512:(i + 1) * 1024],
                                 start=(i == 0), stop=(i == NDC - 1))
                        for i in range(NDC)][-1]),
                        waits=[("v_xn", vxn), ("ldw", ldw),
                               ("v_ve", max(0, N("v_ve") - 1))], inc="t_v")
                    E("vector", (lambda v, bank=bank: v.tensor_copy(
                        out=pool[:, PVE:PVE + 512], in_=ps2b[:, bank, :])),
                        waits=[("t_v", N("t_v")), ("vw", N("vw"))],
                        inc="v_ve")
                    dma("gpsimd",
                        bass.AP(vscr, (4 * tg + tt) * 65536, [[512, P], [1, 512]]),
                        pool[:, PVE:PVE + 512],
                        waits=[("v_ve", N("v_ve")), ("zw", ZW)], inc="vw")
            E("vector", (lambda v: v.tensor_scalar_mul(
                out=s2row[:], in0=k2max[:], scalar1=1.0 / 128.0)),
                waits=[("v_km", N("v_km"))], inc="v_s2")

        P1D = dict(kw=N("kw"), vw=N("vw"), v_s2=N("v_s2"),
                   v_ko=N("v_ko"), a_k2=N("a_k2"), cs=N("cs"),
                   g_sh=N("g_sh"), v_ve=N("v_ve"), t_k2=N("t_k2"))

        if 2 in phases:
            dma("sync", pool[:, PCO:PCO + 1024], cosO.ap(),
                waits=[("kw", P1D["kw"]), ("a_k2", P1D["a_k2"]),
                       ("v_ko", P1D["v_ko"]), ("v_ve", P1D["v_ve"]),
                       ("vw", P1D["vw"])], inc="cso")
            dma("sync", pool[:, PSO:PSO + 1024], sinO.ap(), inc="cso")
            for grp in range(2):
                dma("sync",
                    bass.AP(a2, 8192, [[16384, P], [512, NDC], [1, 512]]),
                    bass.AP(xTo, grp * 512, [[1024, P], [P * 1024, NDC], [1, 512]]),
                    waits=[("v_xn", N("v_xn"))], inc="xs")
                ln_group(a2, 8192, a2, 0, 512, 1e-5, ("xs", N("xs")),
                         [("t_q", N("t_q")), ("v_qo", N("v_qo")),
                          ("v_ko", P1D["v_ko"]), ("t_k2", P1D["t_k2"])],
                         PQRAW, PQSH)
                vxn = N("v_xn")
                for h in range(H):
                    g, hg = h // HG, h % HG
                    bank = h % 2
                    dma("sync",
                        bass.AP(pool, PW2, [[5632, P], [P, NDC], [1, P]]),
                        bass.AP(wqT, h * P, [[D, P], [P * D, NDC], [1, P]]),
                        waits=[("t_q", N("t_q")), ("v_ko", P1D["v_ko"]),
                               ("g_sh", P1D["g_sh"])], inc="wqs")
                    E("tensor", (lambda t, bank=bank: [
                        t.matmul(ps2a[:, bank, :],
                                 pool[:, PW2 + i * P:PW2 + (i + 1) * P],
                                 a2[:, i * 512:(i + 1) * 512],
                                 start=(i == 0), stop=(i == NDC - 1))
                        for i in range(NDC)][-1]),
                        waits=[("v_xn", vxn), ("wqs", N("wqs")),
                               ("v_qc", max(0, N("v_qc") - 1))], inc="t_q")
                    E("vector", (lambda v, bank=bank: v.tensor_copy(
                        out=pool[:, PQRAW:PQRAW + 512], in_=ps2a[:, bank, :])),
                        waits=[("t_q", N("t_q")), ("g_qs", N("g_qs")),
                               ("a_sq", N("a_sq"))], inc="v_qc")
                    dma("gpsimd", pool[0:64, PQSH:PQSH + 512],
                        pool[64:128, PQRAW:PQRAW + 512],
                        waits=[("v_qc", N("v_qc"))], inc="g_qs")
                    dma("gpsimd", pool[64:128, PQSH:PQSH + 512],
                        pool[0:64, PQRAW:PQRAW + 512], inc="g_qs")
                    qcol = g * 4096 + hg * 1024 + grp * 512
                    E("vector", (lambda v, grp=grp: v.tensor_mul(
                        pool[:, PQRAW:PQRAW + 512], pool[:, PQRAW:PQRAW + 512],
                        pool[:, PCO + grp * 512:PCO + grp * 512 + 512])),
                        waits=[("g_qs", N("g_qs")), ("cso", N("cso"))])
                    E("vector", (lambda v, grp=grp: v.tensor_mul(
                        pool[:, PQSH:PQSH + 512], pool[:, PQSH:PQSH + 512],
                        pool[:, PSO + grp * 512:PSO + grp * 512 + 512])))
                    E("vector", (lambda v, qcol=qcol: v.tensor_add(
                        a1[:, qcol:qcol + 512], pool[:, PQRAW:PQRAW + 512],
                        pool[:, PQSH:PQSH + 512])), inc="v_qo")
                    E("scalar", (lambda sc, qcol=qcol: sc.activation(
                        out=pool[:, PSQ2:PSQ2 + 512], in_=a1[:, qcol:qcol + 512],
                        func=AFT.Square)),
                        waits=[("v_qo", N("v_qo")), ("t_q2", N("t_q2"))],
                        inc="a_q2")
                    E("tensor", (lambda t: t.matmul(
                        bass.AP(ps2c, 512, [[1024, 1], [1, 512]]),
                        ones128[:], pool[:, PSQ2:PSQ2 + 512],
                        start=True, stop=True)),
                        waits=[("a_q2", N("a_q2")), ("v_q2", N("v_q2")),
                               ("v_km", P1D["v_s2"] and N("v_km"))], inc="t_q2")
                    E("vector", (lambda v: v.tensor_copy(
                        out=rowb[0:1, 0:512],
                        in_=bass.AP(ps2c, 512, [[1024, 1], [1, 512]]))),
                        waits=[("t_q2", N("t_q2")), ("g_q2", N("g_q2"))],
                        inc="v_q2")
                    dma("gpsimd",
                        bass.AP(q2scr, h * 1024 + grp * 512, [[512, 1], [1, 512]]),
                        rowb[0:1, 0:512],
                        waits=[("v_q2", N("v_q2"))], inc="g_q2")

        P2D = dict(v_qo=N("v_qo"), g_q2=N("g_q2"), t_q=N("t_q"),
                   v_qc=N("v_qc"))

        NIT = 0
        if 3 in phases:
            NJ = [8 * j + 9 for j in range(8)]
            sections = []
            iters = []
            it0 = 0
            for g in range(KVH):
                for j in range(8):
                    for i in range(NJ[j]):
                        iters.append((g, j, i))
                    sections.append((g, j, it0, it0 + NJ[j] - 1))
                    it0 += NJ[j]
            NIT = len(iters)
            blocks = []
            bidx = 0
            it0 = 0
            for g in range(KVH):
                for j in range(8):
                    nblk = (NJ[j] + BLK - 1) // BLK
                    for n in range(nblk):
                        f = it0 + n * BLK
                        l = it0 + min((n + 1) * BLK, NJ[j]) - 1
                        blocks.append((g, j, n, bidx, f, l))
                        bidx += 1
                    it0 += NJ[j]
            it2blk = {}
            for b in blocks:
                for it in range(b[4], b[5] + 1):
                    it2blk[it] = b
            masked = {}
            for it, (g, j, i) in enumerate(iters):
                if i >= 8 * j:
                    masked[it] = (j, i - 8 * j)
            masked_list = sorted(masked)
            mask_pos = {it: k + 1 for k, it in enumerate(masked_list)}
            mask_idx = {it: k for k, it in enumerate(masked_list)}

            ldc = N("ldc")
            BAR3 = [("kw", P1D["kw"]), ("vw", P1D["vw"]), ("ldc", ldc),
                    ("v_qo", P2D["v_qo"]), ("g_q2", P2D["g_q2"]),
                    ("zw", ZW), ("v_s2", P1D["v_s2"])]
            TSC0 = N("t_sc")
            TZA0 = N("t_za")
            AE0 = N("a_e")

            def sc_cnt(it):
                return TSC0 + it + 1

            def za_cnt(it):
                return TZA0 + it + 1

            def e_cnt(it):
                return AE0 + it + 1

            def emit_cvec(g, tsc_prev):
                for hg in range(HG):
                    h = 4 * g + hg
                    dma("sync",
                        ctmp[0:1, :],
                        bass.AP(q2scr, h * 1024, [[1024, 1], [1, 1024]]),
                        waits=BAR3 + [("a_cv", N("a_cv")), ("v_cv", N("v_cv"))],
                        inc="cq")
                    E("scalar", (lambda sc, g=g: sc.activation(
                        out=ctmp[0:1, :], in_=ctmp[0:1, :], func=AFT.Sqrt,
                        scale=s2row[0:1, g:g + 1])),
                        waits=[("cq", N("cq")), ("v_cv", N("v_cv"))],
                        inc="a_cv")
                    E("vector", (lambda v, hg=hg: v.tensor_scalar(
                        out=bfa[0:1, hg * 1024:hg * 1024 + 1024],
                        in0=ctmp[0:1, :],
                        scalar1=-SQ128, scalar2=SHIFT * SQ128,
                        op0=ALU.mult, op1=ALU.add)),
                        waits=[("a_cv", N("a_cv")), ("t_sc", tsc_prev)],
                        inc="v_cv")

            def emit_block_loads(b):
                g, j, n, bi, f, l = b
                dbl = bi % 2
                kwv = ("t_sc", sc_cnt(blocks[bi - 2][5])) if bi >= 2                     else ("t_sc", 0)
                vwv = ("t_za", za_cnt(blocks[bi - 2][5])) if bi >= 2                     else ("t_za", 0)
                dma("sync",
                    bass.AP(pool, PK + dbl * 512, [[5632, P], [P, 4], [1, P]]),
                    bass.AP(kscr, 4 * n * 65536 + g * 16384,
                            [[P, P], [65536, 4], [1, P]]),
                    waits=BAR3 + [kwv], inc="kl")
                dma("gpsimd",
                    bass.AP(pool, PV + dbl * 512, [[5632, P], [P, 4], [1, P]]),
                    bass.AP(vscr, 4 * n * 65536 + g * P,
                            [[512, P], [65536, 4], [1, P]]),
                    waits=BAR3 + [vwv], inc="vl")

            mask_loaded = [0]

            def emit_mask_load_mi(mi):
                it = masked_list[mi]
                jj, b = masked[it]
                wv = ("v_m", mi - 3) if mi >= 4 else ("v_m", 0)
                dma("gpsimd",
                    pool[:, PMSK + (mi % 4) * 512:PMSK + (mi % 4) * 512 + 512],
                    bass.AP(mtabD, (jj * 9 + b) * 65536, [[512, P], [1, 512]]),
                    waits=BAR3 + [wv], inc="ml")
                mask_loaded[0] = mi + 1

            def emit_iter(it, si, s_f):
                g, j, i = iters[it]
                b = it2blk[it]
                bi, n = b[3], b[2]
                dbl = bi % 2
                sbank = it % 2
                sec = g * 8 + j
                kslot = PK + dbl * 512 + (i - 4 * n) * P
                vslot = PV + dbl * 512 + (i - 4 * n) * P
                qcol = g * 4096 + j * P

                ccol = j * P

                def fsc(t, kslot=kslot, qcol=qcol, ccol=ccol, sbank=sbank):
                    t.matmul(ps2a[:, sbank, :],
                             pool[:, kslot:kslot + P],
                             bass.AP(a1, qcol, [[16384, P], [1024, 4], [1, P]]),
                             start=True, stop=False, skip_group_check=True)
                    return t.matmul(
                        ps2a[:, sbank, :], onescbf[:],
                        bass.AP(bfa, ccol, [[16384, 1], [1024, 4], [1, P]]),
                        start=False, stop=True, skip_group_check=True)
                w = [("kl", kdma_after[bi] * 16), ("v_cv", N("v_cv"))]
                if it >= 2:
                    w.append(("a_e", e_cnt(it - 2)))
                E("tensor", fsc, waits=w, inc="t_sc")

                we = [("t_sc", sc_cnt(it))]
                if it >= 2:
                    we.append(("t_za", za_cnt(it - 2)))
                E("scalar", (lambda sc, sbank=sbank: sc.activation(
                    out=pool[:, PEXP + sbank * 512:PEXP + sbank * 512 + 512],
                    in_=ps2a[:, sbank, :], func=AFT.Exp, scale=SCALE)),
                    waits=we, inc="a_e")

                if it in masked:
                    mi = mask_idx[it]
                    E("vector", (lambda v, sbank=sbank, mi=mi: v.tensor_tensor(
                        out=pool[:, PEXP + sbank * 512:PEXP + sbank * 512 + 512],
                        in0=pool[:, PEXP + sbank * 512:PEXP + sbank * 512 + 512],
                        in1=pool[:, PMSK + (mi % 4) * 512:
                                 PMSK + (mi % 4) * 512 + 512],
                        op=ALU.mult)),
                        waits=[("a_e", e_cnt(it)), ("ml", mdma_after[it] * 16)],
                        inc="v_m")

                zps = ps1a if sec % 2 == 0 else ps1b
                obank = sec % 2
                first = (it == s_f)

                def fza(t, vslot=vslot, sbank=sbank, zps=zps, obank=obank,
                        first=first):
                    pexp_ap = pool[:, PEXP + sbank * 512:PEXP + sbank * 512 + 512]
                    t.matmul(bass.AP(zps, 0, [[512, 1], [1, 512]]),
                             ones128[:], pexp_ap, start=first, stop=False,
                             skip_group_check=True)
                    return t.matmul(ps2b[:, obank, :],
                                    pool[:, vslot:vslot + P], pexp_ap,
                                    start=first, stop=False,
                                    skip_group_check=True)
                wz = [("vl", vdma_after[bi] * 16)]
                if first:
                    wz += [("a_rz", max(0, si - 1)),
                           ("v_e3", max(0, 2 * si - 2))]
                if it in masked:
                    wz.append(("v_m", mask_pos[it]))
                else:
                    wz.append(("a_e", e_cnt(it)))
                E("tensor", fza, waits=wz, inc="t_za")

            kdma_after = {}
            vdma_after = {}
            kc = 0
            for b in blocks:
                kc += 1
                kdma_after[b[3]] = kc
                vdma_after[b[3]] = kc
            mdma_after = {it: mask_idx[it] + 1 for it in masked}

            for si, (g, j, s_f, s_l) in enumerate(sections):
                if j == 0:
                    emit_cvec(g, TSC0 + s_f)
                obank = si % 2
                zps = ps1a if si % 2 == 0 else ps1b
                sec_blocks = [b for b in blocks if b[0] == g and b[1] == j]
                for k2 in range(min(2, len(sec_blocks))):
                    emit_block_loads(sec_blocks[k2])
                if si == 0:
                    for _mi in range(4):
                        emit_mask_load_mi(_mi)
                for it in range(s_f, s_l + 1):
                    i = it - s_f
                    if it in masked:
                        mi = mask_idx[it]
                        while mask_loaded[0] <= mi + 4 and \
                                mask_loaded[0] < len(masked_list):
                            emit_mask_load_mi(mask_loaded[0])
                    if i % BLK == 0:
                        nb = i // BLK + 2
                        if nb < len(sec_blocks):
                            emit_block_loads(sec_blocks[nb])
                    emit_iter(it, si, s_f)
                za_end = za_cnt(s_l)
                E("vector", (lambda v, zps=zps: v.reciprocal(
                    out=rowb[0:1, 0:512],
                    in_=bass.AP(zps, 0, [[512, 1], [1, 512]]))),
                    waits=[("t_za", za_end), ("t_ob", N("t_ob"))], inc="a_rz")
                E("tensor", (lambda t: t.matmul(
                    ps2c[:, 1, :], onescol[:], rowb[0:1, 0:512],
                    start=True, stop=True, skip_group_check=True)),
                    waits=[("a_rz", N("a_rz")), ("v_e3", max(0, 2 * si - 1))],
                    inc="t_ob")
                E("vector", (lambda v: v.tensor_copy(
                    out=pool[:, PZB:PZB + 512], in_=ps2c[:, 1, :])),
                    waits=[("t_ob", N("t_ob"))], inc="v_e3")
                ocol = 4 * g * 1024 + j * P
                E("vector", (lambda v, obank=obank, ocol=ocol: v.tensor_tensor(
                    out=bass.AP(a2, ocol, [[16384, P], [1024, 4], [1, P]]),
                    in0=bass.AP(ps2b, obank * 512, [[1024, P], [P, 4], [1, P]]),
                    in1=bass.AP(pool, PZB, [[5632, P], [P, 4], [1, P]]),
                    op=ALU.mult)), inc="v_e3")

        P3D = dict(v_e3=N("v_e3"), t_za=N("t_za"), t_sc=N("t_sc"),
                   a_e=N("a_e"), kl=N("kl"), vl=N("vl"), v_se=N("v_se"))

        if debug and 2 in phases:
            dma("sync", dbgT["qsD"].ap(), a1[:],
                waits=[("v_qo", P2D["v_qo"])] +
                      ([("t_sc", P3D["t_sc"])] if 3 in phases else []),
                inc="dbg")
        if debug and 3 in phases:
            dma("sync", dbgT["oTD"].ap(), a2[:],
                waits=[("v_e3", P3D["v_e3"])], inc="dbg")
            dma("gpsimd", bass.AP(dbgT["cvD"], 0, [[4096, 1], [1, 4096]]),
                bfa[0:1, 0:4096], waits=[("v_cv", N("v_cv"))], inc="dbg2")

        if 4 in phases:
            for grp in range(2):
                for j in range(NDC):
                    bank = (grp * NDC + j) % 2
                    dma("sync",
                        bass.AP(pool, PW2, [[5632, P], [P, NDC], [1, P]]),
                        bass.AP(woT, j * P, [[D, P], [P * D, NDC], [1, P]]),
                        waits=[("t_s4", N("t_s4")), ("v_e3", P3D["v_e3"]),
                               ("kl", P3D["kl"])], inc="wos")
                    E("tensor", (lambda t, bank=bank, grp=grp: [
                        t.matmul(ps2a[:, bank, :],
                                 pool[:, PW2 + i * P:PW2 + (i + 1) * P],
                                 a2[:, i * 1024 + grp * 512:
                                    i * 1024 + grp * 512 + 512],
                                 start=(i == 0), stop=(i == NDC - 1))
                        for i in range(NDC)][-1]),
                        waits=[("wos", N("wos")), ("v_e3", P3D["v_e3"]),
                               ("a_e", P3D["a_e"]),
                               ("v_s4", max(0, N("v_s4") - 1))], inc="t_s4")
                    E("vector", (lambda v, bank=bank: v.tensor_copy(
                        out=pool[:, PVE:PVE + 512], in_=ps2a[:, bank, :])),
                        waits=[("t_s4", N("t_s4")), ("g_s4", N("g_s4")),
                               ("vl", P3D["vl"])], inc="v_s4")
                    dma("gpsimd",
                        bass.AP(stscr, j * P * 1024 + grp * 512,
                                [[1024, P], [1, 512]]),
                        pool[:, PVE:PVE + 512],
                        waits=[("v_s4", N("v_s4"))], inc="g_s4")

        P4D = dict(g_s4=N("g_s4"), t_s4=N("t_s4"), v_s4=N("v_s4"))

        P5D = {}
        if 5 in phases:
            B5 = [("g_s4", P4D["g_s4"]), ("t_s4", P4D["t_s4"]),
                  ("v_s4", P4D["v_s4"])]
            dma("sync",
                bass.AP(a1, 0, [[16384, P], [1024, NDC], [1, 1024]]),
                bass.AP(xTo, 0, [[1024, P], [P * 1024, NDC], [1, 1024]]),
                waits=B5 + [("t_sc", P3D["t_sc"]), ("v_xn", N("v_xn"))],
                inc="x5")
            for rnd, eps in ((0, 1e-5), (1, 1e-8)):
                for cg in range(2):
                    ln_group(a1, cg * 512, a1, cg * 512, 512, eps,
                             ("x5", N("x5")),
                             [("v_s4", P4D["v_s4"]), ("t_q2", N("t_q2")),
                              ("v_xn", N("v_xn")), ("v_se", P3D["v_se"])],
                             PQR, PQM, cstride=1024)
            VXN5 = N("v_xn")
            for j in range(8):
                for i in range(NDC):
                    idx = j * NDC + i
                    psu = ps1a if idx % 2 == 0 else ps1b
                    E("tensor", (lambda t, j=j, i=i, psu=psu: t.transpose(
                        psu[:, 0:P], a1[:, i * 1024 + j * P:i * 1024 + j * P + P],
                        eye[:])),
                        waits=([("v_xn", VXN5), ("v_e3", P3D["v_e3"])]
                               if idx == 0 else []) +
                              [("v_tp", max(0, N("v_tp") - 1))], inc="t_tp")
                    E("vector", (lambda v, j=j, i=i, psu=psu: v.tensor_copy(
                        out=a2[:, j * D + i * P:j * D + (i + 1) * P],
                        in_=psu[:, 0:P])),
                        waits=[("t_tp", N("t_tp"))], inc="v_tp")
            for j in range(8):
                E("vector", (lambda v, j=j: v.reduce_max(
                    maxab[:], a2[:, j * D:(j + 1) * D], axis=AXX,
                    apply_absolute_value=True)),
                    waits=[("v_tp", 16 * (j + 1)), ("a_u", N("a_u"))],
                    inc="v_qm")
                E("vector", (lambda v, j=j: v.tensor_scalar_mul(
                    out=ivtok[:, j:j + 1], in0=maxab[:], scalar1=1.0 / 127.0)),
                    waits=[("v_qm", N("v_qm"))], inc="v_iv0")
                E("vector", (lambda v: v.reciprocal(
                    out=stok[:], in_=maxab[:])),
                    waits=[("v_qm", N("v_qm")), ("a_u", N("a_u"))], inc="v_rc")
                E("vector", (lambda v: v.tensor_scalar_mul(
                    out=stok[:], in0=stok[:], scalar1=127.0)),
                    waits=[("v_rc", N("v_rc"))], inc="a_st")
                E("vector", (lambda v, j=j: v.tensor_scalar(
                    out=a2[:, j * D:(j + 1) * D], in0=a2[:, j * D:(j + 1) * D],
                    scalar1=stok[:], scalar2=128.5,
                    op0=ALU.mult, op1=ALU.add)),
                    waits=[("a_st", N("a_st"))], inc="a_u")
                E("vector", (lambda v, j=j: v.tensor_copy(
                    out=pool[:, PQR:PQR + D], in_=a2[:, j * D:(j + 1) * D])),
                    waits=[("a_u", N("a_u"))])
                for k in range(7, -1, -1):
                    cst = float(1 << k)
                    E("vector", (lambda v, cst=cst: v.tensor_single_scalar(
                        out=pool[:, PQM:PQM + D], in_=pool[:, PQR:PQR + D],
                        scalar=cst, op=ALU.is_ge)))
                    E("vector", (lambda v, cst=cst: v.tensor_scalar_mul(
                        out=pool[:, PQM:PQM + D], in0=pool[:, PQM:PQM + D],
                        scalar1=cst)))
                    E("vector", (lambda v: v.tensor_sub(
                        pool[:, PQR:PQR + D], pool[:, PQR:PQR + D],
                        pool[:, PQM:PQM + D])))
                E("vector", (lambda v, j=j: v.scalar_tensor_tensor(
                    out=a2[:, j * D:(j + 1) * D], in0=a2[:, j * D:(j + 1) * D],
                    scalar=-128.0, in1=pool[:, PQR:PQR + D],
                    op0=ALU.add, op1=ALU.subtract)), inc="v_n")
            for j in range(8):
                for i in range(NDC):
                    idx = j * NDC + i
                    psu = ps1a if idx % 2 == 0 else ps1b
                    E("tensor", (lambda t, j=j, i=i, psu=psu: t.transpose(
                        psu[:, 0:P], a2[:, j * D + i * P:j * D + (i + 1) * P],
                        eye[:])),
                        waits=[("v_n", N("v_n"))] +
                              [("v_tp2", max(0, N("v_tp2") - 1))], inc="t_tp2")
                    E("vector", (lambda v, j=j, i=i, psu=psu: v.tensor_copy(
                        out=bfa[:, i * 1024 + j * P:i * 1024 + j * P + P],
                        in_=psu[:, 0:P])),
                        waits=[("t_tp2", N("t_tp2"))], inc="v_tp2")
            XQDONE = N("v_tp2")
            if P5_TRUNC:
                sch.ops
            for j in range(8):
                psu = ps1a if j % 2 == 0 else ps1b
                E("tensor", (lambda t, psu=psu, j=j: t.transpose(
                    psu[0:1, 0:P], ivtok[:, j:j + 1], eye[:])),
                    waits=[("v_iv0", 8), ("v_iv1", max(0, N("v_iv1") - 1)),
                           ("v_tp2", XQDONE)], inc="t_iv")
                E("vector", (lambda v, psu=psu, j=j: v.tensor_copy(
                    out=rowb[0:1, j * P:(j + 1) * P], in_=psu[0:1, 0:P])),
                    waits=[("t_iv", N("t_iv"))], inc="v_iv1")
            for hf in range(2):
                E("tensor", (lambda t, hf=hf: t.matmul(
                    ps1a[:, 0:512], onescol[:],
                    rowb[0:1, hf * 512:hf * 512 + 512],
                    start=True, stop=True, skip_group_check=True)),
                    waits=[("v_iv1", 8), ("v_iv2", N("v_iv2"))], inc="t_iv2")
                E("vector", (lambda v, hf=hf: v.tensor_copy(
                    out=a1[:, A1IV + hf * 512:A1IV + hf * 512 + 512],
                    in_=ps1a[:, 0:512])),
                    waits=[("t_iv2", N("t_iv2"))], inc="v_iv2")
            IVDONE = N("v_iv2")
            for fc in range(P5_NFC):
                grpH, posH = fc // 4, fc % 4
                dma("sync",
                    bass.AP(pool, PWG, [[5632, P], [P, NDC], [1, P]]),
                    bass.AP(wgT, fc * P, [[S, P], [P * S, NDC], [1, P]]),
                    waits=[("a_sg", N("a_sg")), ("x5", N("x5")),
                           ("v_n", N("v_n")), ("a_u", N("a_u"))],
                    inc="wgl")
                dma("sync",
                    bass.AP(pool, PWU, [[5632, P], [P, NDC], [1, P]]),
                    bass.AP(wuT, fc * P, [[S, P], [P * S, NDC], [1, P]]),
                    waits=[("a_su", N("a_su")), ("v_n", N("v_n"))], inc="wul")
                E("scalar", (lambda sc: sc.activation(
                    out=sgnb[:, 0:2048], in_=pool[:, PWG:PWG + 2048],
                    func=AFT.Sign)),
                    waits=[("wgl", N("wgl")), ("t_g", N("t_g"))],
                    inc="a_sg")
                E("scalar", (lambda sc: sc.activation(
                    out=sgnb[:, 2048:4096], in_=pool[:, PWU:PWU + 2048],
                    func=AFT.Sign)),
                    waits=[("wul", N("wul")), ("t_u", N("t_u"))],
                    inc="a_su")
                E("tensor", (lambda t: [
                    t.matmul(ps2a[:, hf, :], sgnb[:, i * P:(i + 1) * P],
                             bfa[:, i * 1024 + hf * 512:i * 1024 + hf * 512 + 512],
                             start=(i == 0), stop=(i == NDC - 1))
                    for hf in range(2) for i in range(NDC)][-1]),
                    waits=[("a_sg", N("a_sg")), ("v_tp2", XQDONE),
                           ("v_t1", N("v_t1"))], inc="t_g")
                E("tensor", (lambda t: [
                    t.matmul(ps2c[:, hf, :],
                             sgnb[:, 2048 + i * P:2048 + (i + 1) * P],
                             bfa[:, i * 1024 + hf * 512:i * 1024 + hf * 512 + 512],
                             start=(i == 0), stop=(i == NDC - 1))
                    for hf in range(2) for i in range(NDC)][-1]),
                    waits=[("a_su", N("a_su")), ("v_t3", N("v_t3")),
                           ("t_q2", N("t_q2"))], inc="t_u")
                E("vector", (lambda v: v.tensor_tensor(
                    out=a1[:, A1T1:A1T1 + 1024],
                    in0=bass.AP(ps2a, 0, [[1024, P], [1, 1024]]),
                    in1=a1[:, A1IV:A1IV + 1024], op=ALU.mult)),
                    waits=[("t_g", N("t_g")), ("v_iv2", IVDONE),
                           ("a_si", N("a_si"))], inc="v_t1")
                E("scalar", (lambda sc: sc.activation(
                    out=a1[:, A1T1:A1T1 + 1024], in_=a1[:, A1T1:A1T1 + 1024],
                    func=AFT.Silu)),
                    waits=[("v_t1", N("v_t1"))], inc="a_si")
                E("vector", (lambda v: v.tensor_tensor(
                    out=a1[:, A1T3:A1T3 + 1024], in0=a1[:, A1T1:A1T1 + 1024],
                    in1=bass.AP(ps2c, 0, [[1024, P], [1, 1024]]),
                    op=ALU.mult)),
                    waits=[("a_si", N("a_si")), ("t_u", N("t_u"))], inc="v_t3")
                po = posH * 1024
                E("vector", (lambda v, po=po: v.tensor_mul(
                    a1[:, po:po + 1024], a1[:, A1T3:A1T3 + 1024],
                    a1[:, A1IV:A1IV + 1024])),
                    waits=[("t_d", N("t_d"))], inc="v_pp")
                if posH == 3:
                    vpp = N("v_pp")
                    for j in range(NDC):
                        dbl = j % 2
                        dma("sync",
                            bass.AP(pool, PWD5 + dbl * 512,
                                    [[5632, P], [P, 4], [1, P]]),
                            bass.AP(wdT, (grpH * 512) * D + j * P,
                                    [[D, P], [P * D, 4], [1, P]]),
                            waits=[("a_sd", N("a_sd") - (1 if j >= 2 else 0)),
                                   ("v_qm", 8)], inc="wdl")
                        E("scalar", (lambda sc, dbl=dbl: sc.activation(
                            out=a1[:, A1WD + dbl * 512:A1WD + dbl * 512 + 512],
                            in_=pool[:, PWD5 + dbl * 512:PWD5 + dbl * 512 + 512],
                            func=AFT.Sign)),
                            waits=[("wdl", N("wdl")),
                                   ("t_d", max(0, N("t_d") - 1))], inc="a_sd")
                        E("tensor", (lambda t, dbl=dbl, vpp=vpp: [
                            t.matmul(
                                ps2b[:, hf, :],
                                a1[:, A1WD + dbl * 512 + c4 * P:
                                   A1WD + dbl * 512 + (c4 + 1) * P],
                                a1[:, c4 * 1024 + hf * 512:
                                   c4 * 1024 + hf * 512 + 512],
                                start=(c4 == 0), stop=(c4 == 3),
                                skip_group_check=True)
                            for hf in range(2)
                            for c4 in range(4)][-1]),
                            waits=[("a_sd", N("a_sd")), ("v_pp", vpp),
                                   ("v_dn", N("v_dn"))], inc="t_d")
                        if grpH == 0:
                            E("vector", (lambda v, j=j: v.tensor_copy(
                                out=a2[:, j * 1024:(j + 1) * 1024],
                                in_=bass.AP(ps2b, 0, [[1024, P], [1, 1024]]))),
                                waits=[("t_d", N("t_d")), ("v_n", N("v_n"))],
                                inc="v_dn")
                        else:
                            E("vector", (lambda v, j=j: v.tensor_tensor(
                                out=a2[:, j * 1024:(j + 1) * 1024],
                                in0=bass.AP(ps2b, 0, [[1024, P], [1, 1024]]),
                                in1=a2[:, j * 1024:(j + 1) * 1024],
                                op=ALU.add)),
                                waits=[("t_d", N("t_d"))], inc="v_dn")
                        if debug and j == 0:
                            dma("gpsimd",
                                bass.AP(dbgT["xqD"], grpH * 1024,
                                        [[16384, P], [1, 1024]]),
                                a2[:, 0:1024],
                                waits=[("v_dn", N("v_dn"))], inc="dbg2")
            P5D = dict(v_dn=N("v_dn"), t_d=N("t_d"))

        if debug and 5 in phases:
            dma("sync", dbgT["dnD"].ap(), a2[:],
                waits=[("v_dn", P5D["v_dn"])], inc="dbg")
            dma("sync", dbgT["bfaD"].ap(), bfa[:],
                waits=[("v_tp2", N("v_tp2"))], inc="dbg")

        if 6 in phases:
            for i in range(NDC):
                E("scalar", (lambda sc, i=i: sc.activation(
                    out=a2[:, i * 1024:(i + 1) * 1024],
                    in_=a2[:, i * 1024:(i + 1) * 1024], func=AFT.Silu)),
                    waits=[("v_dn", P5D["v_dn"])] if i == 0 else [],
                    inc="a_gf")
            AGF = N("a_gf")
            for i in range(NDC):
                dma("sync",
                    bass.AP(pool, PSTX, [[5632, P], [1, 1024]]),
                    bass.AP(stscr, i * P * 1024, [[1024, P], [1, 1024]]),
                    waits=[("v_o6", N("v_o6")), ("t_d", P5D["t_d"]),
                           ("g_s4", P4D["g_s4"])], inc="sx6")
                dma("sync",
                    bass.AP(pool, PSTX + 1024, [[5632, P], [1, 1024]]),
                    bass.AP(xTo, i * P * 1024, [[1024, P], [1, 1024]]),
                    inc="sx6")
                E("vector", (lambda v, i=i: v.tensor_tensor(
                    out=a1[:, i * 1024:(i + 1) * 1024],
                    in0=pool[:, PSTX:PSTX + 1024],
                    in1=a2[:, i * 1024:(i + 1) * 1024], op=ALU.mult)),
                    waits=[("sx6", N("sx6")), ("a_gf", AGF)] +
                          ([("t_d", P5D["t_d"])] if i == 0 else []))
                E("vector", (lambda v, i=i: v.tensor_add(
                    a1[:, i * 1024:(i + 1) * 1024],
                    a1[:, i * 1024:(i + 1) * 1024],
                    pool[:, PSTX + 1024:PSTX + 2048])), inc="v_o6")
            VO6 = N("v_o6")
            for j in range(8):
                for i in range(NDC):
                    idx = j * NDC + i
                    psu = ps1a if idx % 2 == 0 else ps1b
                    E("tensor", (lambda t, j=j, i=i, psu=psu: t.transpose(
                        psu[:, 0:P], a1[:, i * 1024 + j * P:i * 1024 + j * P + P],
                        eye[:])),
                        waits=([("v_o6", VO6)] if idx == 0 else []) +
                              [("v_t6", max(0, N("v_t6") - 1))], inc="t_t6")
                    E("vector", (lambda v, i=i, psu=psu: v.tensor_copy(
                        out=pool[:, POTL + i * P:POTL + (i + 1) * P],
                        in_=psu[:, 0:P])),
                        waits=[("t_t6", N("t_t6")),
                               ("st", N("st"))],
                        inc="v_t6")
                dma("sync",
                    bass.AP(out_tok, j * P * D, [[D, P], [1, D]]),
                    bass.AP(pool, POTL, [[5632, P], [1, D]]),
                    waits=[("v_t6", N("v_t6"))], inc="st")
            E("vector", (lambda v: v.engine_nop()),
              waits=[("st", N("st"))])

        semset = {}
        for eng in ENGS:
            for (fn, w, inc, amt) in sch.ops[eng]:
                for (sm, v) in w:
                    semset.setdefault(sm, 0)
                if inc:
                    semset.setdefault(inc, 0)
        sems = {sm: e(nc.semaphore("sem_" + sm)) for sm in semset}
        block = e(nc.Block())

        def mk(en):
            def run(ee):
                for oi, (fn, w, inc, amt) in enumerate(sch.ops[en]):
                    for (sm, v) in w:
                        ee.wait_ge(sems[sm], v)
                    try:
                        inst = fn(ee)
                    except Exception as ex:
                        raise RuntimeError(
                            f"emit failed: engine={en} op_index={oi} "
                            f"inc={inc}: {type(ex).__name__}: {ex}") from ex
                    if inc is not None:
                        assert inst is not None, (en, inc)
                        inst.then_inc(sems[inc], amt)
            return run
        block.sync(mk("sync"))
        block.gpsimd(mk("gpsimd"))
        block.tensor(mk("tensor"))
        block.vector(mk("vector"))
        block.scalar(mk("scalar"))
    return nc


_TBL = {}


def _tables():
    if _TBL:
        return _TBL
    import jax
    jax.config.update("jax_enable_x64", True)
    import jax.numpy as jnp
    with jax.default_device(jax.devices("cpu")[0]):
        inv_freq = 1.0 / (10000.0 ** (jnp.arange(0, DH, 2, dtype=jnp.float32) / DH))
        freqs = jnp.arange(S, dtype=jnp.float32)[:, None] * inv_freq[None, :]
        cos = np.asarray(jnp.repeat(jnp.cos(freqs), 2, axis=-1))
        sin = np.asarray(jnp.repeat(jnp.sin(freqs), 2, axis=-1))
    sinN = sin.copy()
    sinN[:, :64] = -sinN[:, :64]
    _TBL["cos"] = np.ascontiguousarray(cos.T)
    _TBL["sinN"] = np.ascontiguousarray(sinN.T)
    tri = (np.arange(P)[:, None] <= np.arange(P)[None, :]).astype(np.float32)
    _TBL["tri4"] = np.tile(tri, (1, 4))
    m = np.zeros((P, 1024), dtype=np.float32)
    m[:, :512] = _TBL["tri4"]
    _TBL["masks"] = m
    _TBL["eye"] = np.eye(P, dtype=np.float32)
    return _TBL


def stage_inputs(L0_attn_x, L0_attn_Wq, L0_attn_Wk, L0_attn_Wv, L0_attn_Wo,
                 L0_ffn_W_g, L0_ffn_W_u, L0_ffn_W_d):
    t = _tables()
    xT = np.ascontiguousarray(np.asarray(L0_attn_x)[0].T)
    wkvT = np.ascontiguousarray(
        np.concatenate([np.asarray(L0_attn_Wk).T, np.asarray(L0_attn_Wv).T],
                       axis=1))
    wqT = np.ascontiguousarray(np.asarray(L0_attn_Wq).T)
    woT = np.ascontiguousarray(np.asarray(L0_attn_Wo).T)
    wgT = np.ascontiguousarray(np.asarray(L0_ffn_W_g).T)
    wuT = np.ascontiguousarray(np.asarray(L0_ffn_W_u).T)
    wdT = np.ascontiguousarray(np.asarray(L0_ffn_W_d).T)
    maps = []
    for c in range(8):
        tiles = snake_tiles(c)
        cols = np.concatenate([np.arange(tt * P, tt * P + P) for tt in tiles])
        cp = np.zeros((1, 8), dtype=np.int32)
        cp[0, 0] = c
        mtab = np.zeros((72, P, 512), dtype=np.float32)
        for j in range(8):
            tt = tiles[j]
            for b in range(9):
                i = 8 * j + b
                if i <= tt:
                    mtab[j * 9 + b] = 1.0
                elif i == tt + 1 and i <= 63:
                    mtab[j * 9 + b] = t["tri4"]
        maps.append(dict(
            xT=xT, xTo=np.ascontiguousarray(xT[:, cols]),
            wkvT=wkvT, wqT=wqT, woT=woT, wgT=wgT, wuT=wuT, wdT=wdT,
            cosF=t["cos"], sinF=t["sinN"],
            cosO=np.ascontiguousarray(t["cos"][:, cols]),
            sinO=np.ascontiguousarray(t["sinN"][:, cols]),
            eyeD=t["eye"], masksD=t["masks"], cpD=cp,
            mtabD=mtab))
    return maps


def kernel(**inputs):
    maps = stage_inputs(**inputs)
    nc = build_nc()
    res = run_bass_kernel_spmd(nc, maps, core_ids=list(range(8)))
    out = np.zeros((S, D), dtype=np.float32)
    for c in range(8):
        tiles = snake_tiles(c)
        oc = res.results[c]["out_tok"]
        for j, tt in enumerate(tiles):
            out[tt * P:(tt + 1) * P, :] = oc[j * P:(j + 1) * P, :]
    return out.reshape(1, S, D)


# version 08
# speedup vs baseline: 1.2371x; 1.0031x over previous
import numpy as np
import concourse.bass as bass
import concourse.mybir as mybir
from concourse.bass_utils import run_bass_kernel_spmd
from contextlib import ExitStack

F32 = mybir.dt.float32
F32R = mybir.dt.float32r
BF16 = mybir.dt.bfloat16
I32 = mybir.dt.int32
AFT = mybir.ActivationFunctionType
ALU = mybir.AluOpType
AXX = mybir.AxisListType.X

P = 128
S = 8192
D = 2048
NDC = 16
KVH = 4
HG = 4
H = 16
DH = 128
NTG = 16
SCALE = float(DH) ** -0.5
SQ128 = float(DH) ** 0.5
SHIFT = 40.0
L_PAIR = [19, 51, 83, 115]
BLK = 4
NCH = 72
P5_TRUNC = 0
P5_NFC = 64

ENGS = ("sync", "gpsimd", "tensor", "vector", "scalar")


def snake_tiles(c):
    return [8 * j + (c if j % 2 == 0 else 7 - c) for j in range(8)]


class Sched:
    def __init__(self):
        self.ops = {e: [] for e in ENGS}
        self.cnt = {}

    def n(self, sem):
        return self.cnt.get(sem, 0)

    def emit(self, eng, fn, waits=(), inc=None, amt=1):
        w = [(s, v) for (s, v) in waits if v > 0]
        self.ops[eng].append((fn, w, inc, amt))
        if inc is not None:
            self.cnt[inc] = self.cnt.get(inc, 0) + amt


def build_nc(phases=(1, 2, 3, 4, 5, 6), debug=False):
    nc = bass.Bass("TRN2", target_bir_lowering=False, detect_race_conditions=False)
    dt_scr = "ExternalOutput" if debug else "Internal"

    xT = nc.dram_tensor("xT", [D, S], F32, kind="ExternalInput")
    xTo = nc.dram_tensor("xTo", [D, 1024], F32, kind="ExternalInput")
    wkvT = nc.dram_tensor("wkvT", [D, 1024], F32, kind="ExternalInput")
    wqT = nc.dram_tensor("wqT", [D, D], F32, kind="ExternalInput")
    woT = nc.dram_tensor("woT", [D, D], F32, kind="ExternalInput")
    wgT = nc.dram_tensor("wgT", [D, S], F32, kind="ExternalInput")
    wuT = nc.dram_tensor("wuT", [D, S], F32, kind="ExternalInput")
    wdT = nc.dram_tensor("wdT", [S, D], F32, kind="ExternalInput")
    cosF = nc.dram_tensor("cosF", [P, S], F32, kind="ExternalInput")
    sinF = nc.dram_tensor("sinF", [P, S], F32, kind="ExternalInput")
    cosO = nc.dram_tensor("cosO", [P, 1024], F32, kind="ExternalInput")
    sinO = nc.dram_tensor("sinO", [P, 1024], F32, kind="ExternalInput")
    eyeD = nc.dram_tensor("eyeD", [P, P], F32, kind="ExternalInput")
    masksD = nc.dram_tensor("masksD", [P, 1024], F32, kind="ExternalInput")
    mtabD = nc.dram_tensor("mtabD", [72, P, 512], F32, kind="ExternalInput")
    cpD = nc.dram_tensor("cpD", [1, 8], I32, kind="ExternalInput")

    out_tok = nc.dram_tensor("out_tok", [1024, D], F32, kind="ExternalOutput")
    kscr = nc.dram_tensor("kscr", [NCH, KVH, P, P], F32, kind=dt_scr)
    vscr = nc.dram_tensor("vscr", [NCH, P, 512], F32, kind=dt_scr)
    q2scr = nc.dram_tensor("q2scr", [H, 1024], F32, kind=dt_scr)
    stscr = nc.dram_tensor("stscr", [D, 1024], F32, kind=dt_scr)
    dbgT = {}
    if debug:
        for nm, shp in (("qsD", [P, 16384]), ("oTD", [P, 16384]),
                        ("cvD", [P, 4096]), ("dnD", [P, 16384]),
                        ("xqD", [P, 16384]), ("a1D", [P, 16384])):
            dbgT[nm] = nc.dram_tensor(nm, shp, F32, kind="ExternalOutput")
        dbgT["bfaD"] = nc.dram_tensor("bfaD", [P, 16384], BF16,
                                      kind="ExternalOutput")

    with ExitStack() as ctx:
        e = ctx.enter_context
        a1 = e(nc.sbuf_tensor("a1", [P, 16384], F32))
        a2 = e(nc.sbuf_tensor("a2", [P, 16384], F32))
        bfa = e(nc.sbuf_tensor("bfa", [P, 16384], BF16))
        sgnb = e(nc.sbuf_tensor("sgnb", [P, 4096], BF16))
        pool = e(nc.sbuf_tensor("pool", [P, 5632], F32))
        masks = e(nc.sbuf_tensor("masks", [P, 1024], F32))
        eye = e(nc.sbuf_tensor("eye", [P, P], F32))
        ones128 = e(nc.sbuf_tensor("ones128", [P, 1], F32))
        ones2d = e(nc.sbuf_tensor("ones2d", [P, P], F32))
        onescol = e(nc.sbuf_tensor("onescol", [1, P], F32))
        onescbf = e(nc.sbuf_tensor("onescbf", [1, P], BF16))
        k2max = e(nc.sbuf_tensor("k2max", [1, 8], F32))
        s2row = e(nc.sbuf_tensor("s2row", [1, 8], F32))
        cpS = e(nc.sbuf_tensor("cpS", [1, 8], I32))
        epsA = e(nc.sbuf_tensor("epsA", [P, 4], F32))
        zcol2 = e(nc.sbuf_tensor("zcol2", [1, P], F32))
        zrow2 = e(nc.sbuf_tensor("zrow2", [1, 512], F32))
        ctmp = e(nc.sbuf_tensor("ctmp", [1, 1024], F32))
        rowb = e(nc.sbuf_tensor("rowb", [1, 1024], F32))
        maxab = e(nc.sbuf_tensor("maxab", [P, 1], F32))
        stok = e(nc.sbuf_tensor("stok", [P, 1], F32))
        ivtok = e(nc.sbuf_tensor("ivtok", [P, 8], F32))
        ps2a = e(nc.psum_tensor("ps2a", [P, 2, 512], F32))
        ps2b = e(nc.psum_tensor("ps2b", [P, 2, 512], F32))
        ps2c = e(nc.psum_tensor("ps2c", [P, 2, 512], F32))
        ps1a = e(nc.psum_tensor("ps1a", [P, 512], F32))
        ps1b = e(nc.psum_tensor("ps1b", [P, 512], F32))

        sch = Sched()
        E = sch.emit
        N = sch.n
        REGS = {}

        PCOS, PSIN, PKRAW, PKSH, PKOUT, PSQ, PVE, PK2 = (
            0, 512, 1024, 1536, 2048, 2560, 3584, 4096)
        PW2, PCO, PSO, PQRAW, PQSH, PSQ2 = 0, 2048, 3072, 4096, 4608, 5120
        PK, PV, PEXP, PZB, PMSK = 0, 1024, 4096, 5120, 2048
        PWG, PWU, PQR, PQM, PWD5 = 0, 2048, 0, 2048, 4096
        PSTX, POTL = 0, 2048
        A1PH, A1PL, A1T1, A1T3, A1PP, A1WD, A1IV = (
            0, 4096, 8192, 9216, 10240, 11264, 13312)

        def dma(eng, out_ap, in_ap, waits=(), inc=None):
            def fn(ee, o=out_ap, i=in_ap):
                return ee.dma_start(out=o(ee) if callable(o) else o,
                                    in_=i(ee) if callable(i) else i)
            E(eng, fn, waits, inc, 16)

        dma("sync", masks[:], masksD.ap(), inc="ldc")
        dma("sync", eye[:], eyeD.ap(), inc="ldc")
        dma("sync", cpS[:], cpD.ap(), inc="ldc")
        def fmemset(v):
            v.memset(ones128[:], 1.0)
            v.memset(ones2d[:], 1.0)
            v.memset(onescol[:], 1.0)
            v.memset(onescbf[:], 1.0)
            v.memset(k2max[:], 0.0)
            v.memset(zcol2[:], 0.0)
            v.memset(zrow2[:], 0.0)
            v.memset(epsA[:, 0:1], 1e-5)
            v.memset(epsA[:, 1:2], 1e-8)
            v.memset(epsA[:, 2:3], 128.5)
            return v.memset(a2[:, 0:512], 0.0)
        E("vector", fmemset, inc="vz")
        for z in range(64, NCH):
            dma("gpsimd", bass.AP(kscr, z * 65536, [[512, P], [1, 512]]),
                a2[:, 0:512], waits=[("vz", 1)], inc="zw")
            dma("gpsimd", bass.AP(vscr, z * 65536, [[512, P], [1, 512]]),
                a2[:, 0:512], waits=[("vz", 1)], inc="zw")
        ZW = N("zw")
        dma("sync", bass.AP(a1, 0, [[16384, P], [1024, NDC], [1, 1024]]),
            bass.AP(wkvT, 0, [[1024, P], [P * 1024, NDC], [1, 1024]]), inc="ldw")

        def ln_group(xbuf, x_off, nb_xn, xn_off, ntk, eps, gate, first_waits,
                     sq0, sq1, cstride=None):
            eps_ap = epsA[0:1, 0:1] if eps == 1e-5 else epsA[0:1, 1:2]
            """LN of [D, ntk] feature-major: xbuf@x_off (chunk i at
            x_off+i*cstride) -> nb_xn@xn_off (chunk i at xn_off+i*cstride)."""
            cs = cstride if cstride is not None else ntk
            base_ts = N("t_sum")
            for i in range(NDC):
                sl = sq0 if i % 2 == 0 else sq1
                wsq = [gate] + first_waits + ([("vz", 1)] if i == 0 else [])
                wsq.append(("t_sum", base_ts if i < 2 else N("t_sum") - 1))
                E("scalar", (lambda sc, i=i, sl=sl: sc.activation(
                    out=pool[:, sl:sl + ntk],
                    in_=xbuf[:, x_off + i * cs:x_off + i * cs + ntk],
                    func=AFT.Square)), waits=wsq, inc="a_sq")
                wmm = [gate] + (first_waits + [("v_st", N("v_st")),
                                               ("vz", 1), ("t_bc", N("t_bc"))]
                                if i == 0 else [])
                E("tensor", (lambda t, i=i: t.matmul(
                    bass.AP(ps2c, 0, [[1024, 1], [1, ntk]]),
                    ones128[:], xbuf[:, x_off + i * cs:x_off + i * cs + ntk],
                    start=(i == 0), stop=(i == NDC - 1))), waits=wmm)
                E("tensor", (lambda t, i=i, sl=sl: t.matmul(
                    bass.AP(ps2c, 32 * 1024, [[1024, 1], [1, ntk]]),
                    ones128[:], pool[:, sl:sl + ntk],
                    start=(i == 0), stop=(i == NDC - 1))),
                    waits=[("a_sq", N("a_sq"))], inc="t_sum")
            ts = N("t_sum")
            m_ap = pool[0:1, sq0:sq0 + ntk]
            v_ap = pool[0:1, sq1:sq1 + ntk]
            r_ap = ctmp[0:1, 0:ntk]
            E("vector", (lambda v: v.tensor_scalar_mul(
                out=m_ap,
                in0=bass.AP(ps2c, 0, [[1024, 1], [1, ntk]]), scalar1=1.0 / D)),
                waits=[("t_sum", ts)])
            E("vector", (lambda v: v.tensor_mul(v_ap, m_ap, m_ap)))
            E("vector", (lambda v: v.scalar_tensor_tensor(
                out=v_ap,
                in0=bass.AP(ps2c, 32 * 1024, [[1024, 1], [1, ntk]]),
                scalar=1.0 / D, in1=v_ap,
                op0=ALU.mult, op1=ALU.subtract)), inc="v_st")
            E("scalar", (lambda sc: sc.activation(
                out=r_ap, in_=v_ap, func=AFT.Sqrt,
                bias=eps_ap)), waits=[("v_st", N("v_st")), ("a_cv", N("a_cv"))],
                inc="a_rs")
            E("vector", (lambda v: v.reciprocal(out=r_ap, in_=r_ap)),
                waits=[("a_rs", N("a_rs"))], inc="v_rr")
            E("tensor", (lambda t: t.matmul(
                ps1a[:, 0:ntk], onescol[:], m_ap,
                start=True, stop=True)),
                waits=[("v_rr", N("v_rr")), ("v_xn", N("v_xn"))])
            E("tensor", (lambda t: t.matmul(
                ps1b[:, 0:ntk], onescol[:], r_ap,
                start=True, stop=True)), inc="t_bc")
            tbc = N("t_bc")
            for i in range(NDC):
                E("vector", (lambda v, i=i: v.tensor_sub(
                    nb_xn[:, xn_off + i * cs:xn_off + i * cs + ntk],
                    xbuf[:, x_off + i * cs:x_off + i * cs + ntk],
                    ps1a[:, 0:ntk])), waits=[("t_bc", tbc)] if i == 0 else [])
                E("vector", (lambda v, i=i: v.tensor_mul(
                    nb_xn[:, xn_off + i * cs:xn_off + i * cs + ntk],
                    nb_xn[:, xn_off + i * cs:xn_off + i * cs + ntk],
                    ps1b[:, 0:ntk])), inc="v_xn")

        if 1 in phases:
            ldw = N("ldw")
            for tg in range(NTG):
                dma("sync",
                    bass.AP(a2, 8192, [[16384, P], [512, NDC], [1, 512]]),
                    bass.AP(xT, tg * 512, [[S, P], [P * S, NDC], [1, 512]]),
                    waits=[("v_xn", N("v_xn"))], inc="xs")
                dma("sync", pool[:, PCOS:PCOS + 512],
                    bass.AP(cosF, tg * 512, [[S, P], [1, 512]]),
                    waits=[("v_ko", N("v_ko"))], inc="cs")
                dma("sync", pool[:, PSIN:PSIN + 512],
                    bass.AP(sinF, tg * 512, [[S, P], [1, 512]]), inc="cs")
                ln_group(a2, 8192, a2, 0, 512, 1e-5, ("xs", N("xs")),
                         [("t_k", N("t_k")), ("t_v", N("t_v"))], PSQ, PSQ + 512)
                vxn = N("v_xn")
                for g in range(KVH):
                    bank = g % 2
                    E("tensor", (lambda t, g=g, bank=bank: [
                        t.matmul(ps2a[:, bank, :],
                                 a1[:, i * 1024 + g * P:i * 1024 + g * P + P],
                                 a2[:, i * 512:(i + 1) * 512],
                                 start=(i == 0), stop=(i == NDC - 1))
                        for i in range(NDC)][-1]),
                        waits=[("v_xn", vxn), ("ldw", ldw),
                               ("v_kc", max(0, N("v_kc") - 1))], inc="t_k")
                    E("vector", (lambda v, bank=bank: v.tensor_copy(
                        out=pool[:, PKRAW:PKRAW + 512], in_=ps2a[:, bank, :])),
                        waits=[("t_k", N("t_k")), ("g_sh", N("g_sh"))],
                        inc="v_kc")
                    dma("gpsimd", pool[0:64, PKSH:PKSH + 512],
                        pool[64:128, PKRAW:PKRAW + 512],
                        waits=[("v_kc", N("v_kc"))], inc="g_sh")
                    dma("gpsimd", pool[64:128, PKSH:PKSH + 512],
                        pool[0:64, PKRAW:PKRAW + 512], inc="g_sh")
                    E("vector", (lambda v: v.tensor_mul(
                        pool[:, PKRAW:PKRAW + 512], pool[:, PKRAW:PKRAW + 512],
                        pool[:, PCOS:PCOS + 512])),
                        waits=[("g_sh", N("g_sh")), ("cs", N("cs")),
                               ("kw", N("kw")), ("a_k2", N("a_k2"))])
                    E("vector", (lambda v: v.tensor_mul(
                        pool[:, PKSH:PKSH + 512], pool[:, PKSH:PKSH + 512],
                        pool[:, PSIN:PSIN + 512])))
                    E("vector", (lambda v: v.tensor_add(
                        pool[:, PKOUT:PKOUT + 512], pool[:, PKRAW:PKRAW + 512],
                        pool[:, PKSH:PKSH + 512])), inc="v_ko")
                    vko = N("v_ko")
                    E("scalar", (lambda sc: sc.activation(
                        out=pool[:, PK2:PK2 + 512],
                        in_=pool[:, PKOUT:PKOUT + 512], func=AFT.Square)),
                        waits=[("v_ko", vko), ("t_k2", N("t_k2"))], inc="a_k2")
                    E("tensor", (lambda t: t.matmul(
                        bass.AP(ps2c, 512, [[1024, 1], [1, 512]]),
                        ones128[:], pool[:, PK2:PK2 + 512],
                        start=True, stop=True)),
                        waits=[("a_k2", N("a_k2")), ("v_km", N("v_km"))],
                        inc="t_k2")
                    E("vector", (lambda v: v.reduce_max(
                        ctmp[0:1, 512:513],
                        bass.AP(ps2c, 512, [[1024, 1], [1, 512]]), axis=AXX)),
                        waits=[("t_k2", N("t_k2"))], inc="v_kr")
                    E("vector", (lambda v, g=g: v.tensor_max(
                        k2max[0:1, g:g + 1], k2max[0:1, g:g + 1],
                        ctmp[0:1, 512:513])),
                        waits=[("v_kr", N("v_kr"))], inc="v_km")
                    dma("gpsimd",
                        bass.AP(kscr, (4 * tg) * 65536 + g * 16384,
                                [[P, P], [65536, 4], [1, P]]),
                        pool[:, PKOUT:PKOUT + 512],
                        waits=[("v_ko", vko), ("zw", ZW)], inc="kw")
                for tt in range(4):
                    bank = tt % 2
                    E("tensor", (lambda t, tt=tt, bank=bank: [
                        t.matmul(ps2b[:, bank, :],
                                 a2[:, i * 512 + tt * P:i * 512 + tt * P + P],
                                 a1[:, i * 1024 + 512:(i + 1) * 1024],
                                 start=(i == 0), stop=(i == NDC - 1))
                        for i in range(NDC)][-1]),
                        waits=[("v_xn", vxn), ("ldw", ldw),
                               ("v_ve", max(0, N("v_ve") - 1))], inc="t_v")
                    E("vector", (lambda v, bank=bank: v.tensor_copy(
                        out=pool[:, PVE:PVE + 512], in_=ps2b[:, bank, :])),
                        waits=[("t_v", N("t_v")), ("vw", N("vw"))],
                        inc="v_ve")
                    dma("gpsimd",
                        bass.AP(vscr, (4 * tg + tt) * 65536, [[512, P], [1, 512]]),
                        pool[:, PVE:PVE + 512],
                        waits=[("v_ve", N("v_ve")), ("zw", ZW)], inc="vw")
            E("vector", (lambda v: v.tensor_scalar_mul(
                out=s2row[:], in0=k2max[:], scalar1=1.0 / 128.0)),
                waits=[("v_km", N("v_km"))], inc="v_s2")

        P1D = dict(kw=N("kw"), vw=N("vw"), v_s2=N("v_s2"),
                   v_ko=N("v_ko"), a_k2=N("a_k2"), cs=N("cs"),
                   g_sh=N("g_sh"), v_ve=N("v_ve"), t_k2=N("t_k2"))

        if 2 in phases:
            dma("sync", pool[:, PCO:PCO + 1024], cosO.ap(),
                waits=[("kw", P1D["kw"]), ("a_k2", P1D["a_k2"]),
                       ("v_ko", P1D["v_ko"]), ("v_ve", P1D["v_ve"]),
                       ("vw", P1D["vw"])], inc="cso")
            dma("sync", pool[:, PSO:PSO + 1024], sinO.ap(), inc="cso")
            for grp in range(2):
                dma("sync",
                    bass.AP(a2, 8192, [[16384, P], [512, NDC], [1, 512]]),
                    bass.AP(xTo, grp * 512, [[1024, P], [P * 1024, NDC], [1, 512]]),
                    waits=[("v_xn", N("v_xn"))], inc="xs")
                ln_group(a2, 8192, a2, 0, 512, 1e-5, ("xs", N("xs")),
                         [("t_q", N("t_q")), ("v_qo", N("v_qo")),
                          ("v_ko", P1D["v_ko"]), ("t_k2", P1D["t_k2"])],
                         PQRAW, PQSH)
                vxn = N("v_xn")
                for h in range(H):
                    g, hg = h // HG, h % HG
                    bank = h % 2
                    dma("sync",
                        bass.AP(pool, PW2, [[5632, P], [P, NDC], [1, P]]),
                        bass.AP(wqT, h * P, [[D, P], [P * D, NDC], [1, P]]),
                        waits=[("t_q", N("t_q")), ("v_ko", P1D["v_ko"]),
                               ("g_sh", P1D["g_sh"])], inc="wqs")
                    E("tensor", (lambda t, bank=bank: [
                        t.matmul(ps2a[:, bank, :],
                                 pool[:, PW2 + i * P:PW2 + (i + 1) * P],
                                 a2[:, i * 512:(i + 1) * 512],
                                 start=(i == 0), stop=(i == NDC - 1))
                        for i in range(NDC)][-1]),
                        waits=[("v_xn", vxn), ("wqs", N("wqs")),
                               ("v_qc", max(0, N("v_qc") - 1))], inc="t_q")
                    E("vector", (lambda v, bank=bank: v.tensor_copy(
                        out=pool[:, PQRAW:PQRAW + 512], in_=ps2a[:, bank, :])),
                        waits=[("t_q", N("t_q")), ("g_qs", N("g_qs")),
                               ("a_sq", N("a_sq"))], inc="v_qc")
                    dma("gpsimd", pool[0:64, PQSH:PQSH + 512],
                        pool[64:128, PQRAW:PQRAW + 512],
                        waits=[("v_qc", N("v_qc"))], inc="g_qs")
                    dma("gpsimd", pool[64:128, PQSH:PQSH + 512],
                        pool[0:64, PQRAW:PQRAW + 512], inc="g_qs")
                    qcol = g * 4096 + hg * 1024 + grp * 512
                    E("vector", (lambda v, grp=grp: v.tensor_mul(
                        pool[:, PQRAW:PQRAW + 512], pool[:, PQRAW:PQRAW + 512],
                        pool[:, PCO + grp * 512:PCO + grp * 512 + 512])),
                        waits=[("g_qs", N("g_qs")), ("cso", N("cso"))])
                    E("vector", (lambda v, grp=grp: v.tensor_mul(
                        pool[:, PQSH:PQSH + 512], pool[:, PQSH:PQSH + 512],
                        pool[:, PSO + grp * 512:PSO + grp * 512 + 512])))
                    E("vector", (lambda v, qcol=qcol: v.tensor_add(
                        a1[:, qcol:qcol + 512], pool[:, PQRAW:PQRAW + 512],
                        pool[:, PQSH:PQSH + 512])), inc="v_qo")
                    E("scalar", (lambda sc, qcol=qcol: sc.activation(
                        out=pool[:, PSQ2:PSQ2 + 512], in_=a1[:, qcol:qcol + 512],
                        func=AFT.Square)),
                        waits=[("v_qo", N("v_qo")), ("t_q2", N("t_q2"))],
                        inc="a_q2")
                    E("tensor", (lambda t: t.matmul(
                        bass.AP(ps2c, 512, [[1024, 1], [1, 512]]),
                        ones128[:], pool[:, PSQ2:PSQ2 + 512],
                        start=True, stop=True)),
                        waits=[("a_q2", N("a_q2")), ("v_q2", N("v_q2")),
                               ("v_km", P1D["v_s2"] and N("v_km"))], inc="t_q2")
                    E("vector", (lambda v: v.tensor_copy(
                        out=rowb[0:1, 0:512],
                        in_=bass.AP(ps2c, 512, [[1024, 1], [1, 512]]))),
                        waits=[("t_q2", N("t_q2")), ("g_q2", N("g_q2"))],
                        inc="v_q2")
                    dma("gpsimd",
                        bass.AP(q2scr, h * 1024 + grp * 512, [[512, 1], [1, 512]]),
                        rowb[0:1, 0:512],
                        waits=[("v_q2", N("v_q2"))], inc="g_q2")

        P2D = dict(v_qo=N("v_qo"), g_q2=N("g_q2"), t_q=N("t_q"),
                   v_qc=N("v_qc"))

        NIT = 0
        if 3 in phases:
            NJ = [8 * j + 9 for j in range(8)]
            sections = []
            iters = []
            it0 = 0
            for g in range(KVH):
                for j in range(8):
                    for i in range(NJ[j]):
                        iters.append((g, j, i))
                    sections.append((g, j, it0, it0 + NJ[j] - 1))
                    it0 += NJ[j]
            NIT = len(iters)
            blocks = []
            bidx = 0
            it0 = 0
            for g in range(KVH):
                for j in range(8):
                    nblk = (NJ[j] + BLK - 1) // BLK
                    for n in range(nblk):
                        f = it0 + n * BLK
                        l = it0 + min((n + 1) * BLK, NJ[j]) - 1
                        blocks.append((g, j, n, bidx, f, l))
                        bidx += 1
                    it0 += NJ[j]
            it2blk = {}
            for b in blocks:
                for it in range(b[4], b[5] + 1):
                    it2blk[it] = b
            masked = {}
            for it, (g, j, i) in enumerate(iters):
                if i >= 8 * j:
                    masked[it] = (j, i - 8 * j)
            masked_list = sorted(masked)
            mask_pos = {it: k + 1 for k, it in enumerate(masked_list)}
            mask_idx = {it: k for k, it in enumerate(masked_list)}

            ldc = N("ldc")
            BAR3 = [("kw", P1D["kw"]), ("vw", P1D["vw"]), ("ldc", ldc),
                    ("v_qo", P2D["v_qo"]), ("g_q2", P2D["g_q2"]),
                    ("zw", ZW), ("v_s2", P1D["v_s2"])]
            TSC0 = N("t_sc")
            TZA0 = N("t_za")
            AE0 = N("a_e")

            def sc_cnt(it):
                return TSC0 + it + 1

            def za_cnt(it):
                return TZA0 + it + 1

            def e_cnt(it):
                return AE0 + it + 1

            def emit_cvec(g, tsc_prev):
                for hg in range(HG):
                    h = 4 * g + hg
                    dma("sync",
                        ctmp[0:1, :],
                        bass.AP(q2scr, h * 1024, [[1024, 1], [1, 1024]]),
                        waits=BAR3 + [("a_cv", N("a_cv")), ("v_cv", N("v_cv"))],
                        inc="cq")
                    E("scalar", (lambda sc, g=g: sc.activation(
                        out=ctmp[0:1, :], in_=ctmp[0:1, :], func=AFT.Sqrt,
                        scale=s2row[0:1, g:g + 1])),
                        waits=[("cq", N("cq")), ("v_cv", N("v_cv"))],
                        inc="a_cv")
                    E("vector", (lambda v, hg=hg: v.tensor_scalar(
                        out=bfa[0:1, hg * 1024:hg * 1024 + 1024],
                        in0=ctmp[0:1, :],
                        scalar1=-SQ128, scalar2=SHIFT * SQ128,
                        op0=ALU.mult, op1=ALU.add)),
                        waits=[("a_cv", N("a_cv")), ("t_sc", tsc_prev)],
                        inc="v_cv")

            def emit_block_loads(b):
                g, j, n, bi, f, l = b
                dbl = bi % 2
                kwv = ("t_sc", sc_cnt(blocks[bi - 2][5])) if bi >= 2                     else ("t_sc", 0)
                vwv = ("t_za", za_cnt(blocks[bi - 2][5])) if bi >= 2                     else ("t_za", 0)
                dma("sync",
                    bass.AP(pool, PK + dbl * 512, [[5632, P], [P, 4], [1, P]]),
                    bass.AP(kscr, 4 * n * 65536 + g * 16384,
                            [[P, P], [65536, 4], [1, P]]),
                    waits=BAR3 + [kwv], inc="kl")
                dma("gpsimd",
                    bass.AP(pool, PV + dbl * 512, [[5632, P], [P, 4], [1, P]]),
                    bass.AP(vscr, 4 * n * 65536 + g * P,
                            [[512, P], [65536, 4], [1, P]]),
                    waits=BAR3 + [vwv], inc="vl")

            mask_loaded = [0]

            def emit_mask_load_mi(mi):
                it = masked_list[mi]
                jj, b = masked[it]
                wv = ("v_m", mi - 3) if mi >= 4 else ("v_m", 0)
                dma("gpsimd",
                    pool[:, PMSK + (mi % 4) * 512:PMSK + (mi % 4) * 512 + 512],
                    bass.AP(mtabD, (jj * 9 + b) * 65536, [[512, P], [1, 512]]),
                    waits=BAR3 + [wv], inc="ml")
                mask_loaded[0] = mi + 1

            def emit_iter(it, si, s_f):
                g, j, i = iters[it]
                b = it2blk[it]
                bi, n = b[3], b[2]
                dbl = bi % 2
                sbank = it % 2
                sec = g * 8 + j
                kslot = PK + dbl * 512 + (i - 4 * n) * P
                vslot = PV + dbl * 512 + (i - 4 * n) * P
                qcol = g * 4096 + j * P

                ccol = j * P

                def fsc(t, kslot=kslot, qcol=qcol, ccol=ccol, sbank=sbank):
                    t.matmul(ps2a[:, sbank, :],
                             pool[:, kslot:kslot + P],
                             bass.AP(a1, qcol, [[16384, P], [1024, 4], [1, P]]),
                             start=True, stop=False, skip_group_check=True)
                    return t.matmul(
                        ps2a[:, sbank, :], onescbf[:],
                        bass.AP(bfa, ccol, [[16384, 1], [1024, 4], [1, P]]),
                        start=False, stop=True, skip_group_check=True)
                w = [("kl", kdma_after[bi] * 16), ("v_cv", N("v_cv"))]
                if it >= 2:
                    w.append(("a_e", e_cnt(it - 2)))
                E("tensor", fsc, waits=w, inc="t_sc")

                we = [("t_sc", sc_cnt(it))]
                if it >= 2:
                    we.append(("t_za", za_cnt(it - 2)))
                E("scalar", (lambda sc, sbank=sbank: sc.activation(
                    out=pool[:, PEXP + sbank * 512:PEXP + sbank * 512 + 512],
                    in_=ps2a[:, sbank, :], func=AFT.Exp, scale=SCALE)),
                    waits=we, inc="a_e")

                if it in masked:
                    mi = mask_idx[it]
                    E("vector", (lambda v, sbank=sbank, mi=mi: v.tensor_tensor(
                        out=pool[:, PEXP + sbank * 512:PEXP + sbank * 512 + 512],
                        in0=pool[:, PEXP + sbank * 512:PEXP + sbank * 512 + 512],
                        in1=pool[:, PMSK + (mi % 4) * 512:
                                 PMSK + (mi % 4) * 512 + 512],
                        op=ALU.mult)),
                        waits=[("a_e", e_cnt(it)), ("ml", mdma_after[it] * 16)],
                        inc="v_m")

                zps = ps1a if sec % 2 == 0 else ps1b
                obank = sec % 2
                first = (it == s_f)

                def fza(t, vslot=vslot, sbank=sbank, zps=zps, obank=obank,
                        first=first):
                    pexp_ap = pool[:, PEXP + sbank * 512:PEXP + sbank * 512 + 512]
                    t.matmul(zps[:, 0:512],
                             ones2d[:], pexp_ap, start=first, stop=False,
                             skip_group_check=True)
                    return t.matmul(ps2b[:, obank, :],
                                    pool[:, vslot:vslot + P], pexp_ap,
                                    start=first, stop=False,
                                    skip_group_check=True)
                wz = [("vl", vdma_after[bi] * 16)]
                if first:
                    wz += [("a_rz", max(0, si - 1)),
                           ("v_e3", max(0, si - 1))]
                if it in masked:
                    wz.append(("v_m", mask_pos[it]))
                else:
                    wz.append(("a_e", e_cnt(it)))
                E("tensor", fza, waits=wz, inc="t_za")

            kdma_after = {}
            vdma_after = {}
            kc = 0
            for b in blocks:
                kc += 1
                kdma_after[b[3]] = kc
                vdma_after[b[3]] = kc
            mdma_after = {it: mask_idx[it] + 1 for it in masked}

            for si, (g, j, s_f, s_l) in enumerate(sections):
                if j == 0:
                    emit_cvec(g, TSC0 + s_f)
                obank = si % 2
                zps = ps1a if si % 2 == 0 else ps1b
                sec_blocks = [b for b in blocks if b[0] == g and b[1] == j]
                for k2 in range(min(2, len(sec_blocks))):
                    emit_block_loads(sec_blocks[k2])
                if si == 0:
                    for _mi in range(4):
                        emit_mask_load_mi(_mi)
                for it in range(s_f, s_l + 1):
                    i = it - s_f
                    if it in masked:
                        mi = mask_idx[it]
                        while mask_loaded[0] <= mi + 4 and \
                                mask_loaded[0] < len(masked_list):
                            emit_mask_load_mi(mask_loaded[0])
                    if i % BLK == 0:
                        nb = i // BLK + 2
                        if nb < len(sec_blocks):
                            emit_block_loads(sec_blocks[nb])
                    emit_iter(it, si, s_f)
                za_end = za_cnt(s_l)
                E("vector", (lambda v, zps=zps: v.reciprocal(
                    out=pool[:, PZB:PZB + 512], in_=zps[:, 0:512])),
                    waits=[("t_za", za_end), ("v_e3", si)], inc="a_rz")
                ocol = 4 * g * 1024 + j * P
                E("vector", (lambda v, obank=obank, ocol=ocol: v.tensor_tensor(
                    out=bass.AP(a2, ocol, [[16384, P], [1024, 4], [1, P]]),
                    in0=bass.AP(ps2b, obank * 512, [[1024, P], [P, 4], [1, P]]),
                    in1=bass.AP(pool, PZB, [[5632, P], [P, 4], [1, P]]),
                    op=ALU.mult)),
                    waits=[("a_rz", N("a_rz"))], inc="v_e3")

        P3D = dict(v_e3=N("v_e3"), t_za=N("t_za"), t_sc=N("t_sc"),
                   a_e=N("a_e"), kl=N("kl"), vl=N("vl"), v_se=N("v_se"))

        if debug and 2 in phases:
            dma("sync", dbgT["qsD"].ap(), a1[:],
                waits=[("v_qo", P2D["v_qo"])] +
                      ([("t_sc", P3D["t_sc"])] if 3 in phases else []),
                inc="dbg")
        if debug and 3 in phases:
            dma("sync", dbgT["oTD"].ap(), a2[:],
                waits=[("v_e3", P3D["v_e3"])], inc="dbg")
            dma("gpsimd", bass.AP(dbgT["cvD"], 0, [[4096, 1], [1, 4096]]),
                bfa[0:1, 0:4096], waits=[("v_cv", N("v_cv"))], inc="dbg2")

        if 4 in phases:
            for grp in range(2):
                for j in range(NDC):
                    bank = (grp * NDC + j) % 2
                    dma("sync",
                        bass.AP(pool, PW2, [[5632, P], [P, NDC], [1, P]]),
                        bass.AP(woT, j * P, [[D, P], [P * D, NDC], [1, P]]),
                        waits=[("t_s4", N("t_s4")), ("v_e3", P3D["v_e3"]),
                               ("kl", P3D["kl"])], inc="wos")
                    E("tensor", (lambda t, bank=bank, grp=grp: [
                        t.matmul(ps2a[:, bank, :],
                                 pool[:, PW2 + i * P:PW2 + (i + 1) * P],
                                 a2[:, i * 1024 + grp * 512:
                                    i * 1024 + grp * 512 + 512],
                                 start=(i == 0), stop=(i == NDC - 1))
                        for i in range(NDC)][-1]),
                        waits=[("wos", N("wos")), ("v_e3", P3D["v_e3"]),
                               ("a_e", P3D["a_e"]),
                               ("v_s4", max(0, N("v_s4") - 1))], inc="t_s4")
                    E("vector", (lambda v, bank=bank: v.tensor_copy(
                        out=pool[:, PVE:PVE + 512], in_=ps2a[:, bank, :])),
                        waits=[("t_s4", N("t_s4")), ("g_s4", N("g_s4")),
                               ("vl", P3D["vl"])], inc="v_s4")
                    dma("gpsimd",
                        bass.AP(stscr, j * P * 1024 + grp * 512,
                                [[1024, P], [1, 512]]),
                        pool[:, PVE:PVE + 512],
                        waits=[("v_s4", N("v_s4"))], inc="g_s4")

        P4D = dict(g_s4=N("g_s4"), t_s4=N("t_s4"), v_s4=N("v_s4"))

        P5D = {}
        if 5 in phases:
            B5 = [("g_s4", P4D["g_s4"]), ("t_s4", P4D["t_s4"]),
                  ("v_s4", P4D["v_s4"])]
            dma("sync",
                bass.AP(a1, 0, [[16384, P], [1024, NDC], [1, 1024]]),
                bass.AP(xTo, 0, [[1024, P], [P * 1024, NDC], [1, 1024]]),
                waits=B5 + [("t_sc", P3D["t_sc"]), ("v_xn", N("v_xn"))],
                inc="x5")
            for rnd, eps in ((0, 1e-5), (1, 1e-8)):
                for cg in range(2):
                    ln_group(a1, cg * 512, a1, cg * 512, 512, eps,
                             ("x5", N("x5")),
                             [("v_s4", P4D["v_s4"]), ("t_q2", N("t_q2")),
                              ("v_xn", N("v_xn")), ("v_se", P3D["v_se"])],
                             PQR, PQM, cstride=1024)
            VXN5 = N("v_xn")
            for j in range(8):
                for i in range(NDC):
                    idx = j * NDC + i
                    psu = ps1a if idx % 2 == 0 else ps1b
                    E("tensor", (lambda t, j=j, i=i, psu=psu: t.transpose(
                        psu[:, 0:P], a1[:, i * 1024 + j * P:i * 1024 + j * P + P],
                        eye[:])),
                        waits=([("v_xn", VXN5), ("v_e3", P3D["v_e3"])]
                               if idx == 0 else []) +
                              [("v_tp", max(0, N("v_tp") - 1))], inc="t_tp")
                    E("vector", (lambda v, j=j, i=i, psu=psu: v.tensor_copy(
                        out=a2[:, j * D + i * P:j * D + (i + 1) * P],
                        in_=psu[:, 0:P])),
                        waits=[("t_tp", N("t_tp"))], inc="v_tp")
            for j in range(8):
                E("vector", (lambda v, j=j: v.reduce_max(
                    maxab[:], a2[:, j * D:(j + 1) * D], axis=AXX,
                    apply_absolute_value=True)),
                    waits=[("v_tp", 16 * (j + 1)), ("a_u", N("a_u"))],
                    inc="v_qm")
                E("vector", (lambda v, j=j: v.tensor_scalar_mul(
                    out=ivtok[:, j:j + 1], in0=maxab[:], scalar1=1.0 / 127.0)),
                    waits=[("v_qm", N("v_qm"))], inc="v_iv0")
                E("vector", (lambda v: v.reciprocal(
                    out=stok[:], in_=maxab[:])),
                    waits=[("v_qm", N("v_qm")), ("a_u", N("a_u"))], inc="v_rc")
                E("vector", (lambda v: v.tensor_scalar_mul(
                    out=stok[:], in0=stok[:], scalar1=127.0)),
                    waits=[("v_rc", N("v_rc"))], inc="a_st")
                E("vector", (lambda v, j=j: v.tensor_scalar(
                    out=a2[:, j * D:(j + 1) * D], in0=a2[:, j * D:(j + 1) * D],
                    scalar1=stok[:], scalar2=128.5,
                    op0=ALU.mult, op1=ALU.add)),
                    waits=[("a_st", N("a_st"))], inc="a_u")
                E("vector", (lambda v, j=j: v.tensor_copy(
                    out=pool[:, PQR:PQR + D], in_=a2[:, j * D:(j + 1) * D])),
                    waits=[("a_u", N("a_u"))])
                for k in range(7, -1, -1):
                    cst = float(1 << k)
                    E("vector", (lambda v, cst=cst: v.tensor_single_scalar(
                        out=pool[:, PQM:PQM + D], in_=pool[:, PQR:PQR + D],
                        scalar=cst, op=ALU.is_ge)))
                    E("vector", (lambda v, cst=cst: v.tensor_scalar_mul(
                        out=pool[:, PQM:PQM + D], in0=pool[:, PQM:PQM + D],
                        scalar1=cst)))
                    E("vector", (lambda v: v.tensor_sub(
                        pool[:, PQR:PQR + D], pool[:, PQR:PQR + D],
                        pool[:, PQM:PQM + D])))
                E("vector", (lambda v, j=j: v.scalar_tensor_tensor(
                    out=a2[:, j * D:(j + 1) * D], in0=a2[:, j * D:(j + 1) * D],
                    scalar=-128.0, in1=pool[:, PQR:PQR + D],
                    op0=ALU.add, op1=ALU.subtract)), inc="v_n")
            for j in range(8):
                for i in range(NDC):
                    idx = j * NDC + i
                    psu = ps1a if idx % 2 == 0 else ps1b
                    E("tensor", (lambda t, j=j, i=i, psu=psu: t.transpose(
                        psu[:, 0:P], a2[:, j * D + i * P:j * D + (i + 1) * P],
                        eye[:])),
                        waits=[("v_n", N("v_n"))] +
                              [("v_tp2", max(0, N("v_tp2") - 1))], inc="t_tp2")
                    E("vector", (lambda v, j=j, i=i, psu=psu: v.tensor_copy(
                        out=bfa[:, i * 1024 + j * P:i * 1024 + j * P + P],
                        in_=psu[:, 0:P])),
                        waits=[("t_tp2", N("t_tp2"))], inc="v_tp2")
            XQDONE = N("v_tp2")
            if P5_TRUNC:
                sch.ops
            for j in range(8):
                psu = ps1a if j % 2 == 0 else ps1b
                E("tensor", (lambda t, psu=psu, j=j: t.transpose(
                    psu[0:1, 0:P], ivtok[:, j:j + 1], eye[:])),
                    waits=[("v_iv0", 8), ("v_iv1", max(0, N("v_iv1") - 1)),
                           ("v_tp2", XQDONE)], inc="t_iv")
                E("vector", (lambda v, psu=psu, j=j: v.tensor_copy(
                    out=rowb[0:1, j * P:(j + 1) * P], in_=psu[0:1, 0:P])),
                    waits=[("t_iv", N("t_iv"))], inc="v_iv1")
            for hf in range(2):
                E("tensor", (lambda t, hf=hf: t.matmul(
                    ps1a[:, 0:512], onescol[:],
                    rowb[0:1, hf * 512:hf * 512 + 512],
                    start=True, stop=True, skip_group_check=True)),
                    waits=[("v_iv1", 8), ("v_iv2", N("v_iv2"))], inc="t_iv2")
                E("vector", (lambda v, hf=hf: v.tensor_copy(
                    out=a1[:, A1IV + hf * 512:A1IV + hf * 512 + 512],
                    in_=ps1a[:, 0:512])),
                    waits=[("t_iv2", N("t_iv2"))], inc="v_iv2")
            IVDONE = N("v_iv2")
            for fc in range(P5_NFC):
                grpH, posH = fc // 4, fc % 4
                dma("sync",
                    bass.AP(pool, PWG, [[5632, P], [P, NDC], [1, P]]),
                    bass.AP(wgT, fc * P, [[S, P], [P * S, NDC], [1, P]]),
                    waits=[("a_sg", N("a_sg")), ("x5", N("x5")),
                           ("v_n", N("v_n")), ("a_u", N("a_u"))],
                    inc="wgl")
                dma("sync",
                    bass.AP(pool, PWU, [[5632, P], [P, NDC], [1, P]]),
                    bass.AP(wuT, fc * P, [[S, P], [P * S, NDC], [1, P]]),
                    waits=[("a_su", N("a_su")), ("v_n", N("v_n"))], inc="wul")
                E("scalar", (lambda sc: sc.activation(
                    out=sgnb[:, 0:2048], in_=pool[:, PWG:PWG + 2048],
                    func=AFT.Sign)),
                    waits=[("wgl", N("wgl")), ("t_g", N("t_g"))],
                    inc="a_sg")
                E("scalar", (lambda sc: sc.activation(
                    out=sgnb[:, 2048:4096], in_=pool[:, PWU:PWU + 2048],
                    func=AFT.Sign)),
                    waits=[("wul", N("wul")), ("t_u", N("t_u"))],
                    inc="a_su")
                E("tensor", (lambda t: [
                    t.matmul(ps2a[:, hf, :], sgnb[:, i * P:(i + 1) * P],
                             bfa[:, i * 1024 + hf * 512:i * 1024 + hf * 512 + 512],
                             start=(i == 0), stop=(i == NDC - 1))
                    for hf in range(2) for i in range(NDC)][-1]),
                    waits=[("a_sg", N("a_sg")), ("v_tp2", XQDONE),
                           ("v_t1", N("v_t1"))], inc="t_g")
                E("tensor", (lambda t: [
                    t.matmul(ps2c[:, hf, :],
                             sgnb[:, 2048 + i * P:2048 + (i + 1) * P],
                             bfa[:, i * 1024 + hf * 512:i * 1024 + hf * 512 + 512],
                             start=(i == 0), stop=(i == NDC - 1))
                    for hf in range(2) for i in range(NDC)][-1]),
                    waits=[("a_su", N("a_su")), ("v_t3", N("v_t3")),
                           ("t_q2", N("t_q2")), ("v_dn", N("v_dn"))],
                    inc="t_u")
                E("vector", (lambda v: v.tensor_tensor(
                    out=a1[:, A1T1:A1T1 + 1024],
                    in0=bass.AP(ps2a, 0, [[1024, P], [1, 1024]]),
                    in1=a1[:, A1IV:A1IV + 1024], op=ALU.mult)),
                    waits=[("t_g", N("t_g")), ("v_iv2", IVDONE),
                           ("a_si", N("a_si"))], inc="v_t1")
                E("scalar", (lambda sc: sc.activation(
                    out=a1[:, A1T1:A1T1 + 1024], in_=a1[:, A1T1:A1T1 + 1024],
                    func=AFT.Silu)),
                    waits=[("v_t1", N("v_t1"))], inc="a_si")
                E("vector", (lambda v: v.tensor_tensor(
                    out=a1[:, A1T3:A1T3 + 1024], in0=a1[:, A1T1:A1T1 + 1024],
                    in1=bass.AP(ps2c, 0, [[1024, P], [1, 1024]]),
                    op=ALU.mult)),
                    waits=[("a_si", N("a_si")), ("t_u", N("t_u"))], inc="v_t3")
                po = posH * 1024
                E("vector", (lambda v, po=po: v.tensor_mul(
                    a1[:, po:po + 1024], a1[:, A1T3:A1T3 + 1024],
                    a1[:, A1IV:A1IV + 1024])),
                    waits=[("t_d", N("t_d"))], inc="v_pp")
                if posH == 3:
                    vpp = N("v_pp")
                    for j in range(NDC):
                        dbl = j % 2
                        psd = ps2b if j % 2 == 0 else ps2c
                        dma("sync",
                            bass.AP(pool, PWD5 + dbl * 512,
                                    [[5632, P], [P, 4], [1, P]]),
                            bass.AP(wdT, (grpH * 512) * D + j * P,
                                    [[D, P], [P * D, 4], [1, P]]),
                            waits=[("a_sd", N("a_sd") - (1 if j >= 2 else 0)),
                                   ("v_qm", 8)], inc="wdl")
                        E("scalar", (lambda sc, dbl=dbl: sc.activation(
                            out=a1[:, A1WD + dbl * 512:A1WD + dbl * 512 + 512],
                            in_=pool[:, PWD5 + dbl * 512:PWD5 + dbl * 512 + 512],
                            func=AFT.Sign)),
                            waits=[("wdl", N("wdl")),
                                   ("t_d", max(0, N("t_d") - 1))], inc="a_sd")
                        E("tensor", (lambda t, dbl=dbl, vpp=vpp, psd=psd: [
                            t.matmul(
                                psd[:, hf, :],
                                a1[:, A1WD + dbl * 512 + c4 * P:
                                   A1WD + dbl * 512 + (c4 + 1) * P],
                                a1[:, c4 * 1024 + hf * 512:
                                   c4 * 1024 + hf * 512 + 512],
                                start=(c4 == 0), stop=(c4 == 3),
                                skip_group_check=True)
                            for hf in range(2)
                            for c4 in range(4)][-1]),
                            waits=[("a_sd", N("a_sd")), ("v_pp", vpp),
                                   ("v_dn", max(0, N("v_dn") - 1))], inc="t_d")
                        if grpH == 0:
                            E("vector", (lambda v, j=j, psd=psd: v.tensor_copy(
                                out=a2[:, j * 1024:(j + 1) * 1024],
                                in_=bass.AP(psd, 0, [[1024, P], [1, 1024]]))),
                                waits=[("t_d", N("t_d")), ("v_n", N("v_n"))],
                                inc="v_dn")
                        else:
                            E("vector", (lambda v, j=j, psd=psd: v.tensor_tensor(
                                out=a2[:, j * 1024:(j + 1) * 1024],
                                in0=bass.AP(psd, 0, [[1024, P], [1, 1024]]),
                                in1=a2[:, j * 1024:(j + 1) * 1024],
                                op=ALU.add)),
                                waits=[("t_d", N("t_d"))], inc="v_dn")
                        if debug and j == 0:
                            dma("gpsimd",
                                bass.AP(dbgT["xqD"], grpH * 1024,
                                        [[16384, P], [1, 1024]]),
                                a2[:, 0:1024],
                                waits=[("v_dn", N("v_dn"))], inc="dbg2")
            P5D = dict(v_dn=N("v_dn"), t_d=N("t_d"))

        if debug and 5 in phases:
            dma("sync", dbgT["dnD"].ap(), a2[:],
                waits=[("v_dn", P5D["v_dn"])], inc="dbg")
            dma("sync", dbgT["bfaD"].ap(), bfa[:],
                waits=[("v_tp2", N("v_tp2"))], inc="dbg")

        if 6 in phases:
            for i in range(NDC):
                E("scalar", (lambda sc, i=i: sc.activation(
                    out=a2[:, i * 1024:(i + 1) * 1024],
                    in_=a2[:, i * 1024:(i + 1) * 1024], func=AFT.Silu)),
                    waits=[("v_dn", P5D["v_dn"])] if i == 0 else [],
                    inc="a_gf")
            AGF = N("a_gf")
            for i in range(NDC):
                dma("sync",
                    bass.AP(pool, PSTX, [[5632, P], [1, 1024]]),
                    bass.AP(stscr, i * P * 1024, [[1024, P], [1, 1024]]),
                    waits=[("v_o6", N("v_o6")), ("t_d", P5D["t_d"]),
                           ("g_s4", P4D["g_s4"])], inc="sx6")
                dma("sync",
                    bass.AP(pool, PSTX + 1024, [[5632, P], [1, 1024]]),
                    bass.AP(xTo, i * P * 1024, [[1024, P], [1, 1024]]),
                    inc="sx6")
                E("vector", (lambda v, i=i: v.tensor_tensor(
                    out=a1[:, i * 1024:(i + 1) * 1024],
                    in0=pool[:, PSTX:PSTX + 1024],
                    in1=a2[:, i * 1024:(i + 1) * 1024], op=ALU.mult)),
                    waits=[("sx6", N("sx6")), ("a_gf", AGF)] +
                          ([("t_d", P5D["t_d"])] if i == 0 else []))
                E("vector", (lambda v, i=i: v.tensor_add(
                    a1[:, i * 1024:(i + 1) * 1024],
                    a1[:, i * 1024:(i + 1) * 1024],
                    pool[:, PSTX + 1024:PSTX + 2048])), inc="v_o6")
            VO6 = N("v_o6")
            for j in range(8):
                for i in range(NDC):
                    idx = j * NDC + i
                    psu = ps1a if idx % 2 == 0 else ps1b
                    E("tensor", (lambda t, j=j, i=i, psu=psu: t.transpose(
                        psu[:, 0:P], a1[:, i * 1024 + j * P:i * 1024 + j * P + P],
                        eye[:])),
                        waits=([("v_o6", VO6)] if idx == 0 else []) +
                              [("v_t6", max(0, N("v_t6") - 1))], inc="t_t6")
                    E("vector", (lambda v, i=i, psu=psu: v.tensor_copy(
                        out=pool[:, POTL + i * P:POTL + (i + 1) * P],
                        in_=psu[:, 0:P])),
                        waits=[("t_t6", N("t_t6")),
                               ("st", N("st"))],
                        inc="v_t6")
                dma("sync",
                    bass.AP(out_tok, j * P * D, [[D, P], [1, D]]),
                    bass.AP(pool, POTL, [[5632, P], [1, D]]),
                    waits=[("v_t6", N("v_t6"))], inc="st")
            E("vector", (lambda v: v.engine_nop()),
              waits=[("st", N("st"))])

        semset = {}
        for eng in ENGS:
            for (fn, w, inc, amt) in sch.ops[eng]:
                for (sm, v) in w:
                    semset.setdefault(sm, 0)
                if inc:
                    semset.setdefault(inc, 0)
        sems = {sm: e(nc.semaphore("sem_" + sm)) for sm in semset}
        block = e(nc.Block())

        def mk(en):
            def run(ee):
                for oi, (fn, w, inc, amt) in enumerate(sch.ops[en]):
                    for (sm, v) in w:
                        ee.wait_ge(sems[sm], v)
                    try:
                        inst = fn(ee)
                    except Exception as ex:
                        raise RuntimeError(
                            f"emit failed: engine={en} op_index={oi} "
                            f"inc={inc}: {type(ex).__name__}: {ex}") from ex
                    if inc is not None:
                        assert inst is not None, (en, inc)
                        inst.then_inc(sems[inc], amt)
            return run
        block.sync(mk("sync"))
        block.gpsimd(mk("gpsimd"))
        block.tensor(mk("tensor"))
        block.vector(mk("vector"))
        block.scalar(mk("scalar"))
    return nc


_TBL = {}


def _tables():
    if _TBL:
        return _TBL
    import jax
    jax.config.update("jax_enable_x64", True)
    import jax.numpy as jnp
    with jax.default_device(jax.devices("cpu")[0]):
        inv_freq = 1.0 / (10000.0 ** (jnp.arange(0, DH, 2, dtype=jnp.float32) / DH))
        freqs = jnp.arange(S, dtype=jnp.float32)[:, None] * inv_freq[None, :]
        cos = np.asarray(jnp.repeat(jnp.cos(freqs), 2, axis=-1))
        sin = np.asarray(jnp.repeat(jnp.sin(freqs), 2, axis=-1))
    sinN = sin.copy()
    sinN[:, :64] = -sinN[:, :64]
    _TBL["cos"] = np.ascontiguousarray(cos.T)
    _TBL["sinN"] = np.ascontiguousarray(sinN.T)
    tri = (np.arange(P)[:, None] <= np.arange(P)[None, :]).astype(np.float32)
    _TBL["tri4"] = np.tile(tri, (1, 4))
    m = np.zeros((P, 1024), dtype=np.float32)
    m[:, :512] = _TBL["tri4"]
    _TBL["masks"] = m
    _TBL["eye"] = np.eye(P, dtype=np.float32)
    return _TBL


def stage_inputs(L0_attn_x, L0_attn_Wq, L0_attn_Wk, L0_attn_Wv, L0_attn_Wo,
                 L0_ffn_W_g, L0_ffn_W_u, L0_ffn_W_d):
    t = _tables()
    xT = np.ascontiguousarray(np.asarray(L0_attn_x)[0].T)
    wkvT = np.ascontiguousarray(
        np.concatenate([np.asarray(L0_attn_Wk).T, np.asarray(L0_attn_Wv).T],
                       axis=1))
    wqT = np.ascontiguousarray(np.asarray(L0_attn_Wq).T)
    woT = np.ascontiguousarray(np.asarray(L0_attn_Wo).T)
    wgT = np.ascontiguousarray(np.asarray(L0_ffn_W_g).T)
    wuT = np.ascontiguousarray(np.asarray(L0_ffn_W_u).T)
    wdT = np.ascontiguousarray(np.asarray(L0_ffn_W_d).T)
    maps = []
    for c in range(8):
        tiles = snake_tiles(c)
        cols = np.concatenate([np.arange(tt * P, tt * P + P) for tt in tiles])
        cp = np.zeros((1, 8), dtype=np.int32)
        cp[0, 0] = c
        mtab = np.zeros((72, P, 512), dtype=np.float32)
        for j in range(8):
            tt = tiles[j]
            for b in range(9):
                i = 8 * j + b
                if i <= tt:
                    mtab[j * 9 + b] = 1.0
                elif i == tt + 1 and i <= 63:
                    mtab[j * 9 + b] = t["tri4"]
        maps.append(dict(
            xT=xT, xTo=np.ascontiguousarray(xT[:, cols]),
            wkvT=wkvT, wqT=wqT, woT=woT, wgT=wgT, wuT=wuT, wdT=wdT,
            cosF=t["cos"], sinF=t["sinN"],
            cosO=np.ascontiguousarray(t["cos"][:, cols]),
            sinO=np.ascontiguousarray(t["sinN"][:, cols]),
            eyeD=t["eye"], masksD=t["masks"], cpD=cp,
            mtabD=mtab))
    return maps


def kernel(**inputs):
    maps = stage_inputs(**inputs)
    nc = build_nc()
    res = run_bass_kernel_spmd(nc, maps, core_ids=list(range(8)))
    out = np.zeros((S, D), dtype=np.float32)
    for c in range(8):
        tiles = snake_tiles(c)
        oc = res.results[c]["out_tok"]
        for j, tt in enumerate(tiles):
            out[tt * P:(tt + 1) * P, :] = oc[j * P:(j + 1) * P, :]
    return out.reshape(1, S, D)


# version 09
# speedup vs baseline: 1.3573x; 1.0971x over previous
import numpy as np
import concourse.bass as bass
import concourse.mybir as mybir
from concourse.bass_utils import run_bass_kernel_spmd
from contextlib import ExitStack

F32 = mybir.dt.float32
F32R = mybir.dt.float32r
BF16 = mybir.dt.bfloat16
I32 = mybir.dt.int32
AFT = mybir.ActivationFunctionType
ALU = mybir.AluOpType
AXX = mybir.AxisListType.X

P = 128
S = 8192
D = 2048
NDC = 16
KVH = 4
HG = 4
H = 16
DH = 128
NTG = 16
SCALE = float(DH) ** -0.5
SQ128 = float(DH) ** 0.5
SHIFT = 40.0
L_PAIR = [19, 51, 83, 115]
BLK = 4
NCH = 72
P5_TRUNC = 0
P5_NFC = 64

ENGS = ("sync", "gpsimd", "tensor", "vector", "scalar")


def snake_tiles(c):
    return [8 * j + (c if j % 2 == 0 else 7 - c) for j in range(8)]


class Sched:
    def __init__(self):
        self.ops = {e: [] for e in ENGS}
        self.cnt = {}

    def n(self, sem):
        return self.cnt.get(sem, 0)

    def emit(self, eng, fn, waits=(), inc=None, amt=1):
        w = [(s, v) for (s, v) in waits if v > 0]
        self.ops[eng].append((fn, w, inc, amt))
        if inc is not None:
            self.cnt[inc] = self.cnt.get(inc, 0) + amt


def build_nc(phases=(1, 2, 3, 4, 5, 6), debug=False):
    nc = bass.Bass("TRN2", target_bir_lowering=False, detect_race_conditions=False)
    dt_scr = "ExternalOutput" if debug else "Internal"

    xT = nc.dram_tensor("xT", [D, S], F32, kind="ExternalInput")
    xTo = nc.dram_tensor("xTo", [D, 1024], F32, kind="ExternalInput")
    wkvT = nc.dram_tensor("wkvT", [D, 1024], F32, kind="ExternalInput")
    wqT = nc.dram_tensor("wqT", [D, D], F32, kind="ExternalInput")
    woT = nc.dram_tensor("woT", [D, D], F32, kind="ExternalInput")
    wgT = nc.dram_tensor("wgT", [D, S], F32, kind="ExternalInput")
    wuT = nc.dram_tensor("wuT", [D, S], F32, kind="ExternalInput")
    wdT = nc.dram_tensor("wdT", [S, D], F32, kind="ExternalInput")
    cosF = nc.dram_tensor("cosF", [P, S], F32, kind="ExternalInput")
    sinF = nc.dram_tensor("sinF", [P, S], F32, kind="ExternalInput")
    cosO = nc.dram_tensor("cosO", [P, 1024], F32, kind="ExternalInput")
    sinO = nc.dram_tensor("sinO", [P, 1024], F32, kind="ExternalInput")
    eyeD = nc.dram_tensor("eyeD", [P, P], F32, kind="ExternalInput")
    masksD = nc.dram_tensor("masksD", [P, 1024], F32, kind="ExternalInput")
    mtabD = nc.dram_tensor("mtabD", [72, P, 512], F32, kind="ExternalInput")
    cpD = nc.dram_tensor("cpD", [1, 8], I32, kind="ExternalInput")

    out_tok = nc.dram_tensor("out_tok", [1024, D], F32, kind="ExternalOutput")
    kscr = nc.dram_tensor("kscr", [NCH, KVH, P, P], F32, kind=dt_scr)
    vscr = nc.dram_tensor("vscr", [NCH, P, 512], F32, kind=dt_scr)
    q2scr = nc.dram_tensor("q2scr", [H, 1024], F32, kind=dt_scr)
    stscr = nc.dram_tensor("stscr", [D, 1024], F32, kind=dt_scr)
    dbgT = {}
    if debug:
        for nm, shp in (("qsD", [P, 16384]), ("oTD", [P, 16384]),
                        ("cvD", [P, 4096]), ("dnD", [P, 16384]),
                        ("xqD", [P, 16384]), ("a1D", [P, 16384])):
            dbgT[nm] = nc.dram_tensor(nm, shp, F32, kind="ExternalOutput")
        dbgT["bfaD"] = nc.dram_tensor("bfaD", [P, 16384], BF16,
                                      kind="ExternalOutput")

    with ExitStack() as ctx:
        e = ctx.enter_context
        a1 = e(nc.sbuf_tensor("a1", [P, 16384], F32))
        a2 = e(nc.sbuf_tensor("a2", [P, 16384], F32))
        bfa = e(nc.sbuf_tensor("bfa", [P, 16384], BF16))
        sgnb = e(nc.sbuf_tensor("sgnb", [P, 4096], BF16))
        pool = e(nc.sbuf_tensor("pool", [P, 5632], F32))
        masks = e(nc.sbuf_tensor("masks", [P, 1024], F32))
        eye = e(nc.sbuf_tensor("eye", [P, P], F32))
        ones128 = e(nc.sbuf_tensor("ones128", [P, 1], F32))
        ones2d = e(nc.sbuf_tensor("ones2d", [P, P], F32))
        onescol = e(nc.sbuf_tensor("onescol", [1, P], F32))
        onescbf = e(nc.sbuf_tensor("onescbf", [1, P], BF16))
        k2max = e(nc.sbuf_tensor("k2max", [1, 8], F32))
        s2row = e(nc.sbuf_tensor("s2row", [1, 8], F32))
        cpS = e(nc.sbuf_tensor("cpS", [1, 8], I32))
        epsA = e(nc.sbuf_tensor("epsA", [P, 4], F32))
        zcol2 = e(nc.sbuf_tensor("zcol2", [1, P], F32))
        zrow2 = e(nc.sbuf_tensor("zrow2", [1, 512], F32))
        ctmp = e(nc.sbuf_tensor("ctmp", [1, 1024], F32))
        rowb = e(nc.sbuf_tensor("rowb", [1, 1024], F32))
        maxab = e(nc.sbuf_tensor("maxab", [P, 1], F32))
        stok = e(nc.sbuf_tensor("stok", [P, 1], F32))
        ivtok = e(nc.sbuf_tensor("ivtok", [P, 8], F32))
        ps2a = e(nc.psum_tensor("ps2a", [P, 2, 512], F32))
        ps2b = e(nc.psum_tensor("ps2b", [P, 2, 512], F32))
        ps2c = e(nc.psum_tensor("ps2c", [P, 2, 512], F32))
        ps1a = e(nc.psum_tensor("ps1a", [P, 512], F32))
        ps1b = e(nc.psum_tensor("ps1b", [P, 512], F32))

        sch = Sched()
        E = sch.emit
        N = sch.n
        REGS = {}

        PCOS, PSIN, PKRAW, PKSH, PKOUT, PSQ, PVE, PK2 = (
            0, 512, 1024, 1536, 2048, 2560, 3584, 4096)
        PW2, PCO, PSO, PQRAW, PQSH, PSQ2 = 0, 2048, 3072, 4096, 4608, 5120
        PK, PV, PEXP, PZB, PMSK = 0, 1024, 4096, 5120, 2048
        PWG, PWU, PQR, PQM, PWD5 = 0, 2048, 0, 2048, 4096
        PSTX, POTL = 0, 2048
        A1PH, A1PL, A1T1, A1T3, A1PP, A1WD, A1IV = (
            0, 4096, 8192, 9216, 10240, 11264, 13312)

        def dma(eng, out_ap, in_ap, waits=(), inc=None):
            def fn(ee, o=out_ap, i=in_ap):
                return ee.dma_start(out=o(ee) if callable(o) else o,
                                    in_=i(ee) if callable(i) else i)
            E(eng, fn, waits, inc, 16)

        dma("sync", masks[:], masksD.ap(), inc="ldc")
        dma("sync", eye[:], eyeD.ap(), inc="ldc")
        dma("sync", cpS[:], cpD.ap(), inc="ldc")
        def fmemset(v):
            v.memset(ones128[:], 1.0)
            v.memset(ones2d[:], 1.0)
            v.memset(onescol[:], 1.0)
            v.memset(onescbf[:], 1.0)
            v.memset(k2max[:], 0.0)
            v.memset(zcol2[:], 0.0)
            v.memset(zrow2[:], 0.0)
            v.memset(epsA[:, 0:1], 1e-5)
            v.memset(epsA[:, 1:2], 1e-8)
            v.memset(epsA[:, 2:3], 128.5)
            return v.memset(a2[:, 0:512], 0.0)
        E("vector", fmemset, inc="vz")
        for z in range(64, NCH):
            dma("gpsimd", bass.AP(kscr, z * 65536, [[512, P], [1, 512]]),
                a2[:, 0:512], waits=[("vz", 1)], inc="zw")
            dma("gpsimd", bass.AP(vscr, z * 65536, [[512, P], [1, 512]]),
                a2[:, 0:512], waits=[("vz", 1)], inc="zw")
        ZW = N("zw")
        dma("sync", bass.AP(a1, 0, [[16384, P], [1024, NDC], [1, 1024]]),
            bass.AP(wkvT, 0, [[1024, P], [P * 1024, NDC], [1, 1024]]), inc="ldw")

        def ln_group(xbuf, x_off, nb_xn, xn_off, ntk, eps, gate, first_waits,
                     sq0, sq1, cstride=None):
            eps_ap = epsA[0:1, 0:1] if eps == 1e-5 else epsA[0:1, 1:2]
            """LN of [D, ntk] feature-major: xbuf@x_off (chunk i at
            x_off+i*cstride) -> nb_xn@xn_off (chunk i at xn_off+i*cstride)."""
            cs = cstride if cstride is not None else ntk
            base_ts = N("t_sum")
            for i in range(NDC):
                sl = sq0 if i % 2 == 0 else sq1
                wsq = [gate] + first_waits + ([("vz", 1)] if i == 0 else [])
                wsq.append(("t_sum", base_ts if i < 2 else N("t_sum") - 1))
                E("scalar", (lambda sc, i=i, sl=sl: sc.activation(
                    out=pool[:, sl:sl + ntk],
                    in_=xbuf[:, x_off + i * cs:x_off + i * cs + ntk],
                    func=AFT.Square)), waits=wsq, inc="a_sq")
                wmm = [gate] + (first_waits + [("v_st", N("v_st")),
                                               ("vz", 1), ("t_bc", N("t_bc"))]
                                if i == 0 else [])
                E("tensor", (lambda t, i=i: t.matmul(
                    bass.AP(ps2c, 0, [[1024, 1], [1, ntk]]),
                    ones128[:], xbuf[:, x_off + i * cs:x_off + i * cs + ntk],
                    start=(i == 0), stop=(i == NDC - 1))), waits=wmm)
                E("tensor", (lambda t, i=i, sl=sl: t.matmul(
                    bass.AP(ps2c, 32 * 1024, [[1024, 1], [1, ntk]]),
                    ones128[:], pool[:, sl:sl + ntk],
                    start=(i == 0), stop=(i == NDC - 1))),
                    waits=[("a_sq", N("a_sq"))], inc="t_sum")
            ts = N("t_sum")
            m_ap = pool[0:1, sq0:sq0 + ntk]
            v_ap = pool[0:1, sq1:sq1 + ntk]
            r_ap = ctmp[0:1, 0:ntk]
            E("vector", (lambda v: v.tensor_scalar_mul(
                out=m_ap,
                in0=bass.AP(ps2c, 0, [[1024, 1], [1, ntk]]), scalar1=1.0 / D)),
                waits=[("t_sum", ts)])
            E("vector", (lambda v: v.tensor_mul(v_ap, m_ap, m_ap)))
            E("vector", (lambda v: v.scalar_tensor_tensor(
                out=v_ap,
                in0=bass.AP(ps2c, 32 * 1024, [[1024, 1], [1, ntk]]),
                scalar=1.0 / D, in1=v_ap,
                op0=ALU.mult, op1=ALU.subtract)), inc="v_st")
            E("scalar", (lambda sc: sc.activation(
                out=r_ap, in_=v_ap, func=AFT.Sqrt,
                bias=eps_ap)), waits=[("v_st", N("v_st")), ("a_cv", N("a_cv"))],
                inc="a_rs")
            E("vector", (lambda v: v.reciprocal(out=r_ap, in_=r_ap)),
                waits=[("a_rs", N("a_rs"))], inc="v_rr")
            E("tensor", (lambda t: t.matmul(
                ps1a[:, 0:ntk], onescol[:], m_ap,
                start=True, stop=True)),
                waits=[("v_rr", N("v_rr")), ("v_xn", N("v_xn"))])
            E("tensor", (lambda t: t.matmul(
                ps1b[:, 0:ntk], onescol[:], r_ap,
                start=True, stop=True)), inc="t_bc")
            tbc = N("t_bc")
            for i in range(NDC):
                E("vector", (lambda v, i=i: v.tensor_sub(
                    nb_xn[:, xn_off + i * cs:xn_off + i * cs + ntk],
                    xbuf[:, x_off + i * cs:x_off + i * cs + ntk],
                    ps1a[:, 0:ntk])), waits=[("t_bc", tbc)] if i == 0 else [])
                E("vector", (lambda v, i=i: v.tensor_mul(
                    nb_xn[:, xn_off + i * cs:xn_off + i * cs + ntk],
                    nb_xn[:, xn_off + i * cs:xn_off + i * cs + ntk],
                    ps1b[:, 0:ntk])), inc="v_xn")

        if 1 in phases:
            ldw = N("ldw")
            for tg in range(NTG):
                dma("sync",
                    bass.AP(a2, 8192, [[16384, P], [512, NDC], [1, 512]]),
                    bass.AP(xT, tg * 512, [[S, P], [P * S, NDC], [1, 512]]),
                    waits=[("v_xn", N("v_xn"))], inc="xs")
                dma("sync", pool[:, PCOS:PCOS + 512],
                    bass.AP(cosF, tg * 512, [[S, P], [1, 512]]),
                    waits=[("v_ko", N("v_ko"))], inc="cs")
                dma("sync", pool[:, PSIN:PSIN + 512],
                    bass.AP(sinF, tg * 512, [[S, P], [1, 512]]), inc="cs")
                ln_group(a2, 8192, a2, 0, 512, 1e-5, ("xs", N("xs")),
                         [("t_k", N("t_k")), ("t_v", N("t_v"))], PSQ, PSQ + 512)
                vxn = N("v_xn")
                for g in range(KVH):
                    bank = g % 2
                    E("tensor", (lambda t, g=g, bank=bank: [
                        t.matmul(ps2a[:, bank, :],
                                 a1[:, i * 1024 + g * P:i * 1024 + g * P + P],
                                 a2[:, i * 512:(i + 1) * 512],
                                 start=(i == 0), stop=(i == NDC - 1))
                        for i in range(NDC)][-1]),
                        waits=[("v_xn", vxn), ("ldw", ldw),
                               ("v_kc", max(0, N("v_kc") - 1))], inc="t_k")
                    E("vector", (lambda v, bank=bank: v.tensor_copy(
                        out=pool[:, PKRAW:PKRAW + 512], in_=ps2a[:, bank, :])),
                        waits=[("t_k", N("t_k")), ("g_sh", N("g_sh"))],
                        inc="v_kc")
                    dma("gpsimd", pool[0:64, PKSH:PKSH + 512],
                        pool[64:128, PKRAW:PKRAW + 512],
                        waits=[("v_kc", N("v_kc"))], inc="g_sh")
                    dma("gpsimd", pool[64:128, PKSH:PKSH + 512],
                        pool[0:64, PKRAW:PKRAW + 512], inc="g_sh")
                    E("vector", (lambda v: v.tensor_mul(
                        pool[:, PKRAW:PKRAW + 512], pool[:, PKRAW:PKRAW + 512],
                        pool[:, PCOS:PCOS + 512])),
                        waits=[("g_sh", N("g_sh")), ("cs", N("cs")),
                               ("kw", N("kw")), ("a_k2", N("a_k2"))])
                    E("vector", (lambda v: v.tensor_mul(
                        pool[:, PKSH:PKSH + 512], pool[:, PKSH:PKSH + 512],
                        pool[:, PSIN:PSIN + 512])))
                    E("vector", (lambda v: v.tensor_add(
                        pool[:, PKOUT:PKOUT + 512], pool[:, PKRAW:PKRAW + 512],
                        pool[:, PKSH:PKSH + 512])), inc="v_ko")
                    vko = N("v_ko")
                    dma("gpsimd",
                        bass.AP(kscr, (4 * tg) * 65536 + g * 16384,
                                [[P, P], [65536, 4], [1, P]]),
                        pool[:, PKOUT:PKOUT + 512],
                        waits=[("v_ko", vko), ("zw", ZW)], inc="kw")
                for tt in range(4):
                    bank = tt % 2
                    E("tensor", (lambda t, tt=tt, bank=bank: [
                        t.matmul(ps2b[:, bank, :],
                                 a2[:, i * 512 + tt * P:i * 512 + tt * P + P],
                                 a1[:, i * 1024 + 512:(i + 1) * 1024],
                                 start=(i == 0), stop=(i == NDC - 1))
                        for i in range(NDC)][-1]),
                        waits=[("v_xn", vxn), ("ldw", ldw),
                               ("v_ve", max(0, N("v_ve") - 1))], inc="t_v")
                    E("vector", (lambda v, bank=bank: v.tensor_copy(
                        out=pool[:, PVE:PVE + 512], in_=ps2b[:, bank, :])),
                        waits=[("t_v", N("t_v")), ("vw", N("vw"))],
                        inc="v_ve")
                    dma("gpsimd",
                        bass.AP(vscr, (4 * tg + tt) * 65536, [[512, P], [1, 512]]),
                        pool[:, PVE:PVE + 512],
                        waits=[("v_ve", N("v_ve")), ("zw", ZW)], inc="vw")
        P1D = dict(kw=N("kw"), vw=N("vw"), v_s2=N("v_s2"),
                   v_ko=N("v_ko"), a_k2=N("a_k2"), cs=N("cs"),
                   g_sh=N("g_sh"), v_ve=N("v_ve"), t_k2=N("t_k2"))

        if 2 in phases:
            dma("sync", pool[:, PCO:PCO + 1024], cosO.ap(),
                waits=[("kw", P1D["kw"]), ("a_k2", P1D["a_k2"]),
                       ("v_ko", P1D["v_ko"]), ("v_ve", P1D["v_ve"]),
                       ("vw", P1D["vw"])], inc="cso")
            dma("sync", pool[:, PSO:PSO + 1024], sinO.ap(), inc="cso")
            for grp in range(2):
                dma("sync",
                    bass.AP(a2, 8192, [[16384, P], [512, NDC], [1, 512]]),
                    bass.AP(xTo, grp * 512, [[1024, P], [P * 1024, NDC], [1, 512]]),
                    waits=[("v_xn", N("v_xn"))], inc="xs")
                ln_group(a2, 8192, a2, 0, 512, 1e-5, ("xs", N("xs")),
                         [("t_q", N("t_q")), ("v_qo", N("v_qo")),
                          ("v_ko", P1D["v_ko"]), ("t_k2", P1D["t_k2"])],
                         PQRAW, PQSH)
                vxn = N("v_xn")
                for h in range(H):
                    g, hg = h // HG, h % HG
                    bank = h % 2
                    dma("sync",
                        bass.AP(pool, PW2, [[5632, P], [P, NDC], [1, P]]),
                        bass.AP(wqT, h * P, [[D, P], [P * D, NDC], [1, P]]),
                        waits=[("t_q", N("t_q")), ("v_ko", P1D["v_ko"]),
                               ("g_sh", P1D["g_sh"])], inc="wqs")
                    E("tensor", (lambda t, bank=bank: [
                        t.matmul(ps2a[:, bank, :],
                                 pool[:, PW2 + i * P:PW2 + (i + 1) * P],
                                 a2[:, i * 512:(i + 1) * 512],
                                 start=(i == 0), stop=(i == NDC - 1))
                        for i in range(NDC)][-1]),
                        waits=[("v_xn", vxn), ("wqs", N("wqs")),
                               ("v_qc", max(0, N("v_qc") - 1))], inc="t_q")
                    E("vector", (lambda v, bank=bank: v.tensor_copy(
                        out=pool[:, PQRAW:PQRAW + 512], in_=ps2a[:, bank, :])),
                        waits=[("t_q", N("t_q")), ("g_qs", N("g_qs")),
                               ("a_sq", N("a_sq"))], inc="v_qc")
                    dma("gpsimd", pool[0:64, PQSH:PQSH + 512],
                        pool[64:128, PQRAW:PQRAW + 512],
                        waits=[("v_qc", N("v_qc"))], inc="g_qs")
                    dma("gpsimd", pool[64:128, PQSH:PQSH + 512],
                        pool[0:64, PQRAW:PQRAW + 512], inc="g_qs")
                    qcol = g * 4096 + hg * 1024 + grp * 512
                    E("vector", (lambda v, grp=grp: v.tensor_mul(
                        pool[:, PQRAW:PQRAW + 512], pool[:, PQRAW:PQRAW + 512],
                        pool[:, PCO + grp * 512:PCO + grp * 512 + 512])),
                        waits=[("g_qs", N("g_qs")), ("cso", N("cso"))])
                    E("vector", (lambda v, grp=grp: v.tensor_mul(
                        pool[:, PQSH:PQSH + 512], pool[:, PQSH:PQSH + 512],
                        pool[:, PSO + grp * 512:PSO + grp * 512 + 512])))
                    E("vector", (lambda v, qcol=qcol: v.tensor_add(
                        a1[:, qcol:qcol + 512], pool[:, PQRAW:PQRAW + 512],
                        pool[:, PQSH:PQSH + 512])), inc="v_qo")

        P2D = dict(v_qo=N("v_qo"), g_q2=N("g_q2"), t_q=N("t_q"),
                   v_qc=N("v_qc"))

        NIT = 0
        if 3 in phases:
            NJ = [8 * j + 9 for j in range(8)]
            sections = []
            iters = []
            it0 = 0
            for g in range(KVH):
                for j in range(8):
                    for i in range(NJ[j]):
                        iters.append((g, j, i))
                    sections.append((g, j, it0, it0 + NJ[j] - 1))
                    it0 += NJ[j]
            NIT = len(iters)
            blocks = []
            bidx = 0
            it0 = 0
            for g in range(KVH):
                for j in range(8):
                    nblk = (NJ[j] + BLK - 1) // BLK
                    for n in range(nblk):
                        f = it0 + n * BLK
                        l = it0 + min((n + 1) * BLK, NJ[j]) - 1
                        blocks.append((g, j, n, bidx, f, l))
                        bidx += 1
                    it0 += NJ[j]
            it2blk = {}
            for b in blocks:
                for it in range(b[4], b[5] + 1):
                    it2blk[it] = b
            masked = {}
            for it, (g, j, i) in enumerate(iters):
                if i >= 8 * j:
                    masked[it] = (j, i - 8 * j)
            masked_list = sorted(masked)
            mask_pos = {it: k + 1 for k, it in enumerate(masked_list)}
            mask_idx = {it: k for k, it in enumerate(masked_list)}

            ldc = N("ldc")
            BAR3 = [("kw", P1D["kw"]), ("vw", P1D["vw"]), ("ldc", ldc),
                    ("v_qo", P2D["v_qo"]), ("g_q2", P2D["g_q2"]),
                    ("zw", ZW), ("v_s2", P1D["v_s2"])]
            TSC0 = N("t_sc")
            TZA0 = N("t_za")
            AE0 = N("a_e")

            def sc_cnt(it):
                return TSC0 + it + 1

            def za_cnt(it):
                return TZA0 + it + 1

            def e_cnt(it):
                return AE0 + it + 1

            def emit_block_loads(b):
                g, j, n, bi, f, l = b
                dbl = bi % 2
                kwv = ("t_sc", sc_cnt(blocks[bi - 2][5])) if bi >= 2                     else ("t_sc", 0)
                vwv = ("t_za", za_cnt(blocks[bi - 2][5])) if bi >= 2                     else ("t_za", 0)
                dma("sync",
                    bass.AP(pool, PK + dbl * 512, [[5632, P], [P, 4], [1, P]]),
                    bass.AP(kscr, 4 * n * 65536 + g * 16384,
                            [[P, P], [65536, 4], [1, P]]),
                    waits=BAR3 + [kwv], inc="kl")
                dma("gpsimd",
                    bass.AP(pool, PV + dbl * 512, [[5632, P], [P, 4], [1, P]]),
                    bass.AP(vscr, 4 * n * 65536 + g * P,
                            [[512, P], [65536, 4], [1, P]]),
                    waits=BAR3 + [vwv], inc="vl")

            mask_loaded = [0]

            def emit_mask_load_mi(mi):
                it = masked_list[mi]
                jj, b = masked[it]
                wv = ("v_m", mi - 3) if mi >= 4 else ("v_m", 0)
                dma("gpsimd",
                    pool[:, PMSK + (mi % 4) * 512:PMSK + (mi % 4) * 512 + 512],
                    bass.AP(mtabD, (jj * 9 + b) * 65536, [[512, P], [1, 512]]),
                    waits=BAR3 + [wv], inc="ml")
                mask_loaded[0] = mi + 1

            def emit_iter(it, si, s_f):
                g, j, i = iters[it]
                b = it2blk[it]
                bi, n = b[3], b[2]
                dbl = bi % 2
                sbank = it % 2
                sec = g * 8 + j
                kslot = PK + dbl * 512 + (i - 4 * n) * P
                vslot = PV + dbl * 512 + (i - 4 * n) * P
                qcol = g * 4096 + j * P

                def fsc(t, kslot=kslot, qcol=qcol, sbank=sbank):
                    return t.matmul(
                        ps2a[:, sbank, :],
                        pool[:, kslot:kslot + P],
                        bass.AP(a1, qcol, [[16384, P], [1024, 4], [1, P]]),
                        start=True, stop=True, skip_group_check=True)
                w = [("kl", kdma_after[bi] * 16)]
                if it >= 2:
                    w.append(("a_e", e_cnt(it - 2)))
                E("tensor", fsc, waits=w, inc="t_sc")

                we = [("t_sc", sc_cnt(it))]
                if it >= 2:
                    we.append(("t_za", za_cnt(it - 2)))
                E("scalar", (lambda sc, sbank=sbank: sc.activation(
                    out=pool[:, PEXP + sbank * 512:PEXP + sbank * 512 + 512],
                    in_=ps2a[:, sbank, :], func=AFT.Exp, scale=SCALE)),
                    waits=we, inc="a_e")

                if it in masked:
                    mi = mask_idx[it]
                    E("vector", (lambda v, sbank=sbank, mi=mi: v.tensor_tensor(
                        out=pool[:, PEXP + sbank * 512:PEXP + sbank * 512 + 512],
                        in0=pool[:, PEXP + sbank * 512:PEXP + sbank * 512 + 512],
                        in1=pool[:, PMSK + (mi % 4) * 512:
                                 PMSK + (mi % 4) * 512 + 512],
                        op=ALU.mult)),
                        waits=[("a_e", e_cnt(it)), ("ml", mdma_after[it] * 16)],
                        inc="v_m")

                zps = ps1a if sec % 2 == 0 else ps1b
                obank = sec % 2
                first = (it == s_f)

                def fza(t, vslot=vslot, sbank=sbank, zps=zps, obank=obank,
                        first=first):
                    pexp_ap = pool[:, PEXP + sbank * 512:PEXP + sbank * 512 + 512]
                    t.matmul(zps[:, 0:512],
                             ones2d[:], pexp_ap, start=first, stop=False,
                             skip_group_check=True)
                    return t.matmul(ps2b[:, obank, :],
                                    pool[:, vslot:vslot + P], pexp_ap,
                                    start=first, stop=False,
                                    skip_group_check=True)
                wz = [("vl", vdma_after[bi] * 16)]
                if first:
                    wz += [("a_rz", max(0, si - 1)),
                           ("v_e3", max(0, si - 1))]
                if it in masked:
                    wz.append(("v_m", mask_pos[it]))
                else:
                    wz.append(("a_e", e_cnt(it)))
                E("tensor", fza, waits=wz, inc="t_za")

            kdma_after = {}
            vdma_after = {}
            kc = 0
            for b in blocks:
                kc += 1
                kdma_after[b[3]] = kc
                vdma_after[b[3]] = kc
            mdma_after = {it: mask_idx[it] + 1 for it in masked}

            for si, (g, j, s_f, s_l) in enumerate(sections):
                obank = si % 2
                zps = ps1a if si % 2 == 0 else ps1b
                sec_blocks = [b for b in blocks if b[0] == g and b[1] == j]
                for k2 in range(min(2, len(sec_blocks))):
                    emit_block_loads(sec_blocks[k2])
                if si == 0:
                    for _mi in range(4):
                        emit_mask_load_mi(_mi)
                for it in range(s_f, s_l + 1):
                    i = it - s_f
                    if it in masked:
                        mi = mask_idx[it]
                        while mask_loaded[0] <= mi + 4 and \
                                mask_loaded[0] < len(masked_list):
                            emit_mask_load_mi(mask_loaded[0])
                    if i % BLK == 0:
                        nb = i // BLK + 2
                        if nb < len(sec_blocks):
                            emit_block_loads(sec_blocks[nb])
                    emit_iter(it, si, s_f)
                za_end = za_cnt(s_l)
                E("vector", (lambda v, zps=zps: v.reciprocal(
                    out=pool[:, PZB:PZB + 512], in_=zps[:, 0:512])),
                    waits=[("t_za", za_end), ("v_e3", si)], inc="a_rz")
                ocol = 4 * g * 1024 + j * P
                E("vector", (lambda v, obank=obank, ocol=ocol: v.tensor_tensor(
                    out=bass.AP(a2, ocol, [[16384, P], [1024, 4], [1, P]]),
                    in0=bass.AP(ps2b, obank * 512, [[1024, P], [P, 4], [1, P]]),
                    in1=bass.AP(pool, PZB, [[5632, P], [P, 4], [1, P]]),
                    op=ALU.mult)),
                    waits=[("a_rz", N("a_rz"))], inc="v_e3")

        P3D = dict(v_e3=N("v_e3"), t_za=N("t_za"), t_sc=N("t_sc"),
                   a_e=N("a_e"), kl=N("kl"), vl=N("vl"), v_se=N("v_se"))

        if debug and 2 in phases:
            dma("sync", dbgT["qsD"].ap(), a1[:],
                waits=[("v_qo", P2D["v_qo"])] +
                      ([("t_sc", P3D["t_sc"])] if 3 in phases else []),
                inc="dbg")
        if debug and 3 in phases:
            dma("sync", dbgT["oTD"].ap(), a2[:],
                waits=[("v_e3", P3D["v_e3"])], inc="dbg")

        if 4 in phases:
            for grp in range(2):
                for j in range(NDC):
                    bank = (grp * NDC + j) % 2
                    dma("sync",
                        bass.AP(pool, PW2, [[5632, P], [P, NDC], [1, P]]),
                        bass.AP(woT, j * P, [[D, P], [P * D, NDC], [1, P]]),
                        waits=[("t_s4", N("t_s4")), ("v_e3", P3D["v_e3"]),
                               ("kl", P3D["kl"])], inc="wos")
                    E("tensor", (lambda t, bank=bank, grp=grp: [
                        t.matmul(ps2a[:, bank, :],
                                 pool[:, PW2 + i * P:PW2 + (i + 1) * P],
                                 a2[:, i * 1024 + grp * 512:
                                    i * 1024 + grp * 512 + 512],
                                 start=(i == 0), stop=(i == NDC - 1))
                        for i in range(NDC)][-1]),
                        waits=[("wos", N("wos")), ("v_e3", P3D["v_e3"]),
                               ("a_e", P3D["a_e"]),
                               ("v_s4", max(0, N("v_s4") - 1))], inc="t_s4")
                    E("vector", (lambda v, bank=bank: v.tensor_copy(
                        out=pool[:, PVE:PVE + 512], in_=ps2a[:, bank, :])),
                        waits=[("t_s4", N("t_s4")), ("g_s4", N("g_s4")),
                               ("vl", P3D["vl"])], inc="v_s4")
                    dma("gpsimd",
                        bass.AP(stscr, j * P * 1024 + grp * 512,
                                [[1024, P], [1, 512]]),
                        pool[:, PVE:PVE + 512],
                        waits=[("v_s4", N("v_s4"))], inc="g_s4")

        P4D = dict(g_s4=N("g_s4"), t_s4=N("t_s4"), v_s4=N("v_s4"))

        P5D = {}
        if 5 in phases:
            B5 = [("g_s4", P4D["g_s4"]), ("t_s4", P4D["t_s4"]),
                  ("v_s4", P4D["v_s4"])]
            dma("sync",
                bass.AP(a1, 0, [[16384, P], [1024, NDC], [1, 1024]]),
                bass.AP(xTo, 0, [[1024, P], [P * 1024, NDC], [1, 1024]]),
                waits=B5 + [("t_sc", P3D["t_sc"]), ("v_xn", N("v_xn"))],
                inc="x5")
            for rnd, eps in ((0, 1e-5), (1, 1e-8)):
                for cg in range(2):
                    ln_group(a1, cg * 512, a1, cg * 512, 512, eps,
                             ("x5", N("x5")),
                             [("v_s4", P4D["v_s4"]), ("t_q2", N("t_q2")),
                              ("v_xn", N("v_xn")), ("v_se", P3D["v_se"])],
                             PQR, PQM, cstride=1024)
            VXN5 = N("v_xn")
            for j in range(8):
                for i in range(NDC):
                    idx = j * NDC + i
                    psu = ps1a if idx % 2 == 0 else ps1b
                    E("tensor", (lambda t, j=j, i=i, psu=psu: t.transpose(
                        psu[:, 0:P], a1[:, i * 1024 + j * P:i * 1024 + j * P + P],
                        eye[:])),
                        waits=([("v_xn", VXN5), ("v_e3", P3D["v_e3"])]
                               if idx == 0 else []) +
                              [("v_tp", max(0, N("v_tp") - 1))], inc="t_tp")
                    E("vector", (lambda v, j=j, i=i, psu=psu: v.tensor_copy(
                        out=a2[:, j * D + i * P:j * D + (i + 1) * P],
                        in_=psu[:, 0:P])),
                        waits=[("t_tp", N("t_tp"))], inc="v_tp")
            for j in range(8):
                E("vector", (lambda v, j=j: v.reduce_max(
                    maxab[:], a2[:, j * D:(j + 1) * D], axis=AXX,
                    apply_absolute_value=True)),
                    waits=[("v_tp", 16 * (j + 1)), ("a_u", N("a_u"))],
                    inc="v_qm")
                E("vector", (lambda v, j=j: v.tensor_scalar_mul(
                    out=ivtok[:, j:j + 1], in0=maxab[:], scalar1=1.0 / 127.0)),
                    waits=[("v_qm", N("v_qm"))], inc="v_iv0")
                E("vector", (lambda v: v.reciprocal(
                    out=stok[:], in_=maxab[:])),
                    waits=[("v_qm", N("v_qm")), ("a_u", N("a_u"))], inc="v_rc")
                E("vector", (lambda v: v.tensor_scalar_mul(
                    out=stok[:], in0=stok[:], scalar1=127.0)),
                    waits=[("v_rc", N("v_rc"))], inc="a_st")
                E("vector", (lambda v, j=j: v.tensor_scalar(
                    out=a2[:, j * D:(j + 1) * D], in0=a2[:, j * D:(j + 1) * D],
                    scalar1=stok[:], scalar2=128.5,
                    op0=ALU.mult, op1=ALU.add)),
                    waits=[("a_st", N("a_st"))], inc="a_u")
                E("vector", (lambda v, j=j: v.tensor_copy(
                    out=pool[:, PQR:PQR + D], in_=a2[:, j * D:(j + 1) * D])),
                    waits=[("a_u", N("a_u"))])
                for k in range(7, -1, -1):
                    cst = float(1 << k)
                    E("vector", (lambda v, cst=cst: v.tensor_single_scalar(
                        out=pool[:, PQM:PQM + D], in_=pool[:, PQR:PQR + D],
                        scalar=cst, op=ALU.is_ge)))
                    E("vector", (lambda v, cst=cst: v.tensor_scalar_mul(
                        out=pool[:, PQM:PQM + D], in0=pool[:, PQM:PQM + D],
                        scalar1=cst)))
                    E("vector", (lambda v: v.tensor_sub(
                        pool[:, PQR:PQR + D], pool[:, PQR:PQR + D],
                        pool[:, PQM:PQM + D])))
                E("vector", (lambda v, j=j: v.scalar_tensor_tensor(
                    out=a2[:, j * D:(j + 1) * D], in0=a2[:, j * D:(j + 1) * D],
                    scalar=-128.0, in1=pool[:, PQR:PQR + D],
                    op0=ALU.add, op1=ALU.subtract)), inc="v_n")
            for j in range(8):
                for i in range(NDC):
                    idx = j * NDC + i
                    psu = ps1a if idx % 2 == 0 else ps1b
                    E("tensor", (lambda t, j=j, i=i, psu=psu: t.transpose(
                        psu[:, 0:P], a2[:, j * D + i * P:j * D + (i + 1) * P],
                        eye[:])),
                        waits=[("v_n", N("v_n"))] +
                              [("v_tp2", max(0, N("v_tp2") - 1))], inc="t_tp2")
                    E("vector", (lambda v, j=j, i=i, psu=psu: v.tensor_copy(
                        out=bfa[:, i * 1024 + j * P:i * 1024 + j * P + P],
                        in_=psu[:, 0:P])),
                        waits=[("t_tp2", N("t_tp2"))], inc="v_tp2")
            XQDONE = N("v_tp2")
            if P5_TRUNC:
                sch.ops
            for j in range(8):
                psu = ps1a if j % 2 == 0 else ps1b
                E("tensor", (lambda t, psu=psu, j=j: t.transpose(
                    psu[0:1, 0:P], ivtok[:, j:j + 1], eye[:])),
                    waits=[("v_iv0", 8), ("v_iv1", max(0, N("v_iv1") - 1)),
                           ("v_tp2", XQDONE)], inc="t_iv")
                E("vector", (lambda v, psu=psu, j=j: v.tensor_copy(
                    out=rowb[0:1, j * P:(j + 1) * P], in_=psu[0:1, 0:P])),
                    waits=[("t_iv", N("t_iv"))], inc="v_iv1")
            for hf in range(2):
                E("tensor", (lambda t, hf=hf: t.matmul(
                    ps1a[:, 0:512], onescol[:],
                    rowb[0:1, hf * 512:hf * 512 + 512],
                    start=True, stop=True, skip_group_check=True)),
                    waits=[("v_iv1", 8), ("v_iv2", N("v_iv2"))], inc="t_iv2")
                E("vector", (lambda v, hf=hf: v.tensor_copy(
                    out=a1[:, A1IV + hf * 512:A1IV + hf * 512 + 512],
                    in_=ps1a[:, 0:512])),
                    waits=[("t_iv2", N("t_iv2"))], inc="v_iv2")
            IVDONE = N("v_iv2")
            for fc in range(P5_NFC):
                grpH, posH = fc // 4, fc % 4
                dma("sync",
                    bass.AP(pool, PWG, [[5632, P], [P, NDC], [1, P]]),
                    bass.AP(wgT, fc * P, [[S, P], [P * S, NDC], [1, P]]),
                    waits=[("a_sg", N("a_sg")), ("x5", N("x5")),
                           ("v_n", N("v_n")), ("a_u", N("a_u"))],
                    inc="wgl")
                dma("sync",
                    bass.AP(pool, PWU, [[5632, P], [P, NDC], [1, P]]),
                    bass.AP(wuT, fc * P, [[S, P], [P * S, NDC], [1, P]]),
                    waits=[("a_su", N("a_su")), ("v_n", N("v_n"))], inc="wul")
                E("scalar", (lambda sc: sc.activation(
                    out=sgnb[:, 0:2048], in_=pool[:, PWG:PWG + 2048],
                    func=AFT.Sign)),
                    waits=[("wgl", N("wgl")), ("t_g", N("t_g"))],
                    inc="a_sg")
                E("scalar", (lambda sc: sc.activation(
                    out=sgnb[:, 2048:4096], in_=pool[:, PWU:PWU + 2048],
                    func=AFT.Sign)),
                    waits=[("wul", N("wul")), ("t_u", N("t_u"))],
                    inc="a_su")
                E("tensor", (lambda t: [
                    t.matmul(ps2a[:, hf, :], sgnb[:, i * P:(i + 1) * P],
                             bfa[:, i * 1024 + hf * 512:i * 1024 + hf * 512 + 512],
                             start=(i == 0), stop=(i == NDC - 1))
                    for hf in range(2) for i in range(NDC)][-1]),
                    waits=[("a_sg", N("a_sg")), ("v_tp2", XQDONE),
                           ("v_t1", N("v_t1"))], inc="t_g")
                E("tensor", (lambda t: [
                    t.matmul(ps2c[:, hf, :],
                             sgnb[:, 2048 + i * P:2048 + (i + 1) * P],
                             bfa[:, i * 1024 + hf * 512:i * 1024 + hf * 512 + 512],
                             start=(i == 0), stop=(i == NDC - 1))
                    for hf in range(2) for i in range(NDC)][-1]),
                    waits=[("a_su", N("a_su")), ("v_t3", N("v_t3")),
                           ("t_q2", N("t_q2")), ("v_dn", N("v_dn"))],
                    inc="t_u")
                E("vector", (lambda v: v.tensor_tensor(
                    out=a1[:, A1T1:A1T1 + 1024],
                    in0=bass.AP(ps2a, 0, [[1024, P], [1, 1024]]),
                    in1=a1[:, A1IV:A1IV + 1024], op=ALU.mult)),
                    waits=[("t_g", N("t_g")), ("v_iv2", IVDONE),
                           ("a_si", N("a_si"))], inc="v_t1")
                E("scalar", (lambda sc: sc.activation(
                    out=a1[:, A1T1:A1T1 + 1024], in_=a1[:, A1T1:A1T1 + 1024],
                    func=AFT.Silu)),
                    waits=[("v_t1", N("v_t1"))], inc="a_si")
                E("vector", (lambda v: v.tensor_tensor(
                    out=a1[:, A1T3:A1T3 + 1024], in0=a1[:, A1T1:A1T1 + 1024],
                    in1=bass.AP(ps2c, 0, [[1024, P], [1, 1024]]),
                    op=ALU.mult)),
                    waits=[("a_si", N("a_si")), ("t_u", N("t_u"))], inc="v_t3")
                po = posH * 1024
                E("vector", (lambda v, po=po: v.tensor_mul(
                    a1[:, po:po + 1024], a1[:, A1T3:A1T3 + 1024],
                    a1[:, A1IV:A1IV + 1024])),
                    waits=[("t_d", N("t_d"))], inc="v_pp")
                if posH == 3:
                    vpp = N("v_pp")
                    for j in range(NDC):
                        dbl = j % 2
                        psd = ps2b if j % 2 == 0 else ps2c
                        dma("sync",
                            bass.AP(pool, PWD5 + dbl * 512,
                                    [[5632, P], [P, 4], [1, P]]),
                            bass.AP(wdT, (grpH * 512) * D + j * P,
                                    [[D, P], [P * D, 4], [1, P]]),
                            waits=[("a_sd", N("a_sd") - (1 if j >= 2 else 0)),
                                   ("v_qm", 8)], inc="wdl")
                        E("scalar", (lambda sc, dbl=dbl: sc.activation(
                            out=a1[:, A1WD + dbl * 512:A1WD + dbl * 512 + 512],
                            in_=pool[:, PWD5 + dbl * 512:PWD5 + dbl * 512 + 512],
                            func=AFT.Sign)),
                            waits=[("wdl", N("wdl")),
                                   ("t_d", max(0, N("t_d") - 1))], inc="a_sd")
                        E("tensor", (lambda t, dbl=dbl, vpp=vpp, psd=psd: [
                            t.matmul(
                                psd[:, hf, :],
                                a1[:, A1WD + dbl * 512 + c4 * P:
                                   A1WD + dbl * 512 + (c4 + 1) * P],
                                a1[:, c4 * 1024 + hf * 512:
                                   c4 * 1024 + hf * 512 + 512],
                                start=(c4 == 0), stop=(c4 == 3),
                                skip_group_check=True)
                            for hf in range(2)
                            for c4 in range(4)][-1]),
                            waits=[("a_sd", N("a_sd")), ("v_pp", vpp),
                                   ("v_dn", max(0, N("v_dn") - 1))], inc="t_d")
                        if grpH == 0:
                            E("vector", (lambda v, j=j, psd=psd: v.tensor_copy(
                                out=a2[:, j * 1024:(j + 1) * 1024],
                                in_=bass.AP(psd, 0, [[1024, P], [1, 1024]]))),
                                waits=[("t_d", N("t_d")), ("v_n", N("v_n"))],
                                inc="v_dn")
                        else:
                            E("vector", (lambda v, j=j, psd=psd: v.tensor_tensor(
                                out=a2[:, j * 1024:(j + 1) * 1024],
                                in0=bass.AP(psd, 0, [[1024, P], [1, 1024]]),
                                in1=a2[:, j * 1024:(j + 1) * 1024],
                                op=ALU.add)),
                                waits=[("t_d", N("t_d"))], inc="v_dn")
                        if debug and j == 0:
                            dma("gpsimd",
                                bass.AP(dbgT["xqD"], grpH * 1024,
                                        [[16384, P], [1, 1024]]),
                                a2[:, 0:1024],
                                waits=[("v_dn", N("v_dn"))], inc="dbg2")
            P5D = dict(v_dn=N("v_dn"), t_d=N("t_d"))

        if debug and 5 in phases:
            dma("sync", dbgT["dnD"].ap(), a2[:],
                waits=[("v_dn", P5D["v_dn"])], inc="dbg")
            dma("sync", dbgT["bfaD"].ap(), bfa[:],
                waits=[("v_tp2", N("v_tp2"))], inc="dbg")

        if 6 in phases:
            for i in range(NDC):
                E("scalar", (lambda sc, i=i: sc.activation(
                    out=a2[:, i * 1024:(i + 1) * 1024],
                    in_=a2[:, i * 1024:(i + 1) * 1024], func=AFT.Silu)),
                    waits=[("v_dn", P5D["v_dn"])] if i == 0 else [],
                    inc="a_gf")
            AGF = N("a_gf")
            for i in range(NDC):
                dma("sync",
                    bass.AP(pool, PSTX, [[5632, P], [1, 1024]]),
                    bass.AP(stscr, i * P * 1024, [[1024, P], [1, 1024]]),
                    waits=[("v_o6", N("v_o6")), ("t_d", P5D["t_d"]),
                           ("g_s4", P4D["g_s4"])], inc="sx6")
                dma("sync",
                    bass.AP(pool, PSTX + 1024, [[5632, P], [1, 1024]]),
                    bass.AP(xTo, i * P * 1024, [[1024, P], [1, 1024]]),
                    inc="sx6")
                E("vector", (lambda v, i=i: v.tensor_tensor(
                    out=a1[:, i * 1024:(i + 1) * 1024],
                    in0=pool[:, PSTX:PSTX + 1024],
                    in1=a2[:, i * 1024:(i + 1) * 1024], op=ALU.mult)),
                    waits=[("sx6", N("sx6")), ("a_gf", AGF)] +
                          ([("t_d", P5D["t_d"])] if i == 0 else []))
                E("vector", (lambda v, i=i: v.tensor_add(
                    a1[:, i * 1024:(i + 1) * 1024],
                    a1[:, i * 1024:(i + 1) * 1024],
                    pool[:, PSTX + 1024:PSTX + 2048])), inc="v_o6")
            VO6 = N("v_o6")
            for j in range(8):
                for i in range(NDC):
                    idx = j * NDC + i
                    psu = ps1a if idx % 2 == 0 else ps1b
                    E("tensor", (lambda t, j=j, i=i, psu=psu: t.transpose(
                        psu[:, 0:P], a1[:, i * 1024 + j * P:i * 1024 + j * P + P],
                        eye[:])),
                        waits=([("v_o6", VO6)] if idx == 0 else []) +
                              [("v_t6", max(0, N("v_t6") - 1))], inc="t_t6")
                    E("vector", (lambda v, i=i, psu=psu: v.tensor_copy(
                        out=pool[:, POTL + i * P:POTL + (i + 1) * P],
                        in_=psu[:, 0:P])),
                        waits=[("t_t6", N("t_t6")),
                               ("st", N("st"))],
                        inc="v_t6")
                dma("sync",
                    bass.AP(out_tok, j * P * D, [[D, P], [1, D]]),
                    bass.AP(pool, POTL, [[5632, P], [1, D]]),
                    waits=[("v_t6", N("v_t6"))], inc="st")
            E("vector", (lambda v: v.engine_nop()),
              waits=[("st", N("st"))])

        semset = {}
        for eng in ENGS:
            for (fn, w, inc, amt) in sch.ops[eng]:
                for (sm, v) in w:
                    semset.setdefault(sm, 0)
                if inc:
                    semset.setdefault(inc, 0)
        sems = {sm: e(nc.semaphore("sem_" + sm)) for sm in semset}
        block = e(nc.Block())

        def mk(en):
            def run(ee):
                for oi, (fn, w, inc, amt) in enumerate(sch.ops[en]):
                    for (sm, v) in w:
                        ee.wait_ge(sems[sm], v)
                    try:
                        inst = fn(ee)
                    except Exception as ex:
                        raise RuntimeError(
                            f"emit failed: engine={en} op_index={oi} "
                            f"inc={inc}: {type(ex).__name__}: {ex}") from ex
                    if inc is not None:
                        assert inst is not None, (en, inc)
                        inst.then_inc(sems[inc], amt)
            return run
        block.sync(mk("sync"))
        block.gpsimd(mk("gpsimd"))
        block.tensor(mk("tensor"))
        block.vector(mk("vector"))
        block.scalar(mk("scalar"))
    return nc


_TBL = {}


def _tables():
    if _TBL:
        return _TBL
    import jax
    jax.config.update("jax_enable_x64", True)
    import jax.numpy as jnp
    with jax.default_device(jax.devices("cpu")[0]):
        inv_freq = 1.0 / (10000.0 ** (jnp.arange(0, DH, 2, dtype=jnp.float32) / DH))
        freqs = jnp.arange(S, dtype=jnp.float32)[:, None] * inv_freq[None, :]
        cos = np.asarray(jnp.repeat(jnp.cos(freqs), 2, axis=-1))
        sin = np.asarray(jnp.repeat(jnp.sin(freqs), 2, axis=-1))
    sinN = sin.copy()
    sinN[:, :64] = -sinN[:, :64]
    _TBL["cos"] = np.ascontiguousarray(cos.T)
    _TBL["sinN"] = np.ascontiguousarray(sinN.T)
    tri = (np.arange(P)[:, None] <= np.arange(P)[None, :]).astype(np.float32)
    _TBL["tri4"] = np.tile(tri, (1, 4))
    m = np.zeros((P, 1024), dtype=np.float32)
    m[:, :512] = _TBL["tri4"]
    _TBL["masks"] = m
    _TBL["eye"] = np.eye(P, dtype=np.float32)
    return _TBL


def stage_inputs(L0_attn_x, L0_attn_Wq, L0_attn_Wk, L0_attn_Wv, L0_attn_Wo,
                 L0_ffn_W_g, L0_ffn_W_u, L0_ffn_W_d):
    t = _tables()
    xT = np.ascontiguousarray(np.asarray(L0_attn_x)[0].T)
    wkvT = np.ascontiguousarray(
        np.concatenate([np.asarray(L0_attn_Wk).T, np.asarray(L0_attn_Wv).T],
                       axis=1))
    wqT = np.ascontiguousarray(np.asarray(L0_attn_Wq).T)
    woT = np.ascontiguousarray(np.asarray(L0_attn_Wo).T)
    wgT = np.ascontiguousarray(np.asarray(L0_ffn_W_g).T)
    wuT = np.ascontiguousarray(np.asarray(L0_ffn_W_u).T)
    wdT = np.ascontiguousarray(np.asarray(L0_ffn_W_d).T)
    maps = []
    for c in range(8):
        tiles = snake_tiles(c)
        cols = np.concatenate([np.arange(tt * P, tt * P + P) for tt in tiles])
        cp = np.zeros((1, 8), dtype=np.int32)
        cp[0, 0] = c
        mtab = np.zeros((72, P, 512), dtype=np.float32)
        for j in range(8):
            tt = tiles[j]
            for b in range(9):
                i = 8 * j + b
                if i <= tt:
                    mtab[j * 9 + b] = 1.0
                elif i == tt + 1 and i <= 63:
                    mtab[j * 9 + b] = t["tri4"]
        maps.append(dict(
            xT=xT, xTo=np.ascontiguousarray(xT[:, cols]),
            wkvT=wkvT, wqT=wqT, woT=woT, wgT=wgT, wuT=wuT, wdT=wdT,
            cosF=t["cos"], sinF=t["sinN"],
            cosO=np.ascontiguousarray(t["cos"][:, cols]),
            sinO=np.ascontiguousarray(t["sinN"][:, cols]),
            eyeD=t["eye"], masksD=t["masks"], cpD=cp,
            mtabD=mtab))
    return maps


def kernel(**inputs):
    maps = stage_inputs(**inputs)
    nc = build_nc()
    res = run_bass_kernel_spmd(nc, maps, core_ids=list(range(8)))
    out = np.zeros((S, D), dtype=np.float32)
    for c in range(8):
        tiles = snake_tiles(c)
        oc = res.results[c]["out_tok"]
        for j, tt in enumerate(tiles):
            out[tt * P:(tt + 1) * P, :] = oc[j * P:(j + 1) * P, :]
    return out.reshape(1, S, D)


# version 10
# speedup vs baseline: 1.6125x; 1.1880x over previous
import numpy as np
import concourse.bass as bass
import concourse.mybir as mybir
from concourse.bass_utils import run_bass_kernel_spmd
from contextlib import ExitStack

F32 = mybir.dt.float32
F32R = mybir.dt.float32r
BF16 = mybir.dt.bfloat16
I32 = mybir.dt.int32
AFT = mybir.ActivationFunctionType
ALU = mybir.AluOpType
AXX = mybir.AxisListType.X

P = 128
S = 8192
D = 2048
NDC = 16
KVH = 4
HG = 4
H = 16
DH = 128
NTG = 16
SCALE = float(DH) ** -0.5
SQ128 = float(DH) ** 0.5
SHIFT = 40.0
L_PAIR = [19, 51, 83, 115]
BLK = 4
NCH = 72
P5_TRUNC = 0
P5_NFC = 64

ENGS = ("sync", "gpsimd", "tensor", "vector", "scalar")


def snake_tiles(c):
    return [8 * j + (c if j % 2 == 0 else 7 - c) for j in range(8)]


class Sched:
    def __init__(self):
        self.ops = {e: [] for e in ENGS}
        self.cnt = {}

    def n(self, sem):
        return self.cnt.get(sem, 0)

    def emit(self, eng, fn, waits=(), inc=None, amt=1):
        w = [(s, v) for (s, v) in waits if v > 0]
        self.ops[eng].append((fn, w, inc, amt))
        if inc is not None:
            self.cnt[inc] = self.cnt.get(inc, 0) + amt


def build_nc(phases=(1, 2, 3, 4, 5, 6), debug=False):
    nc = bass.Bass("TRN2", target_bir_lowering=False, detect_race_conditions=False)
    dt_scr = "ExternalOutput" if debug else "Internal"

    xT = nc.dram_tensor("xT", [D, S], F32, kind="ExternalInput")
    xTo = nc.dram_tensor("xTo", [D, 1024], F32, kind="ExternalInput")
    wkvT = nc.dram_tensor("wkvT", [D, 1024], F32, kind="ExternalInput")
    wqT = nc.dram_tensor("wqT", [D, D], F32, kind="ExternalInput")
    woT = nc.dram_tensor("woT", [D, D], F32, kind="ExternalInput")
    wgT = nc.dram_tensor("wgT", [D, S], F32, kind="ExternalInput")
    wuT = nc.dram_tensor("wuT", [D, S], F32, kind="ExternalInput")
    wdT = nc.dram_tensor("wdT", [S, D], F32, kind="ExternalInput")
    cosF = nc.dram_tensor("cosF", [P, S], F32, kind="ExternalInput")
    sinF = nc.dram_tensor("sinF", [P, S], F32, kind="ExternalInput")
    cosO = nc.dram_tensor("cosO", [P, 1024], F32, kind="ExternalInput")
    sinO = nc.dram_tensor("sinO", [P, 1024], F32, kind="ExternalInput")
    eyeD = nc.dram_tensor("eyeD", [P, P], F32, kind="ExternalInput")
    masksD = nc.dram_tensor("masksD", [P, 1024], F32, kind="ExternalInput")
    mtabD = nc.dram_tensor("mtabD", [72, P, 512], F32, kind="ExternalInput")
    cpD = nc.dram_tensor("cpD", [1, 8], I32, kind="ExternalInput")

    out_tok = nc.dram_tensor("out_tok", [1024, D], F32, kind="ExternalOutput")
    kscr = nc.dram_tensor("kscr", [NCH, KVH, P, P], F32, kind=dt_scr)
    vscr = nc.dram_tensor("vscr", [NCH, P, 512], F32, kind=dt_scr)
    q2scr = nc.dram_tensor("q2scr", [H, 1024], F32, kind=dt_scr)
    stscr = nc.dram_tensor("stscr", [D, 1024], F32, kind=dt_scr)
    dbgT = {}
    if debug:
        for nm, shp in (("qsD", [P, 16384]), ("oTD", [P, 16384]),
                        ("cvD", [P, 4096]), ("dnD", [P, 16384]),
                        ("xqD", [P, 16384]), ("a1D", [P, 16384])):
            dbgT[nm] = nc.dram_tensor(nm, shp, F32, kind="ExternalOutput")
        dbgT["bfaD"] = nc.dram_tensor("bfaD", [P, 16384], BF16,
                                      kind="ExternalOutput")

    with ExitStack() as ctx:
        e = ctx.enter_context
        a1 = e(nc.sbuf_tensor("a1", [P, 16384], F32))
        a2 = e(nc.sbuf_tensor("a2", [P, 16384], F32))
        bfa = e(nc.sbuf_tensor("bfa", [P, 16384], BF16))
        sgnb = e(nc.sbuf_tensor("sgnb", [P, 4096], BF16))
        pool = e(nc.sbuf_tensor("pool", [P, 5632], F32))
        masks = e(nc.sbuf_tensor("masks", [P, 1024], F32))
        eye = e(nc.sbuf_tensor("eye", [P, P], F32))
        ones128 = e(nc.sbuf_tensor("ones128", [P, 1], F32))
        ones2d = e(nc.sbuf_tensor("ones2d", [P, P], F32))
        onescol = e(nc.sbuf_tensor("onescol", [1, P], F32))
        onescbf = e(nc.sbuf_tensor("onescbf", [1, P], BF16))
        k2max = e(nc.sbuf_tensor("k2max", [1, 8], F32))
        s2row = e(nc.sbuf_tensor("s2row", [1, 8], F32))
        cpS = e(nc.sbuf_tensor("cpS", [1, 8], I32))
        epsA = e(nc.sbuf_tensor("epsA", [P, 4], F32))
        zcol2 = e(nc.sbuf_tensor("zcol2", [1, P], F32))
        zrow2 = e(nc.sbuf_tensor("zrow2", [1, 512], F32))
        ctmp = e(nc.sbuf_tensor("ctmp", [1, 1024], F32))
        rowb = e(nc.sbuf_tensor("rowb", [1, 1024], F32))
        maxab = e(nc.sbuf_tensor("maxab", [P, 1], F32))
        stok = e(nc.sbuf_tensor("stok", [P, 1], F32))
        ivtok = e(nc.sbuf_tensor("ivtok", [P, 8], F32))
        ps2a = e(nc.psum_tensor("ps2a", [P, 2, 512], F32))
        ps2b = e(nc.psum_tensor("ps2b", [P, 2, 512], F32))
        ps2c = e(nc.psum_tensor("ps2c", [P, 2, 512], F32))
        ps1a = e(nc.psum_tensor("ps1a", [P, 512], F32))
        ps1b = e(nc.psum_tensor("ps1b", [P, 512], F32))

        sch = Sched()
        E = sch.emit
        N = sch.n
        REGS = {}

        PCOS, PSIN, PKRAW, PKSH, PKOUT, PSQ, PVE, PK2 = (
            0, 512, 1024, 1536, 2048, 2560, 3584, 4096)
        PW2, PCO, PSO, PQRAW, PQSH, PSQ2 = 0, 2048, 3072, 4096, 4608, 5120
        PK, PV, PEXP, PZB, PMSK = 0, 1024, 4096, 5120, 2048
        PWG, PWU, PQR, PQM, PWD5 = 0, 2048, 0, 2048, 4096
        PSTX, POTL = 0, 2048
        A1PH, A1PL, A1T1, A1T3, A1PP, A1WD, A1IV = (
            0, 4096, 8192, 9216, 10240, 11264, 13312)

        def dma(eng, out_ap, in_ap, waits=(), inc=None):
            def fn(ee, o=out_ap, i=in_ap):
                return ee.dma_start(out=o(ee) if callable(o) else o,
                                    in_=i(ee) if callable(i) else i)
            E(eng, fn, waits, inc, 16)

        dma("sync", masks[:], masksD.ap(), inc="ldc")
        dma("sync", eye[:], eyeD.ap(), inc="ldc")
        dma("sync", cpS[:], cpD.ap(), inc="ldc")
        def fmemset(v):
            v.memset(ones128[:], 1.0)
            v.memset(ones2d[:], 1.0)
            v.memset(onescol[:], 1.0)
            v.memset(onescbf[:], 1.0)
            v.memset(k2max[:], 0.0)
            v.memset(zcol2[:], 0.0)
            v.memset(zrow2[:], 0.0)
            v.memset(epsA[:, 0:1], 1e-5)
            v.memset(epsA[:, 1:2], 1e-8)
            v.memset(epsA[:, 2:3], 128.5)
            return v.memset(a2[:, 0:512], 0.0)
        E("vector", fmemset, inc="vz")
        for z in range(64, NCH):
            dma("gpsimd", bass.AP(kscr, z * 65536, [[512, P], [1, 512]]),
                a2[:, 0:512], waits=[("vz", 1)], inc="zw")
            dma("gpsimd", bass.AP(vscr, z * 65536, [[512, P], [1, 512]]),
                a2[:, 0:512], waits=[("vz", 1)], inc="zw")
        ZW = N("zw")
        dma("sync", bass.AP(a1, 0, [[16384, P], [1024, NDC], [1, 1024]]),
            bass.AP(wkvT, 0, [[1024, P], [P * 1024, NDC], [1, 1024]]), inc="ldw")

        def ln_group(xbuf, x_off, nb_xn, xn_off, ntk, eps, gate, first_waits,
                     sq0, sq1, cstride=None):
            eps_ap = epsA[0:1, 0:1] if eps == 1e-5 else epsA[0:1, 1:2]
            """LN of [D, ntk] feature-major: xbuf@x_off (chunk i at
            x_off+i*cstride) -> nb_xn@xn_off (chunk i at xn_off+i*cstride)."""
            cs = cstride if cstride is not None else ntk
            base_ts = N("t_sum")
            for i in range(NDC):
                sl = sq0 if i % 2 == 0 else sq1
                wsq = [gate] + first_waits + ([("vz", 1)] if i == 0 else [])
                wsq.append(("t_sum", base_ts if i < 2 else N("t_sum") - 1))
                E("scalar", (lambda sc, i=i, sl=sl: sc.activation(
                    out=pool[:, sl:sl + ntk],
                    in_=xbuf[:, x_off + i * cs:x_off + i * cs + ntk],
                    func=AFT.Square)), waits=wsq, inc="a_sq")
                wmm = [gate] + (first_waits + [("v_st", N("v_st")),
                                               ("vz", 1), ("t_bc", N("t_bc"))]
                                if i == 0 else [])
                E("tensor", (lambda t, i=i: t.matmul(
                    bass.AP(ps2c, 0, [[1024, 1], [1, ntk]]),
                    ones128[:], xbuf[:, x_off + i * cs:x_off + i * cs + ntk],
                    start=(i == 0), stop=(i == NDC - 1))), waits=wmm)
                E("tensor", (lambda t, i=i, sl=sl: t.matmul(
                    bass.AP(ps2c, 32 * 1024, [[1024, 1], [1, ntk]]),
                    ones128[:], pool[:, sl:sl + ntk],
                    start=(i == 0), stop=(i == NDC - 1))),
                    waits=[("a_sq", N("a_sq"))], inc="t_sum")
            ts = N("t_sum")
            m_ap = pool[0:1, sq0:sq0 + ntk]
            v_ap = pool[0:1, sq1:sq1 + ntk]
            r_ap = ctmp[0:1, 0:ntk]
            E("vector", (lambda v: v.tensor_scalar_mul(
                out=m_ap,
                in0=bass.AP(ps2c, 0, [[1024, 1], [1, ntk]]), scalar1=1.0 / D)),
                waits=[("t_sum", ts)])
            E("vector", (lambda v: v.tensor_mul(v_ap, m_ap, m_ap)))
            E("vector", (lambda v: v.scalar_tensor_tensor(
                out=v_ap,
                in0=bass.AP(ps2c, 32 * 1024, [[1024, 1], [1, ntk]]),
                scalar=1.0 / D, in1=v_ap,
                op0=ALU.mult, op1=ALU.subtract)), inc="v_st")
            E("scalar", (lambda sc: sc.activation(
                out=r_ap, in_=v_ap, func=AFT.Sqrt,
                bias=eps_ap)), waits=[("v_st", N("v_st")), ("a_cv", N("a_cv"))],
                inc="a_rs")
            E("vector", (lambda v: v.reciprocal(out=r_ap, in_=r_ap)),
                waits=[("a_rs", N("a_rs"))], inc="v_rr")
            E("tensor", (lambda t: t.matmul(
                ps1a[:, 0:ntk], onescol[:], m_ap,
                start=True, stop=True)),
                waits=[("v_rr", N("v_rr")), ("v_xn", N("v_xn"))])
            E("tensor", (lambda t: t.matmul(
                ps1b[:, 0:ntk], onescol[:], r_ap,
                start=True, stop=True)), inc="t_bc")
            tbc = N("t_bc")
            for i in range(NDC):
                E("vector", (lambda v, i=i: v.tensor_sub(
                    nb_xn[:, xn_off + i * cs:xn_off + i * cs + ntk],
                    xbuf[:, x_off + i * cs:x_off + i * cs + ntk],
                    ps1a[:, 0:ntk])), waits=[("t_bc", tbc)] if i == 0 else [])
                E("vector", (lambda v, i=i: v.tensor_mul(
                    nb_xn[:, xn_off + i * cs:xn_off + i * cs + ntk],
                    nb_xn[:, xn_off + i * cs:xn_off + i * cs + ntk],
                    ps1b[:, 0:ntk])), inc="v_xn")

        if 1 in phases:
            ldw = N("ldw")
            for tg in range(NTG):
                dma("sync",
                    bass.AP(a2, 8192, [[16384, P], [512, NDC], [1, 512]]),
                    bass.AP(xT, tg * 512, [[S, P], [P * S, NDC], [1, 512]]),
                    waits=[("v_xn", N("v_xn"))], inc="xs")
                dma("sync", pool[:, PCOS:PCOS + 512],
                    bass.AP(cosF, tg * 512, [[S, P], [1, 512]]),
                    waits=[("v_ko", N("v_ko"))], inc="cs")
                dma("sync", pool[:, PSIN:PSIN + 512],
                    bass.AP(sinF, tg * 512, [[S, P], [1, 512]]), inc="cs")
                ln_group(a2, 8192, a2, 0, 512, 1e-5, ("xs", N("xs")),
                         [("t_k", N("t_k")), ("t_v", N("t_v"))], PSQ, PSQ + 512)
                vxn = N("v_xn")
                for g in range(KVH):
                    bank = g % 2
                    E("tensor", (lambda t, g=g, bank=bank: [
                        t.matmul(ps2a[:, bank, :],
                                 a1[:, i * 1024 + g * P:i * 1024 + g * P + P],
                                 a2[:, i * 512:(i + 1) * 512],
                                 start=(i == 0), stop=(i == NDC - 1))
                        for i in range(NDC)][-1]),
                        waits=[("v_xn", vxn), ("ldw", ldw),
                               ("v_kc", max(0, N("v_kc") - 1))], inc="t_k")
                    E("vector", (lambda v, bank=bank: v.tensor_copy(
                        out=pool[:, PKRAW:PKRAW + 512], in_=ps2a[:, bank, :])),
                        waits=[("t_k", N("t_k")), ("g_sh", N("g_sh"))],
                        inc="v_kc")
                    dma("gpsimd", pool[0:64, PKSH:PKSH + 512],
                        pool[64:128, PKRAW:PKRAW + 512],
                        waits=[("v_kc", N("v_kc"))], inc="g_sh")
                    dma("gpsimd", pool[64:128, PKSH:PKSH + 512],
                        pool[0:64, PKRAW:PKRAW + 512], inc="g_sh")
                    E("vector", (lambda v: v.tensor_mul(
                        pool[:, PKRAW:PKRAW + 512], pool[:, PKRAW:PKRAW + 512],
                        pool[:, PCOS:PCOS + 512])),
                        waits=[("g_sh", N("g_sh")), ("cs", N("cs")),
                               ("kw", N("kw")), ("a_k2", N("a_k2"))])
                    E("vector", (lambda v: v.tensor_mul(
                        pool[:, PKSH:PKSH + 512], pool[:, PKSH:PKSH + 512],
                        pool[:, PSIN:PSIN + 512])))
                    E("vector", (lambda v: v.tensor_add(
                        pool[:, PKOUT:PKOUT + 512], pool[:, PKRAW:PKRAW + 512],
                        pool[:, PKSH:PKSH + 512])), inc="v_ko")
                    vko = N("v_ko")
                    dma("gpsimd",
                        bass.AP(kscr, (4 * tg) * 65536 + g * 16384,
                                [[P, P], [65536, 4], [1, P]]),
                        pool[:, PKOUT:PKOUT + 512],
                        waits=[("v_ko", vko), ("zw", ZW)], inc="kw")
                for tt in range(4):
                    bank = tt % 2
                    E("tensor", (lambda t, tt=tt, bank=bank: [
                        t.matmul(ps2b[:, bank, :],
                                 a2[:, i * 512 + tt * P:i * 512 + tt * P + P],
                                 a1[:, i * 1024 + 512:(i + 1) * 1024],
                                 start=(i == 0), stop=(i == NDC - 1))
                        for i in range(NDC)][-1]),
                        waits=[("v_xn", vxn), ("ldw", ldw),
                               ("v_ve", max(0, N("v_ve") - 1))], inc="t_v")
                    E("vector", (lambda v, bank=bank: v.tensor_copy(
                        out=pool[:, PVE:PVE + 512], in_=ps2b[:, bank, :])),
                        waits=[("t_v", N("t_v")), ("vw", N("vw"))],
                        inc="v_ve")
                    dma("gpsimd",
                        bass.AP(vscr, (4 * tg + tt) * 65536, [[512, P], [1, 512]]),
                        pool[:, PVE:PVE + 512],
                        waits=[("v_ve", N("v_ve")), ("zw", ZW)], inc="vw")
        P1D = dict(kw=N("kw"), vw=N("vw"), v_s2=N("v_s2"),
                   v_ko=N("v_ko"), a_k2=N("a_k2"), cs=N("cs"),
                   g_sh=N("g_sh"), v_ve=N("v_ve"), t_k2=N("t_k2"))

        if 2 in phases:
            dma("sync", pool[:, PCO:PCO + 1024], cosO.ap(),
                waits=[("kw", P1D["kw"]), ("a_k2", P1D["a_k2"]),
                       ("v_ko", P1D["v_ko"]), ("v_ve", P1D["v_ve"]),
                       ("vw", P1D["vw"])], inc="cso")
            dma("sync", pool[:, PSO:PSO + 1024], sinO.ap(), inc="cso")
            for grp in range(2):
                dma("sync",
                    bass.AP(a2, 8192, [[16384, P], [512, NDC], [1, 512]]),
                    bass.AP(xTo, grp * 512, [[1024, P], [P * 1024, NDC], [1, 512]]),
                    waits=[("v_xn", N("v_xn"))], inc="xs")
                ln_group(a2, 8192, a2, 0, 512, 1e-5, ("xs", N("xs")),
                         [("t_q", N("t_q")), ("v_qo", N("v_qo")),
                          ("v_ko", P1D["v_ko"]), ("t_k2", P1D["t_k2"])],
                         PQRAW, PQSH)
                vxn = N("v_xn")
                for h in range(H):
                    g, hg = h // HG, h % HG
                    bank = h % 2
                    dma("sync",
                        bass.AP(pool, PW2, [[5632, P], [P, NDC], [1, P]]),
                        bass.AP(wqT, h * P, [[D, P], [P * D, NDC], [1, P]]),
                        waits=[("t_q", N("t_q")), ("v_ko", P1D["v_ko"]),
                               ("g_sh", P1D["g_sh"])], inc="wqs")
                    E("tensor", (lambda t, bank=bank: [
                        t.matmul(ps2a[:, bank, :],
                                 pool[:, PW2 + i * P:PW2 + (i + 1) * P],
                                 a2[:, i * 512:(i + 1) * 512],
                                 start=(i == 0), stop=(i == NDC - 1))
                        for i in range(NDC)][-1]),
                        waits=[("v_xn", vxn), ("wqs", N("wqs")),
                               ("v_qc", max(0, N("v_qc") - 1))], inc="t_q")
                    E("vector", (lambda v, bank=bank: v.tensor_copy(
                        out=pool[:, PQRAW:PQRAW + 512], in_=ps2a[:, bank, :])),
                        waits=[("t_q", N("t_q")), ("g_qs", N("g_qs")),
                               ("a_sq", N("a_sq"))], inc="v_qc")
                    dma("gpsimd", pool[0:64, PQSH:PQSH + 512],
                        pool[64:128, PQRAW:PQRAW + 512],
                        waits=[("v_qc", N("v_qc"))], inc="g_qs")
                    dma("gpsimd", pool[64:128, PQSH:PQSH + 512],
                        pool[0:64, PQRAW:PQRAW + 512], inc="g_qs")
                    qcol = g * 4096 + hg * 1024 + grp * 512
                    E("vector", (lambda v, grp=grp: v.tensor_mul(
                        pool[:, PQRAW:PQRAW + 512], pool[:, PQRAW:PQRAW + 512],
                        pool[:, PCO + grp * 512:PCO + grp * 512 + 512])),
                        waits=[("g_qs", N("g_qs")), ("cso", N("cso"))])
                    E("vector", (lambda v, grp=grp: v.tensor_mul(
                        pool[:, PQSH:PQSH + 512], pool[:, PQSH:PQSH + 512],
                        pool[:, PSO + grp * 512:PSO + grp * 512 + 512])))
                    E("vector", (lambda v, qcol=qcol: v.tensor_add(
                        a1[:, qcol:qcol + 512], pool[:, PQRAW:PQRAW + 512],
                        pool[:, PQSH:PQSH + 512])), inc="v_qo")

        P2D = dict(v_qo=N("v_qo"), g_q2=N("g_q2"), t_q=N("t_q"),
                   v_qc=N("v_qc"))

        NIT = 0
        if 3 in phases:
            NJ = [8 * j + 9 for j in range(8)]
            sections = []
            iters = []
            it0 = 0
            for g in range(KVH):
                for j in range(8):
                    for i in range(NJ[j]):
                        iters.append((g, j, i))
                    sections.append((g, j, it0, it0 + NJ[j] - 1))
                    it0 += NJ[j]
            NIT = len(iters)
            blocks = []
            bidx = 0
            it0 = 0
            for g in range(KVH):
                for j in range(8):
                    nblk = (NJ[j] + BLK - 1) // BLK
                    for n in range(nblk):
                        f = it0 + n * BLK
                        l = it0 + min((n + 1) * BLK, NJ[j]) - 1
                        blocks.append((g, j, n, bidx, f, l))
                        bidx += 1
                    it0 += NJ[j]
            it2blk = {}
            for b in blocks:
                for it in range(b[4], b[5] + 1):
                    it2blk[it] = b
            masked = {}
            for it, (g, j, i) in enumerate(iters):
                if i >= 8 * j:
                    masked[it] = (j, i - 8 * j)
            masked_list = sorted(masked)
            mask_pos = {it: k + 1 for k, it in enumerate(masked_list)}
            mask_idx = {it: k for k, it in enumerate(masked_list)}

            ldc = N("ldc")
            BAR3 = [("kw", P1D["kw"]), ("vw", P1D["vw"]), ("ldc", ldc),
                    ("v_qo", P2D["v_qo"]), ("g_q2", P2D["g_q2"]),
                    ("zw", ZW), ("v_s2", P1D["v_s2"])]
            TSC0 = N("t_sc")
            TZA0 = N("t_za")
            AE0 = N("a_e")

            def sc_cnt(it):
                return TSC0 + it + 1

            def za_cnt(it):
                return TZA0 + it + 1

            def e_cnt(it):
                return AE0 + it + 1

            def emit_block_loads(b):
                g, j, n, bi, f, l = b
                dbl = bi % 2
                kwv = ("t_sc", sc_cnt(blocks[bi - 2][5])) if bi >= 2                     else ("t_sc", 0)
                vwv = ("t_za", za_cnt(blocks[bi - 2][5])) if bi >= 2                     else ("t_za", 0)
                dma("sync",
                    bass.AP(pool, PK + dbl * 512, [[5632, P], [P, 4], [1, P]]),
                    bass.AP(kscr, 4 * n * 65536 + g * 16384,
                            [[P, P], [65536, 4], [1, P]]),
                    waits=BAR3 + [kwv], inc="kl")
                dma("gpsimd",
                    bass.AP(pool, PV + dbl * 512, [[5632, P], [P, 4], [1, P]]),
                    bass.AP(vscr, 4 * n * 65536 + g * P,
                            [[512, P], [65536, 4], [1, P]]),
                    waits=BAR3 + [vwv], inc="vl")

            mask_loaded = [0]

            def emit_mask_load_mi(mi):
                it = masked_list[mi]
                jj, b = masked[it]
                wv = ("v_m", mi - 3) if mi >= 4 else ("v_m", 0)
                dma("gpsimd",
                    pool[:, PMSK + (mi % 4) * 512:PMSK + (mi % 4) * 512 + 512],
                    bass.AP(mtabD, (jj * 9 + b) * 65536, [[512, P], [1, 512]]),
                    waits=BAR3 + [wv], inc="ml")
                mask_loaded[0] = mi + 1

            def emit_sc(it):
                g, j, i = iters[it]
                b = it2blk[it]
                bi, n = b[3], b[2]
                dbl = bi % 2
                sbank = it % 2
                kslot = PK + dbl * 512 + (i - 4 * n) * P
                qcol = g * 4096 + j * P

                def fsc(t, kslot=kslot, qcol=qcol, sbank=sbank):
                    return t.matmul(
                        ps2a[:, sbank, :],
                        pool[:, kslot:kslot + P],
                        bass.AP(a1, qcol, [[16384, P], [1024, 4], [1, P]]),
                        start=True, stop=True, skip_group_check=True)
                w = [("kl", kdma_after[bi] * 16)]
                if it >= 2:
                    w.append(("a_e", e_cnt(it - 2)))
                E("tensor", fsc, waits=w, inc="t_sc")

                we = [("t_sc", sc_cnt(it))]
                if it >= 2:
                    we.append(("t_za", za_cnt(it - 2)))
                E("scalar", (lambda sc, sbank=sbank: sc.activation(
                    out=pool[:, PEXP + sbank * 512:PEXP + sbank * 512 + 512],
                    in_=ps2a[:, sbank, :], func=AFT.Exp, scale=SCALE)),
                    waits=we, inc="a_e")

                if it in masked:
                    mi = mask_idx[it]
                    E("vector", (lambda v, sbank=sbank, mi=mi: v.tensor_tensor(
                        out=pool[:, PEXP + sbank * 512:PEXP + sbank * 512 + 512],
                        in0=pool[:, PEXP + sbank * 512:PEXP + sbank * 512 + 512],
                        in1=pool[:, PMSK + (mi % 4) * 512:
                                 PMSK + (mi % 4) * 512 + 512],
                        op=ALU.mult)),
                        waits=[("a_e", e_cnt(it)), ("ml", mdma_after[it] * 16)],
                        inc="v_m")

            def emit_za(it, si, s_f):
                g, j, i = iters[it]
                b = it2blk[it]
                bi, n = b[3], b[2]
                dbl = bi % 2
                sbank = it % 2
                sec = g * 8 + j
                vslot = PV + dbl * 512 + (i - 4 * n) * P
                zps = ps1a if sec % 2 == 0 else ps1b
                obank = sec % 2
                first = (it == s_f)

                def fza(t, vslot=vslot, sbank=sbank, zps=zps, obank=obank,
                        first=first):
                    pexp_ap = pool[:, PEXP + sbank * 512:PEXP + sbank * 512 + 512]
                    t.matmul(zps[:, 0:512],
                             ones2d[:], pexp_ap, start=first, stop=False,
                             skip_group_check=True)
                    return t.matmul(ps2b[:, obank, :],
                                    pool[:, vslot:vslot + P], pexp_ap,
                                    start=first, stop=False,
                                    skip_group_check=True)
                wz = [("vl", vdma_after[bi] * 16)]
                if first:
                    wz += [("a_rz", max(0, si - 1)),
                           ("v_e3", max(0, si - 1))]
                if it in masked:
                    wz.append(("v_m", mask_pos[it]))
                else:
                    wz.append(("a_e", e_cnt(it)))
                E("tensor", fza, waits=wz, inc="t_za")

            kdma_after = {}
            vdma_after = {}
            kc = 0
            for b in blocks:
                kc += 1
                kdma_after[b[3]] = kc
                vdma_after[b[3]] = kc
            mdma_after = {it: mask_idx[it] + 1 for it in masked}

            pend = None
            for si, (g, j, s_f, s_l) in enumerate(sections):
                obank = si % 2
                zps = ps1a if si % 2 == 0 else ps1b
                sec_blocks = [b for b in blocks if b[0] == g and b[1] == j]
                for k2 in range(min(2, len(sec_blocks))):
                    emit_block_loads(sec_blocks[k2])
                if si == 0:
                    for _mi in range(4):
                        emit_mask_load_mi(_mi)
                for it in range(s_f, s_l + 1):
                    i = it - s_f
                    if it in masked:
                        mi = mask_idx[it]
                        while mask_loaded[0] <= mi + 4 and \
                                mask_loaded[0] < len(masked_list):
                            emit_mask_load_mi(mask_loaded[0])
                    if i % BLK == 0:
                        nb = i // BLK + 2
                        if nb < len(sec_blocks):
                            emit_block_loads(sec_blocks[nb])
                    emit_sc(it)
                    if pend is not None:
                        emit_za(*pend)
                    pend = (it, si, s_f)
                if si == len(sections) - 1 and pend is not None:
                    emit_za(*pend)
                    pend = None
                za_end = za_cnt(s_l)
                E("vector", (lambda v, zps=zps: v.reciprocal(
                    out=pool[:, PZB:PZB + 512], in_=zps[:, 0:512])),
                    waits=[("t_za", za_end), ("v_e3", si)], inc="a_rz")
                ocol = 4 * g * 1024 + j * P
                E("vector", (lambda v, obank=obank, ocol=ocol: v.tensor_tensor(
                    out=bass.AP(a2, ocol, [[16384, P], [1024, 4], [1, P]]),
                    in0=bass.AP(ps2b, obank * 512, [[1024, P], [P, 4], [1, P]]),
                    in1=bass.AP(pool, PZB, [[5632, P], [P, 4], [1, P]]),
                    op=ALU.mult)),
                    waits=[("a_rz", N("a_rz"))], inc="v_e3")

        P3D = dict(v_e3=N("v_e3"), t_za=N("t_za"), t_sc=N("t_sc"),
                   a_e=N("a_e"), kl=N("kl"), vl=N("vl"), v_se=N("v_se"))

        if debug and 2 in phases:
            dma("sync", dbgT["qsD"].ap(), a1[:],
                waits=[("v_qo", P2D["v_qo"])] +
                      ([("t_sc", P3D["t_sc"])] if 3 in phases else []),
                inc="dbg")
        if debug and 3 in phases:
            dma("sync", dbgT["oTD"].ap(), a2[:],
                waits=[("v_e3", P3D["v_e3"])], inc="dbg")

        if 4 in phases:
            for grp in range(2):
                for j in range(NDC):
                    bank = (grp * NDC + j) % 2
                    dma("sync",
                        bass.AP(pool, PW2, [[5632, P], [P, NDC], [1, P]]),
                        bass.AP(woT, j * P, [[D, P], [P * D, NDC], [1, P]]),
                        waits=[("t_s4", N("t_s4")), ("v_e3", P3D["v_e3"]),
                               ("kl", P3D["kl"])], inc="wos")
                    E("tensor", (lambda t, bank=bank, grp=grp: [
                        t.matmul(ps2a[:, bank, :],
                                 pool[:, PW2 + i * P:PW2 + (i + 1) * P],
                                 a2[:, i * 1024 + grp * 512:
                                    i * 1024 + grp * 512 + 512],
                                 start=(i == 0), stop=(i == NDC - 1))
                        for i in range(NDC)][-1]),
                        waits=[("wos", N("wos")), ("v_e3", P3D["v_e3"]),
                               ("a_e", P3D["a_e"]),
                               ("v_s4", max(0, N("v_s4") - 1))], inc="t_s4")
                    E("vector", (lambda v, bank=bank: v.tensor_copy(
                        out=pool[:, PVE:PVE + 512], in_=ps2a[:, bank, :])),
                        waits=[("t_s4", N("t_s4")), ("g_s4", N("g_s4")),
                               ("vl", P3D["vl"])], inc="v_s4")
                    dma("gpsimd",
                        bass.AP(stscr, j * P * 1024 + grp * 512,
                                [[1024, P], [1, 512]]),
                        pool[:, PVE:PVE + 512],
                        waits=[("v_s4", N("v_s4"))], inc="g_s4")

        P4D = dict(g_s4=N("g_s4"), t_s4=N("t_s4"), v_s4=N("v_s4"))

        P5D = {}
        if 5 in phases:
            B5 = [("g_s4", P4D["g_s4"]), ("t_s4", P4D["t_s4"]),
                  ("v_s4", P4D["v_s4"])]
            dma("sync",
                bass.AP(a1, 0, [[16384, P], [1024, NDC], [1, 1024]]),
                bass.AP(xTo, 0, [[1024, P], [P * 1024, NDC], [1, 1024]]),
                waits=B5 + [("t_sc", P3D["t_sc"]), ("v_xn", N("v_xn"))],
                inc="x5")
            for rnd, eps in ((0, 1e-5), (1, 1e-8)):
                for cg in range(2):
                    ln_group(a1, cg * 512, a1, cg * 512, 512, eps,
                             ("x5", N("x5")),
                             [("v_s4", P4D["v_s4"]), ("t_q2", N("t_q2")),
                              ("v_xn", N("v_xn")), ("v_se", P3D["v_se"])],
                             PQR, PQM, cstride=1024)
            VXN5 = N("v_xn")
            for j in range(8):
                for i in range(NDC):
                    idx = j * NDC + i
                    psu = ps1a if idx % 2 == 0 else ps1b
                    E("tensor", (lambda t, j=j, i=i, psu=psu: t.transpose(
                        psu[:, 0:P], a1[:, i * 1024 + j * P:i * 1024 + j * P + P],
                        eye[:])),
                        waits=([("v_xn", VXN5), ("v_e3", P3D["v_e3"])]
                               if idx == 0 else []) +
                              [("v_tp", max(0, N("v_tp") - 1))], inc="t_tp")
                    E("vector", (lambda v, j=j, i=i, psu=psu: v.tensor_copy(
                        out=a2[:, j * D + i * P:j * D + (i + 1) * P],
                        in_=psu[:, 0:P])),
                        waits=[("t_tp", N("t_tp"))], inc="v_tp")
            for j in range(8):
                E("vector", (lambda v, j=j: v.reduce_max(
                    maxab[:], a2[:, j * D:(j + 1) * D], axis=AXX,
                    apply_absolute_value=True)),
                    waits=[("v_tp", 16 * (j + 1)), ("a_u", N("a_u"))],
                    inc="v_qm")
                E("vector", (lambda v, j=j: v.tensor_scalar_mul(
                    out=ivtok[:, j:j + 1], in0=maxab[:], scalar1=1.0 / 127.0)),
                    waits=[("v_qm", N("v_qm"))], inc="v_iv0")
                E("vector", (lambda v: v.reciprocal(
                    out=stok[:], in_=maxab[:])),
                    waits=[("v_qm", N("v_qm")), ("a_u", N("a_u"))], inc="v_rc")
                E("vector", (lambda v: v.tensor_scalar_mul(
                    out=stok[:], in0=stok[:], scalar1=127.0)),
                    waits=[("v_rc", N("v_rc"))], inc="a_st")
                E("vector", (lambda v, j=j: v.tensor_scalar(
                    out=a2[:, j * D:(j + 1) * D], in0=a2[:, j * D:(j + 1) * D],
                    scalar1=stok[:], scalar2=128.5,
                    op0=ALU.mult, op1=ALU.add)),
                    waits=[("a_st", N("a_st"))], inc="a_u")
                E("vector", (lambda v, j=j: v.tensor_copy(
                    out=pool[:, PQR:PQR + D], in_=a2[:, j * D:(j + 1) * D])),
                    waits=[("a_u", N("a_u"))])
                for k in range(7, -1, -1):
                    cst = float(1 << k)
                    E("vector", (lambda v, cst=cst: v.tensor_single_scalar(
                        out=pool[:, PQM:PQM + D], in_=pool[:, PQR:PQR + D],
                        scalar=cst, op=ALU.is_ge)))
                    E("vector", (lambda v, cst=cst: v.tensor_scalar_mul(
                        out=pool[:, PQM:PQM + D], in0=pool[:, PQM:PQM + D],
                        scalar1=cst)))
                    E("vector", (lambda v: v.tensor_sub(
                        pool[:, PQR:PQR + D], pool[:, PQR:PQR + D],
                        pool[:, PQM:PQM + D])))
                E("vector", (lambda v, j=j: v.scalar_tensor_tensor(
                    out=a2[:, j * D:(j + 1) * D], in0=a2[:, j * D:(j + 1) * D],
                    scalar=-128.0, in1=pool[:, PQR:PQR + D],
                    op0=ALU.add, op1=ALU.subtract)), inc="v_n")
            for j in range(8):
                for i in range(NDC):
                    idx = j * NDC + i
                    psu = ps1a if idx % 2 == 0 else ps1b
                    E("tensor", (lambda t, j=j, i=i, psu=psu: t.transpose(
                        psu[:, 0:P], a2[:, j * D + i * P:j * D + (i + 1) * P],
                        eye[:])),
                        waits=[("v_n", N("v_n"))] +
                              [("v_tp2", max(0, N("v_tp2") - 1))], inc="t_tp2")
                    E("vector", (lambda v, j=j, i=i, psu=psu: v.tensor_copy(
                        out=bfa[:, i * 1024 + j * P:i * 1024 + j * P + P],
                        in_=psu[:, 0:P])),
                        waits=[("t_tp2", N("t_tp2"))], inc="v_tp2")
            XQDONE = N("v_tp2")
            if P5_TRUNC:
                sch.ops
            for j in range(8):
                psu = ps1a if j % 2 == 0 else ps1b
                E("tensor", (lambda t, psu=psu, j=j: t.transpose(
                    psu[0:1, 0:P], ivtok[:, j:j + 1], eye[:])),
                    waits=[("v_iv0", 8), ("v_iv1", max(0, N("v_iv1") - 1)),
                           ("v_tp2", XQDONE)], inc="t_iv")
                E("vector", (lambda v, psu=psu, j=j: v.tensor_copy(
                    out=rowb[0:1, j * P:(j + 1) * P], in_=psu[0:1, 0:P])),
                    waits=[("t_iv", N("t_iv"))], inc="v_iv1")
            for hf in range(2):
                E("tensor", (lambda t, hf=hf: t.matmul(
                    ps1a[:, 0:512], onescol[:],
                    rowb[0:1, hf * 512:hf * 512 + 512],
                    start=True, stop=True, skip_group_check=True)),
                    waits=[("v_iv1", 8), ("v_iv2", N("v_iv2"))], inc="t_iv2")
                E("vector", (lambda v, hf=hf: v.tensor_copy(
                    out=a1[:, A1IV + hf * 512:A1IV + hf * 512 + 512],
                    in_=ps1a[:, 0:512])),
                    waits=[("t_iv2", N("t_iv2"))], inc="v_iv2")
            IVDONE = N("v_iv2")
            for fc in range(P5_NFC):
                grpH, posH = fc // 4, fc % 4
                dma("sync",
                    bass.AP(pool, PWG, [[5632, P], [P, NDC], [1, P]]),
                    bass.AP(wgT, fc * P, [[S, P], [P * S, NDC], [1, P]]),
                    waits=[("a_sg", N("a_sg")), ("x5", N("x5")),
                           ("v_n", N("v_n")), ("a_u", N("a_u"))],
                    inc="wgl")
                dma("sync",
                    bass.AP(pool, PWU, [[5632, P], [P, NDC], [1, P]]),
                    bass.AP(wuT, fc * P, [[S, P], [P * S, NDC], [1, P]]),
                    waits=[("a_su", N("a_su")), ("v_n", N("v_n"))], inc="wul")
                E("scalar", (lambda sc: sc.activation(
                    out=sgnb[:, 0:2048], in_=pool[:, PWG:PWG + 2048],
                    func=AFT.Sign)),
                    waits=[("wgl", N("wgl")), ("t_g", N("t_g"))],
                    inc="a_sg")
                E("scalar", (lambda sc: sc.activation(
                    out=sgnb[:, 2048:4096], in_=pool[:, PWU:PWU + 2048],
                    func=AFT.Sign)),
                    waits=[("wul", N("wul")), ("t_u", N("t_u"))],
                    inc="a_su")
                E("tensor", (lambda t: [
                    t.matmul(ps2a[:, hf, :], sgnb[:, i * P:(i + 1) * P],
                             bfa[:, i * 1024 + hf * 512:i * 1024 + hf * 512 + 512],
                             start=(i == 0), stop=(i == NDC - 1))
                    for hf in range(2) for i in range(NDC)][-1]),
                    waits=[("a_sg", N("a_sg")), ("v_tp2", XQDONE),
                           ("v_t1", N("v_t1"))], inc="t_g")
                E("tensor", (lambda t: [
                    t.matmul(ps2c[:, hf, :],
                             sgnb[:, 2048 + i * P:2048 + (i + 1) * P],
                             bfa[:, i * 1024 + hf * 512:i * 1024 + hf * 512 + 512],
                             start=(i == 0), stop=(i == NDC - 1))
                    for hf in range(2) for i in range(NDC)][-1]),
                    waits=[("a_su", N("a_su")), ("v_t3", N("v_t3")),
                           ("t_q2", N("t_q2")), ("v_dn", N("v_dn"))],
                    inc="t_u")
                E("vector", (lambda v: v.tensor_tensor(
                    out=a1[:, A1T1:A1T1 + 1024],
                    in0=bass.AP(ps2a, 0, [[1024, P], [1, 1024]]),
                    in1=a1[:, A1IV:A1IV + 1024], op=ALU.mult)),
                    waits=[("t_g", N("t_g")), ("v_iv2", IVDONE),
                           ("a_si", N("a_si"))], inc="v_t1")
                E("scalar", (lambda sc: sc.activation(
                    out=a1[:, A1T1:A1T1 + 1024], in_=a1[:, A1T1:A1T1 + 1024],
                    func=AFT.Silu)),
                    waits=[("v_t1", N("v_t1"))], inc="a_si")
                E("vector", (lambda v: v.tensor_tensor(
                    out=a1[:, A1T3:A1T3 + 1024], in0=a1[:, A1T1:A1T1 + 1024],
                    in1=bass.AP(ps2c, 0, [[1024, P], [1, 1024]]),
                    op=ALU.mult)),
                    waits=[("a_si", N("a_si")), ("t_u", N("t_u"))], inc="v_t3")
                po = posH * 1024
                E("vector", (lambda v, po=po: v.tensor_mul(
                    a1[:, po:po + 1024], a1[:, A1T3:A1T3 + 1024],
                    a1[:, A1IV:A1IV + 1024])),
                    waits=[("t_d", N("t_d"))], inc="v_pp")
                if posH == 3:
                    vpp = N("v_pp")
                    for j in range(NDC):
                        dbl = j % 2
                        psd = ps2b if j % 2 == 0 else ps2c
                        dma("sync",
                            bass.AP(pool, PWD5 + dbl * 512,
                                    [[5632, P], [P, 4], [1, P]]),
                            bass.AP(wdT, (grpH * 512) * D + j * P,
                                    [[D, P], [P * D, 4], [1, P]]),
                            waits=[("a_sd", N("a_sd") - (1 if j >= 2 else 0)),
                                   ("v_qm", 8)], inc="wdl")
                        E("scalar", (lambda sc, dbl=dbl: sc.activation(
                            out=a1[:, A1WD + dbl * 512:A1WD + dbl * 512 + 512],
                            in_=pool[:, PWD5 + dbl * 512:PWD5 + dbl * 512 + 512],
                            func=AFT.Sign)),
                            waits=[("wdl", N("wdl")),
                                   ("t_d", max(0, N("t_d") - 1))], inc="a_sd")
                        E("tensor", (lambda t, dbl=dbl, vpp=vpp, psd=psd: [
                            t.matmul(
                                psd[:, hf, :],
                                a1[:, A1WD + dbl * 512 + c4 * P:
                                   A1WD + dbl * 512 + (c4 + 1) * P],
                                a1[:, c4 * 1024 + hf * 512:
                                   c4 * 1024 + hf * 512 + 512],
                                start=(c4 == 0), stop=(c4 == 3),
                                skip_group_check=True)
                            for hf in range(2)
                            for c4 in range(4)][-1]),
                            waits=[("a_sd", N("a_sd")), ("v_pp", vpp),
                                   ("v_dn", max(0, N("v_dn") - 1))], inc="t_d")
                        if grpH == 0:
                            E("vector", (lambda v, j=j, psd=psd: v.tensor_copy(
                                out=a2[:, j * 1024:(j + 1) * 1024],
                                in_=bass.AP(psd, 0, [[1024, P], [1, 1024]]))),
                                waits=[("t_d", N("t_d")), ("v_n", N("v_n"))],
                                inc="v_dn")
                        else:
                            E("vector", (lambda v, j=j, psd=psd: v.tensor_tensor(
                                out=a2[:, j * 1024:(j + 1) * 1024],
                                in0=bass.AP(psd, 0, [[1024, P], [1, 1024]]),
                                in1=a2[:, j * 1024:(j + 1) * 1024],
                                op=ALU.add)),
                                waits=[("t_d", N("t_d"))], inc="v_dn")
                        if debug and j == 0:
                            dma("gpsimd",
                                bass.AP(dbgT["xqD"], grpH * 1024,
                                        [[16384, P], [1, 1024]]),
                                a2[:, 0:1024],
                                waits=[("v_dn", N("v_dn"))], inc="dbg2")
            P5D = dict(v_dn=N("v_dn"), t_d=N("t_d"))

        if debug and 5 in phases:
            dma("sync", dbgT["dnD"].ap(), a2[:],
                waits=[("v_dn", P5D["v_dn"])], inc="dbg")
            dma("sync", dbgT["bfaD"].ap(), bfa[:],
                waits=[("v_tp2", N("v_tp2"))], inc="dbg")

        if 6 in phases:
            for i in range(NDC):
                E("scalar", (lambda sc, i=i: sc.activation(
                    out=a2[:, i * 1024:(i + 1) * 1024],
                    in_=a2[:, i * 1024:(i + 1) * 1024], func=AFT.Silu)),
                    waits=[("v_dn", P5D["v_dn"])] if i == 0 else [],
                    inc="a_gf")
            AGF = N("a_gf")
            for i in range(NDC):
                dma("sync",
                    bass.AP(pool, PSTX, [[5632, P], [1, 1024]]),
                    bass.AP(stscr, i * P * 1024, [[1024, P], [1, 1024]]),
                    waits=[("v_o6", N("v_o6")), ("t_d", P5D["t_d"]),
                           ("g_s4", P4D["g_s4"])], inc="sx6")
                dma("sync",
                    bass.AP(pool, PSTX + 1024, [[5632, P], [1, 1024]]),
                    bass.AP(xTo, i * P * 1024, [[1024, P], [1, 1024]]),
                    inc="sx6")
                E("vector", (lambda v, i=i: v.tensor_tensor(
                    out=a1[:, i * 1024:(i + 1) * 1024],
                    in0=pool[:, PSTX:PSTX + 1024],
                    in1=a2[:, i * 1024:(i + 1) * 1024], op=ALU.mult)),
                    waits=[("sx6", N("sx6")), ("a_gf", AGF)] +
                          ([("t_d", P5D["t_d"])] if i == 0 else []))
                E("vector", (lambda v, i=i: v.tensor_add(
                    a1[:, i * 1024:(i + 1) * 1024],
                    a1[:, i * 1024:(i + 1) * 1024],
                    pool[:, PSTX + 1024:PSTX + 2048])), inc="v_o6")
            VO6 = N("v_o6")
            for j in range(8):
                for i in range(NDC):
                    idx = j * NDC + i
                    psu = ps1a if idx % 2 == 0 else ps1b
                    E("tensor", (lambda t, j=j, i=i, psu=psu: t.transpose(
                        psu[:, 0:P], a1[:, i * 1024 + j * P:i * 1024 + j * P + P],
                        eye[:])),
                        waits=([("v_o6", VO6)] if idx == 0 else []) +
                              [("v_t6", max(0, N("v_t6") - 1))], inc="t_t6")
                    E("vector", (lambda v, i=i, psu=psu: v.tensor_copy(
                        out=pool[:, POTL + i * P:POTL + (i + 1) * P],
                        in_=psu[:, 0:P])),
                        waits=[("t_t6", N("t_t6")),
                               ("st", N("st"))],
                        inc="v_t6")
                dma("sync",
                    bass.AP(out_tok, j * P * D, [[D, P], [1, D]]),
                    bass.AP(pool, POTL, [[5632, P], [1, D]]),
                    waits=[("v_t6", N("v_t6"))], inc="st")
            E("vector", (lambda v: v.engine_nop()),
              waits=[("st", N("st"))])

        semset = {}
        for eng in ENGS:
            for (fn, w, inc, amt) in sch.ops[eng]:
                for (sm, v) in w:
                    semset.setdefault(sm, 0)
                if inc:
                    semset.setdefault(inc, 0)
        sems = {sm: e(nc.semaphore("sem_" + sm)) for sm in semset}
        block = e(nc.Block())

        def mk(en):
            def run(ee):
                for oi, (fn, w, inc, amt) in enumerate(sch.ops[en]):
                    for (sm, v) in w:
                        ee.wait_ge(sems[sm], v)
                    try:
                        inst = fn(ee)
                    except Exception as ex:
                        raise RuntimeError(
                            f"emit failed: engine={en} op_index={oi} "
                            f"inc={inc}: {type(ex).__name__}: {ex}") from ex
                    if inc is not None:
                        assert inst is not None, (en, inc)
                        inst.then_inc(sems[inc], amt)
            return run
        block.sync(mk("sync"))
        block.gpsimd(mk("gpsimd"))
        block.tensor(mk("tensor"))
        block.vector(mk("vector"))
        block.scalar(mk("scalar"))
    return nc


_TBL = {}


def _tables():
    if _TBL:
        return _TBL
    import jax
    jax.config.update("jax_enable_x64", True)
    import jax.numpy as jnp
    with jax.default_device(jax.devices("cpu")[0]):
        inv_freq = 1.0 / (10000.0 ** (jnp.arange(0, DH, 2, dtype=jnp.float32) / DH))
        freqs = jnp.arange(S, dtype=jnp.float32)[:, None] * inv_freq[None, :]
        cos = np.asarray(jnp.repeat(jnp.cos(freqs), 2, axis=-1))
        sin = np.asarray(jnp.repeat(jnp.sin(freqs), 2, axis=-1))
    sinN = sin.copy()
    sinN[:, :64] = -sinN[:, :64]
    _TBL["cos"] = np.ascontiguousarray(cos.T)
    _TBL["sinN"] = np.ascontiguousarray(sinN.T)
    tri = (np.arange(P)[:, None] <= np.arange(P)[None, :]).astype(np.float32)
    _TBL["tri4"] = np.tile(tri, (1, 4))
    m = np.zeros((P, 1024), dtype=np.float32)
    m[:, :512] = _TBL["tri4"]
    _TBL["masks"] = m
    _TBL["eye"] = np.eye(P, dtype=np.float32)
    return _TBL


def stage_inputs(L0_attn_x, L0_attn_Wq, L0_attn_Wk, L0_attn_Wv, L0_attn_Wo,
                 L0_ffn_W_g, L0_ffn_W_u, L0_ffn_W_d):
    t = _tables()
    xT = np.ascontiguousarray(np.asarray(L0_attn_x)[0].T)
    wkvT = np.ascontiguousarray(
        np.concatenate([np.asarray(L0_attn_Wk).T, np.asarray(L0_attn_Wv).T],
                       axis=1))
    wqT = np.ascontiguousarray(np.asarray(L0_attn_Wq).T)
    woT = np.ascontiguousarray(np.asarray(L0_attn_Wo).T)
    wgT = np.ascontiguousarray(np.asarray(L0_ffn_W_g).T)
    wuT = np.ascontiguousarray(np.asarray(L0_ffn_W_u).T)
    wdT = np.ascontiguousarray(np.asarray(L0_ffn_W_d).T)
    maps = []
    for c in range(8):
        tiles = snake_tiles(c)
        cols = np.concatenate([np.arange(tt * P, tt * P + P) for tt in tiles])
        cp = np.zeros((1, 8), dtype=np.int32)
        cp[0, 0] = c
        mtab = np.zeros((72, P, 512), dtype=np.float32)
        for j in range(8):
            tt = tiles[j]
            for b in range(9):
                i = 8 * j + b
                if i <= tt:
                    mtab[j * 9 + b] = 1.0
                elif i == tt + 1 and i <= 63:
                    mtab[j * 9 + b] = t["tri4"]
        maps.append(dict(
            xT=xT, xTo=np.ascontiguousarray(xT[:, cols]),
            wkvT=wkvT, wqT=wqT, woT=woT, wgT=wgT, wuT=wuT, wdT=wdT,
            cosF=t["cos"], sinF=t["sinN"],
            cosO=np.ascontiguousarray(t["cos"][:, cols]),
            sinO=np.ascontiguousarray(t["sinN"][:, cols]),
            eyeD=t["eye"], masksD=t["masks"], cpD=cp,
            mtabD=mtab))
    return maps


def kernel(**inputs):
    maps = stage_inputs(**inputs)
    nc = build_nc()
    res = run_bass_kernel_spmd(nc, maps, core_ids=list(range(8)))
    out = np.zeros((S, D), dtype=np.float32)
    for c in range(8):
        tiles = snake_tiles(c)
        oc = res.results[c]["out_tok"]
        for j, tt in enumerate(tiles):
            out[tt * P:(tt + 1) * P, :] = oc[j * P:(j + 1) * P, :]
    return out.reshape(1, S, D)
